# version 02
# baseline (speedup 1.0000x reference)
import math
import numpy as np
import ml_dtypes
import concourse.bass as bass
import concourse.mybir as mybir
from concourse.bass_utils import run_bass_kernel_spmd

F32 = mybir.dt.float32
BF16 = mybir.dt.bfloat16
I32 = mybir.dt.int32
F8 = mybir.dt.float8e4
AF = mybir.ActivationFunctionType
ALU = mybir.AluOpType
AX = mybir.AxisListType
f32 = np.float32

T = 4096
C = 1024
FF = 4096
NTT = 16
MARGIN = 10.0

_CACHE = {}
DEBUG_DUMP = True
_LAST_RES = []


def _host_tables():
    if "tabs" in _CACHE:
        return _CACHE["tabs"]
    dim = C // 2
    pos = np.arange(dim, dtype=np.float64)
    freqs = 1.0 / 10000.0 ** (pos / dim)
    active = int(np.sum(freqs * 2 * math.pi * T >= 1.0))
    active = min(active, dim - C // 4)
    assert active == 256
    fr = freqs[:active]
    th = np.arange(T, dtype=np.float64)[:, None] * fr[None, :]
    cos, sin = np.cos(th), np.sin(th)
    phi = np.zeros((8, 64, T), np.float64)
    psi = np.zeros((8, 64, T), np.float64)
    for h in range(8):
        sl = slice(32 * h, 32 * h + 32)
        phi[h, 0:32] = cos[:, sl].T
        phi[h, 32:64] = sin[:, sl].T
        psi[h, 0:32] = (cos[:, sl] - sin[:, sl]).T
        psi[h, 32:64] = (cos[:, sl] + sin[:, sl]).T
    kq = np.arange(128)
    tabs = (phi.astype(ml_dtypes.bfloat16), psi.astype(ml_dtypes.bfloat16),
            (kq[:, None] <= kq[None, :]).astype(ml_dtypes.bfloat16),
            np.eye(128, dtype=np.float32), np.eye(128, dtype=ml_dtypes.bfloat16))
    _CACHE["tabs"] = tabs
    return tabs


def _groups(ntk, w=3):
    return [list(range(i, min(i + w, ntk))) for i in range(0, ntk, w)]


class _StopEmission(Exception):
    pass


def build_nc(upto=None):
    if upto is None and "nc" in _CACHE:
        return _CACHE["nc"]
    nc = bass.Bass("TRN2", target_bir_lowering=False,
                   detect_race_conditions=False)
    dram = lambda n, s, d, k="ExternalInput": nc.dram_tensor(n, s, d, kind=k).ap()

    x_full = dram("x_full", [T, C], F32)
    x_own = dram("x_own", [2048, C], F32)
    wq_t = dram("wq_t", [C, 512], F32)
    wk_t = dram("wk_t", [C, 512], F32)
    wv_t = dram("wv_t", [C, 512], F32)
    bq_in = dram("bq_in", [512], F32)
    bk_in = dram("bk_in", [512], F32)
    bv_in = dram("bv_in", [512], F32)
    wproj_t = dram("wproj_t", [C, C], F32)
    bproj_in = dram("bproj_in", [C], F32)
    w1_t = dram("w1_t", [C, FF], F32)
    b1_in = dram("b1_in", [FF], F32)
    w2_t = dram("w2_t", [FF, C], F32)
    b2_in = dram("b2_in", [C], F32)
    pos_slab = dram("pos_slab", [8, 64], F32)
    phi_in = dram("phi_in", [8, 64, T], BF16)
    psi_in = dram("psi_in", [8, 64, T], BF16)
    ebias_in = dram("ebias_in", [8], F32)
    tri_in = dram("tri_in", [128, 128], BF16)
    eyef_in = dram("eyef_in", [128, 128], F32)
    eyeb_in = dram("eyeb_in", [128, 128], BF16)
    psel = dram("psel", [1, 1], I32)
    out_d = dram("out", [2048, C], F32, "ExternalOutput")
    if DEBUG_DUMP:
        dbg_d = dram("dbg", [2048, C], F32, "ExternalOutput")

    scr8 = nc.dram_tensor("scr8", [8], F32).ap()
    scr1 = nc.dram_tensor("scr1", [4], F32).ap()
    scr_rc = nc.dram_tensor("scr_rc", [2048], BF16).ap()
    gift = nc.dram_tensor("gift", [128, 4 * 2048], BF16).ap()
    gath = nc.dram_tensor("gath", [2, 128, 4 * 2048], BF16).ap()

    SB_BASE = 17152
    def at(name, shape, dt, off):
        nbytes = int(np.prod(shape[1:])) * mybir.dt.size(dt)
        assert SB_BASE + off + nbytes <= 229376, (name, off, nbytes)
        return nc.alloc_sbuf_tensor_at(name, shape, dt, offset=SB_BASE + off).ap()

    xT = at("xT", [128, 8 * T], BF16, 0)
    w1q = at("w1q", [128, 8 * FF], F8, 0)
    w2q = at("w2q", [128, 32 * C], F8, 32768)
    vbuf = at("vbuf", [128, 32 * 520], BF16, 65536)
    sbf = at("sbf", [128, NTT * C], BF16, 65536)
    kaug = [at(f"kaug{i}", [128, T], BF16, 98816 + 8192 * i) for i in range(3)]
    pstage = at("pstage", [128, 4 * 2048], BF16, 98816)
    hbuf = at("hbuf", [128, FF], F32, 98816)
    peerO = at("peerO", [128, 4 * 2048], BF16, 115200)
    ostg = [at(f"ostg{i}", [128, C], F32, 115200 + 4096 * i) for i in range(2)]
    qaug = [at(f"qaug{i}", [128, T], BF16, 123392 + 8192 * i) for i in range(3)]
    xres = [at(f"xres{i}", [128, C], F32, 131584 + 4096 * i) for i in range(2)]
    tscr = at("tscr", [128, C], F32, 139776)
    wi32 = at("wi32", [128, C], I32, 139776)
    bpbc = at("bpbc", [128, C], BF16, 143872)
    b2bc = at("b2bc", [128, C], BF16, 145920)
    onorm = at("onorm", [128, 4 * T], BF16, 147968)
    wvb = at("wvb", [128, FF], BF16, 147968)
    qg = at("qg", [128, FF], BF16, 147968)
    wpb = at("wpb", [128, 8 * C], BF16, 147968)
    qgT = at("qgT", [128, FF], BF16, 156160)
    xq = at("xq", [128, C], BF16, 164352)
    xqT = at("xqT", [128, C], BF16, 166400)
    b1bc = at("b1bc", [128, FF], BF16, 168448)
    h2raw = at("h2raw", [128, 4 * C], BF16, 176640)
    eyef_s = at("eyef_s", [128, 128], F32, 180736)
    pslab_s = at("pslab_s", [8, 64], F32, 181248)
    csum = at("csum", [8, 1], F32, 181504)
    ebrow = at("ebrow", [8, 1], F32, 181536)
    ebt = at("ebt", [8, 1], F32, 181568)
    wstage = at("wstage", [128, 2048], F32, 186880)
    wstgb = at("wstgb", [128, 2048], BF16, 195072)
    pt = [at("pt0", [128, 1536], BF16, 199424),
          at("pt1", [128, 1536], BF16, 202496)]
    rcbc = at("rcbc", [128, 2048], BF16, 205568)
    tri_s = at("tri_s", [128, 128], BF16, 209664)
    eyeb_s = at("eyeb_s", [128, 128], BF16, 209920)
    ebias_bc = at("ebias_bc", [128, 8], F32, 210176)
    bq8 = at("bq8", [128, 4], F32, 210208)
    bk_s = at("bk_s", [128, 4], F32, 210240)
    pb_s = at("pb_s", [1, 1], I32, 210272)
    bvbc = at("bvbc", [128, 512], BF16, 210304)
    stat = at("stat", [128, 128], F32, 211328)
    ones64 = at("ones64", [128, 64], F32, 211840)
    wsc = at("wsc", [1, 16], F32, 212096)
    wsc2 = at("wsc2", [128, 4], F32, 212160)
    vfill = at("vfill", [128, 64], F32, 181632)

    psum_ctx = [nc.psum_tensor("sh0", [128, 1536], F32),
                nc.psum_tensor("sh1", [128, 1536], F32),
                nc.psum_tensor("otb", [128, 512], F32),
                nc.psum_tensor("aux", [128, 512], F32)]
    sh0, sh1, otb, aux = [c.__enter__().ap() for c in psum_ctx]
    SHP = [sh0, sh1]

    ctx = nc.Block()
    block = ctx.__enter__()
    semctx = [nc.semaphore(n) for n in
              ["s_ld", "s_gd", "s_t", "s_v", "s_s", "s_cc", "s_out"]]
    s_ld, s_gd, s_t, s_v, s_s, s_cc, s_out = [c.__enter__() for c in semctx]
    cnt = dict(ld=0, gd=0, t=0, v=0, s=0, cc=0, out=0)

    stopped = [False]

    def VF(e, n=2):
        li = None
        for _ in range(n):
            li = e.tensor_scalar(out=vfill, in0=vfill, scalar1=1.0,
                                 scalar2=None, op0=ALU.mult)
        return li

    def SY(fn):
        if stopped[0]:
            return
        @block.sync
        def _(e):
            fn(e)

    def GP(fn):
        if stopped[0]:
            return
        @block.gpsimd
        def _(e):
            fn(e)

    def TE(fn):
        if stopped[0]:
            return
        @block.tensor
        def _(e):
            fn(e)

    def VE(fn):
        if stopped[0]:
            return
        @block.vector
        def _(e):
            fn(e)

    def SE(fn):
        if stopped[0]:
            return
        @block.scalar
        def _(e):
            fn(e)

    def phase(name):
        if stopped[0] or upto is None or name != upto:
            return
        def drain(e):
            for sem, key in [(s_ld, "ld"), (s_gd, "gd"), (s_t, "t"),
                             (s_v, "v"), (s_s, "s"), (s_cc, "cc")]:
                e.wait_ge(sem, cnt[key])
            e.memset(ostg[0], 0.0)
            e.dma_start(out=bass.AP(out_d.tensor, 0, [[C, 128], [1, C]]),
                        in_=ostg[0]).then_inc(s_out, 16)
            cnt["out"] += 16
            e.wait_ge(s_out, cnt["out"])
        GP(drain)
        stopped[0] = True

    def p0_sync(e):
        for dst, src in [(pb_s, psel), (tri_s, tri_in), (eyef_s, eyef_in),
                         (eyeb_s, eyeb_in), (pslab_s, pos_slab)]:
            e.dma_start(out=dst, in_=src).then_inc(s_ld, 16)
            cnt["ld"] += 16
        for dst, src in [(bq8, bq_in), (bk_s, bk_in)]:
            e.dma_start(out=dst,
                        in_=bass.AP(src.tensor, 0, [[1, 128], [128, 4]]),
                        allow_slow_non_contiguous=True).then_inc(s_ld, 16)
            cnt["ld"] += 16
        e.dma_start(out=ebrow,
                    in_=bass.AP(ebias_in.tensor, 0, [[1, 8], [1, 1]])
                    ).then_inc(s_ld, 16)
        cnt["ld"] += 16
        e.dma_start(out=wstage[:, 0:512],
                    in_=bass.AP(bv_in.tensor, 0, [[0, 128], [1, 512]])
                    ).then_inc(s_ld, 16)
        cnt["ld"] += 16
    SY(p0_sync)
    ld_p0 = cnt["ld"]

    def p0_vec(e):
        e.wait_ge(s_ld, ld_p0)
        e.reduce_sum(out=csum, in_=pslab_s, axis=AX.X)
        e.memset(bass.AP(vbuf.tensor, 64,
                         [[520 * 32, 128], [520, 32], [65, 8], [1, 1]]), 1.0)
        e.memset(ones64, 1.0)
        e.tensor_tensor(out=ebt, in0=ebrow, in1=csum, op=ALU.add)
        e.memset(stat, 0.0)
        e.memset(wsc, 0.0)
        e.tensor_tensor(out=ebrow, in0=ebt, in1=csum, op=ALU.subtract)
        e.memset(vfill, 0.0)
        e.tensor_scalar(out=bq8, in0=bq8, scalar1=0.125, scalar2=None,
                        op0=ALU.mult)
        e.tensor_copy(out=bvbc, in_=wstage[:, 0:512]).then_inc(s_v, 1)
        cnt["v"] += 1
    VE(p0_vec)
    v_p0 = cnt["v"]

    def p0_gp(e):
        e.wait_ge(s_v, v_p0)
        e.dma_start(out=scr8,
                    in_=bass.AP(ebrow.tensor, 0, [[1, 8], [1, 1]])
                    ).then_inc(s_gd, 16)
        cnt["gd"] += 16
        e.wait_ge(s_gd, cnt["gd"])
        for h in range(8):
            e.dma_start(out=ebias_bc[:, h:h + 1],
                        in_=bass.AP(scr8.tensor, h, [[0, 128], [1, 1]])
                        ).then_inc(s_gd, 16)
            cnt["gd"] += 16
    GP(p0_gp)
    phase("p0")

    xck_evac = [0] * 16
    for ck in range(16):
        def x_ld(e, ck=ck):
            if ck == 0:
                e.wait_ge(s_v, v_p0)
            else:
                e.wait_ge(s_v, xck_evac[ck - 1])
            e.dma_start(out=wstage,
                        in_=bass.AP(x_full.tensor, (256 * ck) * C,
                                    [[C, 128], [128 * C, 2], [1, C]])
                        ).then_inc(s_ld, 16)
            cnt["ld"] += 16
        SY(x_ld)
        ldk = cnt["ld"]

        def x_tp(e, ck=ck, ldk=ldk):
            e.wait_ge(s_ld, ldk)
            if ck >= 1:
                e.wait_ge(s_v, xck_evac[ck - 1])
            li = None
            for a in range(2):
                bank = sh0 if a == 0 else sh1
                for c8 in range(8):
                    li = e.transpose(out=bass.AP(bank.tensor, 128 * c8,
                                                 [[1536, 128], [1, 128]]),
                                     in_=wstage[:, 1024 * a + 128 * c8:
                                                1024 * a + 128 * c8 + 128],
                                     identity=eyef_s)
            li.then_inc(s_t, 1)
            cnt["t"] += 1
        TE(x_tp)
        t_tp = cnt["t"]

        def x_ev(e, ck=ck, t_tp=t_tp):
            e.wait_ge(s_t, t_tp)
            li = None
            for a in range(2):
                bank = sh0 if a == 0 else sh1
                for c8 in range(8):
                    li = e.tensor_copy(
                        out=xT[:, T * c8 + 256 * ck + 128 * a:
                               T * c8 + 256 * ck + 128 * a + 128],
                        in_=bank[:, 128 * c8:128 * c8 + 128])
            li.then_inc(s_v, 1)
            cnt["v"] += 1
        VE(x_ev)
        xck_evac[ck] = cnt["v"]
    phase("xT")

    wv_cast_v = [0, 0]
    for half in range(2):
        def wv_ld(e, half=half):
            e.wait_ge(s_v, xck_evac[15] if half == 0 else wv_cast_v[0])
            e.dma_start(out=wstage,
                        in_=bass.AP(wv_t.tensor, 256 * half,
                                    [[512, 128], [512 * 128, 8], [1, 256]])
                        ).then_inc(s_ld, 16)
            cnt["ld"] += 16
        SY(wv_ld)
        ldw = cnt["ld"]

        def wv_c(e, half=half, ldw=ldw):
            e.wait_ge(s_ld, ldw)
            e.tensor_copy(out=bass.AP(wvb.tensor, 256 * half,
                                      [[FF, 128], [512, 8], [1, 256]]),
                          in_=wstage).then_inc(s_v, 1)
            cnt["v"] += 1
        VE(wv_c)
        wv_cast_v[half] = cnt["v"]
    v_wvb = cnt["v"]

    vm_evac = [0] * 32
    for tt in range(32):
        def vmm(e, tt=tt):
            e.wait_ge(s_v, v_wvb)
            if tt >= 2:
                e.wait_ge(s_v, vm_evac[tt - 2])
            bank = otb if tt % 2 == 0 else aux
            for c8 in range(8):
                sl = xT[:, T * c8 + 128 * tt: T * c8 + 128 * tt + 128]
                e.ldweights(sl)
                li = e.matmul(out=bank, lhsT=sl,
                              rhs=wvb[:, 512 * c8:512 * c8 + 512],
                              start=(c8 == 0), stop=(c8 == 7))
            li.then_inc(s_t, 1)
            cnt["t"] += 1
        TE(vmm)
        t_vm = cnt["t"]

        def vev(e, tt=tt, t_vm=t_vm):
            e.wait_ge(s_t, t_vm)
            bank = otb if tt % 2 == 0 else aux
            e.scalar_tensor_tensor(
                out=bass.AP(vbuf.tensor, 520 * tt,
                            [[520 * 32, 128], [65, 8], [1, 64]]),
                in0=bass.AP(bank.tensor, 0, [[512, 128], [64, 8], [1, 64]]),
                scalar=1.0,
                in1=bass.AP(bvbc.tensor, 0, [[512, 128], [64, 8], [1, 64]]),
                op0=ALU.mult, op1=ALU.add).then_inc(s_v, 1)
            cnt["v"] += 1
        VE(vev)
        vm_evac[tt] = cnt["v"]
    phase("vmm")

    kq_ready = [0] * 8
    kq_aug_ld = [0] * 8
    exp_cp = {}
    mask_cp = {}
    pv_cp = {}
    sc_cp = {}
    head_last_sc = [0] * 8
    head_norm_v = [0] * 8
    stage_free = {"s": 0, "v": 0}

    def emit_pair_fill(hp, guard_t):
        b0, b1 = (2 * hp) % 3, (2 * hp + 1) % 3

        def w_ld(e, hp=hp):
            if hp == 0:
                e.wait_ge(s_v, v_wvb)
            else:
                e.wait_ge(s_v, stage_free["v"])
            e.dma_start(out=wstage[:, 0:1024],
                        in_=bass.AP(wk_t.tensor, 128 * hp,
                                    [[512, 128], [512 * 128, 8], [1, 128]])
                        ).then_inc(s_ld, 16)
            e.dma_start(out=wstage[:, 1024:2048],
                        in_=bass.AP(wq_t.tensor, 128 * hp,
                                    [[512, 128], [512 * 128, 8], [1, 128]])
                        ).then_inc(s_ld, 16)
            cnt["ld"] += 32
        SY(w_ld)
        ld_w = cnt["ld"]

        def w_cast(e, ld_w=ld_w):
            e.wait_ge(s_ld, ld_w)
            e.tensor_copy(out=wstgb, in_=wstage).then_inc(s_v, 1)
            cnt["v"] += 1
        VE(w_cast)
        v_cast = cnt["v"]

        def aug_ld(e, hp=hp, b0=b0, b1=b1, guard_t=guard_t):
            e.wait_ge(s_t, guard_t)
            for h_i, (kb, qb) in enumerate([(kaug[b0], qaug[b0]),
                                            (kaug[b1], qaug[b1])]):
                h = 2 * hp + h_i
                ab = 64 if h % 2 == 0 else 0
                e.dma_start(out=kb[ab:ab + 64, :], in_=psi_in[h, :, :]
                            ).then_inc(s_ld, 16)
                e.dma_start(out=qb[ab:ab + 64, :], in_=phi_in[h, :, :]
                            ).then_inc(s_ld, 16)
                cnt["ld"] += 32
        SY(aug_ld)
        ld_aug = cnt["ld"]

        evk = []
        for which in range(2):
            wbase = 1024 * which
            for st in range(8):
                def kqmm(e, which=which, wbase=wbase, st=st, v_cast=v_cast,
                         guard_t=guard_t):
                    e.wait_ge(s_v, v_cast)
                    e.wait_ge(s_t, guard_t)
                    if evk:
                        e.wait_ge(s_s, evk[-1])
                    for c8 in range(8):
                        sl = wstgb[:, wbase + 128 * c8: wbase + 128 * c8 + 128]
                        e.ldweights(sl)
                        li = e.matmul(out=aux, lhsT=sl,
                                      rhs=xT[:, T * c8 + 512 * st:
                                             T * c8 + 512 * st + 512],
                                      start=(c8 == 0), stop=(c8 == 7))
                    li.then_inc(s_t, 1)
                    cnt["t"] += 1
                TE(kqmm)
                t_mm = cnt["t"]

                def kqev(e, which=which, st=st, t_mm=t_mm, hp=hp, b0=b0, b1=b1):
                    e.wait_ge(s_t, t_mm)
                    for h_i in range(2):
                        h = 2 * hp + h_i
                        dst = (kaug if which == 0 else qaug)[b0 if h_i == 0 else b1]
                        db = 0 if h % 2 == 0 else 64
                        src = aux[64 * h_i:64 * h_i + 64, :]
                        if which == 0:
                            li = e.activation(
                                out=dst[db:db + 64, 512 * st:512 * st + 512],
                                in_=src, func=AF.Identity,
                                bias=bk_s[64 * h_i:64 * h_i + 64, hp:hp + 1])
                        else:
                            li = e.activation(
                                out=dst[db:db + 64, 512 * st:512 * st + 512],
                                in_=src, func=AF.Identity,
                                bias=bq8[64 * h_i:64 * h_i + 64, hp:hp + 1],
                                scale=np.float32(0.125))
                    li.then_inc(s_s, 1)
                    cnt["s"] += 1
                SE(kqev)
                evk.append(cnt["s"])
        stage_free["s"] = cnt["s"]
        stage_free["v"] = v_cast
        for h_i in range(2):
            kq_ready[2 * hp + h_i] = cnt["s"]
            kq_aug_ld[2 * hp + h_i] = ld_aug

    def emit_head(h):
        Ka, Qa = kaug[h % 3], qaug[h % 3]
        for st in range(8):
            ntk = 4 * st + 4
            grps = _groups(ntk)
            for g, kjs in enumerate(grps):
                par = g % 2

                def t_sc(e, h=h, st=st, g=g, kjs=kjs, par=par):
                    e.wait_ge(s_s, kq_ready[h])
                    e.wait_ge(s_ld, kq_aug_ld[h])
                    key = ("shfree", par)
                    if key in exp_cp:
                        e.wait_ge(s_s, exp_cp[key])
                    for i, kj in enumerate(kjs):
                        sl = Ka[:, 128 * kj:128 * kj + 128]
                        e.ldweights(sl)
                        li = e.matmul(out=SHP[par][:, 512 * i:512 * i + 512],
                                      lhsT=sl,
                                      rhs=Qa[:, 512 * st:512 * st + 512],
                                      start=True, stop=True)
                    li.then_inc(s_t, 1)
                    cnt["t"] += 1
                TE(t_sc)
                sc_cp[(h, st, g)] = cnt["t"]

                def s_exp(e, h=h, st=st, g=g, kjs=kjs, par=par):
                    e.wait_ge(s_t, sc_cp[(h, st, g)])
                    key = ("ptfree", par)
                    if key in pv_cp:
                        e.wait_ge(s_t, pv_cp[key])
                    w = 512 * len(kjs)
                    e.activation(out=pt[par][:, 0:w], in_=SHP[par][:, 0:w],
                                 func=AF.Exp,
                                 bias=ebias_bc[:, h:h + 1]).then_inc(s_s, 1)
                    cnt["s"] += 1
                SE(s_exp)
                exp_cp[(h, st, g)] = cnt["s"]
                exp_cp[("shfree", par)] = cnt["s"]

                def v_mask(e, h=h, st=st, g=g, kjs=kjs, par=par):
                    e.wait_ge(s_s, exp_cp[(h, st, g)])
                    li = None
                    for i, kj in enumerate(kjs):
                        d = kj - 4 * st
                        if 0 <= d <= 3:
                            sl = slice(512 * i + 128 * d, 512 * i + 128 * d + 128)
                            li = e.tensor_tensor(out=pt[par][:, sl],
                                                 in0=pt[par][:, sl],
                                                 in1=tri_s, op=ALU.mult)
                    if li is not None:
                        li.then_inc(s_v, 1)
                    else:
                        e.nop().then_inc(s_v, 1)
                    cnt["v"] += 1
                VE(v_mask)
                mask_cp[(h, st, g)] = cnt["v"]

                def t_pv(e, h=h, st=st, g=g, kjs=kjs, par=par, ntk=ntk):
                    e.wait_ge(s_v, mask_cp[(h, st, g)])
                    if g == 0 and ("otfree",) in pv_cp:
                        e.wait_ge(s_v, pv_cp[("otfree",)])
                    for i, kj in enumerate(kjs):
                        c0 = max(0, kj - 4 * st) * 128
                        vsl = vbuf[:, 520 * kj + 65 * h: 520 * kj + 65 * h + 65]
                        e.ldweights(vsl)
                        li = e.matmul(out=otb[0:65, c0:512], lhsT=vsl,
                                      rhs=pt[par][:, 512 * i + c0:512 * i + 512],
                                      start=(kj == 0), stop=(kj == ntk - 1))
                    li.then_inc(s_t, 1)
                    cnt["t"] += 1
                TE(t_pv)
                pv_cp[(h, st, g)] = cnt["t"]
                pv_cp[("ptfree", par)] = cnt["t"]

            def v_strip(e, h=h, st=st, ngr=len(grps)):
                e.wait_ge(s_t, pv_cp[(h, st, ngr - 1)])
                lo = 64 * (h % 2)
                e.tensor_copy(out=onorm[lo:lo + 64,
                                        T * (h // 2) + 512 * st:
                                        T * (h // 2) + 512 * st + 512],
                              in_=otb[0:64, :])
                e.tensor_copy(out=rcbc[0:1, 512 * (st % 4):512 * (st % 4) + 512],
                              in_=otb[64:65, :]).then_inc(s_v, 1)
                cnt["v"] += 1
            VE(v_strip)
            pv_cp[("otfree",)] = cnt["v"]

            if st in (3, 7):
                hh = (st - 3) // 4

                def v_recip(e):
                    e.wait_ge(s_v, pv_cp[("otfree",)])
                    VF(e)
                    with nc.allow_low_precision(reason="bf16 softmax sums"):
                        e.reciprocal(out=rcbc[0:1, :], in_=rcbc[0:1, :]
                                     ).then_inc(s_v, 1)
                    cnt["v"] += 1
                VE(v_recip)
                v_rc = cnt["v"]

                def g_bc(e, v_rc=v_rc):
                    e.wait_ge(s_v, v_rc)
                    e.dma_start(out=scr_rc, in_=rcbc[0:1, :]).then_inc(s_gd, 16)
                    cnt["gd"] += 16
                    e.wait_ge(s_gd, cnt["gd"])
                    e.dma_start(out=rcbc,
                                in_=bass.AP(scr_rc.tensor, 0,
                                            [[0, 128], [1, 2048]])
                                ).then_inc(s_gd, 16)
                    cnt["gd"] += 16
                GP(g_bc)
                gd_rc = cnt["gd"]

                def v_norm(e, h=h, hh=hh, gd_rc=gd_rc):
                    e.wait_ge(s_gd, gd_rc)
                    lo = 64 * (h % 2)
                    sl = slice(T * (h // 2) + 2048 * hh,
                               T * (h // 2) + 2048 * hh + 2048)
                    e.tensor_tensor(out=onorm[lo:lo + 64, sl],
                                    in0=onorm[lo:lo + 64, sl],
                                    in1=rcbc[lo:lo + 64, :],
                                    op=ALU.mult).then_inc(s_v, 1)
                    cnt["v"] += 1
                VE(v_norm)
        head_last_sc[h] = sc_cp[(h, 7, len(_groups(32)) - 1)]
        head_norm_v[h] = cnt["v"]

    def emit_wquant(w_in, wrows, wdst, wcols, m_col, race_t):
        nchunks = wrows // 128
        spl = 2 if wcols >= 2048 else 1
        tw = wcols // spl
        tiles = [(c, j) for c in range(nchunks) for j in range(spl)]
        absv = []
        for idx, (c, j) in enumerate(tiles):
            def wl(e, c=c, j=j, idx=idx):
                if idx == 0:
                    e.wait_ge(s_v, stage_free["v"])
                else:
                    e.wait_ge(s_v, absv[-1])
                e.dma_start(out=wstage[:, 0:tw],
                            in_=bass.AP(w_in.tensor, (128 * c) * wcols + tw * j,
                                        [[wcols, 128], [1, tw]])
                            ).then_inc(s_ld, 16)
                cnt["ld"] += 16
            SY(wl)
            ldw = cnt["ld"]

            def wa(e, idx=idx, ldw=ldw):
                e.wait_ge(s_ld, ldw)
                e.reduce_sum(out=stat[:, 64 + idx:65 + idx],
                             in_=wstage[:, 0:tw], axis=AX.X,
                             apply_absolute_value=True).then_inc(s_v, 1)
                cnt["v"] += 1
            VE(wa)
            absv.append(cnt["v"])
        ntile = len(tiles)

        def w_zero(e, ntile=ntile):
            e.wait_ge(s_v, absv[-1])
            if ntile < 64:
                e.memset(stat[:, 64 + ntile:128], 0.0).then_inc(s_v, 1)
            cnt["v"] += 1
        VE(w_zero)
        v_z = cnt["v"]

        def w_mm(e, v_z=v_z):
            e.wait_ge(s_v, v_z)
            e.wait_ge(s_s, stage_free["s"])
            e.matmul(out=aux[0:64, 0:64], lhsT=ones64, rhs=stat[:, 64:128],
                     start=True, stop=True).then_inc(s_t, 1)
            cnt["t"] += 1
        TE(w_mm)
        t_mm = cnt["t"]

        def w_chain(e, t_mm=t_mm, m_col=m_col):
            e.wait_ge(s_t, t_mm)
            e.reduce_sum(out=wsc[0:1, m_col:m_col + 1], in_=aux[0:1, 0:64],
                         axis=AX.X)
            VF(e)
            e.tensor_scalar(out=wsc[0:1, m_col:m_col + 1],
                            in0=wsc[0:1, m_col:m_col + 1],
                            scalar1=float(np.float32(1.0 / (wrows * wcols))), scalar2=1e-5,
                            op0=ALU.mult, op1=ALU.max)
            VF(e)
            e.reciprocal(out=wsc[0:1, m_col + 1:m_col + 2],
                         in_=wsc[0:1, m_col:m_col + 1]).then_inc(s_v, 1)
            cnt["v"] += 1
        VE(w_chain)
        v_ch = cnt["v"]

        def g_wbc(e, v_ch=v_ch, m_col=m_col):
            e.wait_ge(s_v, v_ch)
            e.dma_start(out=scr1, in_=wsc[0:1, m_col:m_col + 4]
                        ).then_inc(s_gd, 16)
            cnt["gd"] += 16
            e.wait_ge(s_gd, cnt["gd"])
            for k in range(2):
                e.dma_start(out=wsc2[:, m_col + k:m_col + k + 1],
                            in_=bass.AP(scr1.tensor, k, [[0, 128], [1, 1]])
                            ).then_inc(s_gd, 16)
                cnt["gd"] += 16
        GP(g_wbc)
        gd_bc = cnt["gd"]

        rl = []
        for idx, (c, j) in enumerate(tiles):
            def wl2(e, c=c, j=j, idx=idx):
                if rl:
                    e.wait_ge(s_v, rl[-1])
                else:
                    e.wait_ge(s_gd, gd_bc)
                e.dma_start(out=wstage[:, 0:tw],
                            in_=bass.AP(w_in.tensor, (128 * c) * wcols + tw * j,
                                        [[wcols, 128], [1, tw]])
                            ).then_inc(s_ld, 16)
                cnt["ld"] += 16
            SY(wl2)
            ldw = cnt["ld"]

            def wr(e, c=c, j=j, ldw=ldw, m_col=m_col, idx=idx):
                e.wait_ge(s_ld, ldw)
                if idx == 0:
                    e.wait_ge(s_gd, gd_bc)
                    e.wait_ge(s_t, race_t)
                for q in range(tw // 1024):
                    e.tensor_scalar(out=wi32,
                                    in0=wstage[:, 1024 * q:1024 * q + 1024],
                                    scalar1=wsc2[:, m_col + 1:m_col + 2],
                                    scalar2=None, op0=ALU.mult)
                    li = e.tensor_scalar(
                        out=wdst[:, wcols * c + tw * j + 1024 * q:
                                 wcols * c + tw * j + 1024 * q + 1024],
                        in0=wi32, scalar1=-1.0, scalar2=1.0,
                        op0=ALU.max, op1=ALU.min)
                li.then_inc(s_v, 1)
                cnt["v"] += 1
            VE(wr)
            rl.append(cnt["v"])
        stage_free["v"] = cnt["v"]
        return cnt["v"]

    emit_pair_fill(0, guard_t=0)
    phase("fill0")
    emit_head(0)
    phase("head0")
    emit_pair_fill(1, guard_t=head_last_sc[0])
    emit_head(1)
    emit_head(2)
    emit_pair_fill(2, guard_t=head_last_sc[2])
    emit_head(3)
    phase("head3")
    emit_head(4)
    emit_pair_fill(3, guard_t=head_last_sc[4])
    emit_head(5)
    phase("head5")
    v_w1_done = emit_wquant(w1_t, C, w1q, FF, 0, race_t=head_last_sc[5])
    phase("w1q")
    emit_head(6)
    v_w2_done = emit_wquant(w2_t, FF, w2q, C, 2, race_t=head_last_sc[6])
    emit_head(7)
    phase("attn")

    def g_exch(e):
        for h in range(8):
            e.wait_ge(s_v, head_norm_v[h])
        with e.register("rp") as rp:
            e.reg_load(rp, pb_s[0:1, 0:1])
            with e.If_eq(rp, 0):
                for u in range(4):
                    e.dma_start(out=gift[:, 2048 * u:2048 * u + 2048],
                                in_=onorm[:, T * u + 2048:T * u + 4096]
                                ).then_inc(s_gd, 16)
            with e.Else():
                for u in range(4):
                    e.dma_start(out=gift[:, 2048 * u:2048 * u + 2048],
                                in_=onorm[:, T * u:T * u + 2048]
                                ).then_inc(s_gd, 16)
            cnt["gd"] += 64
            e.wait_ge(s_gd, cnt["gd"])
            e.collective_compute(
                "AllGather", ALU.bypass,
                replica_groups=[[0, 1], [2, 3], [4, 5], [6, 7]],
                ins=[gift], outs=[gath]).then_inc(s_cc, 1)
            cnt["cc"] += 1
            e.wait_ge(s_cc, cnt["cc"])
            with e.If_eq(rp, 0):
                for u in range(4):
                    e.dma_start(out=pstage[:, 2048 * u:2048 * u + 2048],
                                in_=onorm[:, T * u:T * u + 2048]
                                ).then_inc(s_gd, 16)
                e.dma_start(out=peerO, in_=gath[1]).then_inc(s_gd, 16)
            with e.Else():
                for u in range(4):
                    e.dma_start(out=pstage[:, 2048 * u:2048 * u + 2048],
                                in_=onorm[:, T * u + 2048:T * u + 4096]
                                ).then_inc(s_gd, 16)
                e.dma_start(out=peerO, in_=gath[0]).then_inc(s_gd, 16)
            cnt["gd"] += 80
    GP(g_exch)
    gd_exch = cnt["gd"]
    phase("exch")

    def bp_ld(e):
        e.wait_ge(s_v, v_w2_done)
        e.dma_start(out=wstage[:, 0:1024],
                    in_=bass.AP(bproj_in.tensor, 0, [[0, 128], [1, C]])
                    ).then_inc(s_ld, 16)
        cnt["ld"] += 16
    SY(bp_ld)
    ld_bp = cnt["ld"]

    def bp_c(e, ld_bp=ld_bp):
        e.wait_ge(s_ld, ld_bp)
        e.tensor_scalar(out=wsc2[:, 3:4], in0=wsc2[:, 2:3],
                        scalar1=float(np.float32(1.0 / 127.0)), scalar2=None,
                        op0=ALU.mult)
        e.tensor_copy(out=bpbc, in_=wstage[:, 0:1024]).then_inc(s_v, 1)
        cnt["v"] += 1
    VE(bp_c)
    v_bp = cnt["v"]

    wp_cast = [0] * 4
    for q4 in range(4):
        def wp_ld(e, q4=q4):
            e.wait_ge(s_v, v_bp if q4 == 0 else wp_cast[q4 - 1])
            if q4 == 0:
                e.wait_ge(s_gd, gd_exch)
            e.dma_start(out=wstage,
                        in_=bass.AP(wproj_t.tensor, 256 * q4,
                                    [[C, 128], [128 * C, 8], [1, 256]])
                        ).then_inc(s_ld, 16)
            cnt["ld"] += 16
        SY(wp_ld)
        ldw = cnt["ld"]

        def wp_c(e, q4=q4, ldw=ldw):
            e.wait_ge(s_ld, ldw)
            if q4 == 0:
                e.wait_ge(s_gd, gd_exch)
            e.tensor_copy(out=bass.AP(wpb.tensor, 256 * q4,
                                      [[8 * C, 128], [C, 8], [1, 256]]),
                          in_=wstage).then_inc(s_v, 1)
            cnt["v"] += 1
        VE(wp_c)
        wp_cast[q4] = cnt["v"]
    v_wp = cnt["v"]
    phase("wp")

    proj_done_v = [0] * NTT
    for tt in range(NTT):
        def xr_ld(e, tt=tt):
            if tt >= 2:
                e.wait_ge(s_v, proj_done_v[tt - 2])
            e.dma_start(out=xres[tt % 2],
                        in_=bass.AP(x_own.tensor, 128 * tt * C,
                                    [[C, 128], [1, C]])).then_inc(s_gd, 16)
            cnt["gd"] += 16
        GP(xr_ld)
        ld_xr = cnt["gd"]

        def pmm(e, tt=tt):
            e.wait_ge(s_v, v_wp)
            e.wait_ge(s_gd, gd_exch)
            if tt >= 2:
                e.wait_ge(s_v, proj_done_v[tt - 2])
            bset = sh0 if tt % 2 == 0 else sh1
            for dgrp in range(2):
                bank = bset[:, 512 * dgrp:512 * dgrp + 512]
                for ci in range(8):
                    if ci < 4:
                        sl = pstage[:, 2048 * ci + 128 * tt:
                                    2048 * ci + 128 * tt + 128]
                    else:
                        sl = peerO[:, 2048 * (ci - 4) + 128 * tt:
                                   2048 * (ci - 4) + 128 * tt + 128]
                    e.ldweights(sl)
                    li = e.matmul(out=bank, lhsT=sl,
                                  rhs=bass.AP(wpb.tensor, C * ci + 512 * dgrp,
                                              [[8 * C, 128], [1, 512]]),
                                  start=(ci == 0), stop=(ci == 7))
            li.then_inc(s_t, 1)
            cnt["t"] += 1
        TE(pmm)
        t_p = cnt["t"]

        def pev(e, tt=tt, t_p=t_p, ld_xr=ld_xr):
            e.wait_ge(s_t, t_p)
            e.wait_ge(s_gd, ld_xr)
            bset = sh0 if tt % 2 == 0 else sh1
            for dgrp in range(2):
                e.scalar_tensor_tensor(
                    out=tscr[:, 512 * dgrp:512 * dgrp + 512],
                    in0=bset[:, 512 * dgrp:512 * dgrp + 512], scalar=1.0,
                    in1=bpbc[:, 512 * dgrp:512 * dgrp + 512],
                    op0=ALU.mult, op1=ALU.add)
            e.tensor_tensor(out=sbf[:, C * tt:C * tt + C], in0=tscr,
                            in1=xres[tt % 2], op=ALU.add).then_inc(s_v, 1)
            cnt["v"] += 1
        VE(pev)
        proj_done_v[tt] = cnt["v"]
    v_proj = cnt["v"]

    phase("proj")
    for c in reversed(psum_ctx):
        c.__exit__(None, None, None)
    psum_ctx2 = [nc.psum_tensor("f1b", [128, 1536], F32),
                 nc.psum_tensor("f2b", [128, 1536], F32),
                 nc.psum_tensor("tpb", [128, 1024], BF16),
                 nc.psum_tensor("tpb2", [128, 1024], BF16)]
    f1b, f2b, tpb, tpb2 = [c.__enter__().ap() for c in psum_ctx2]

    b_v = [0, 0, 0]
    for bi, (src, dst, width) in enumerate(
            [(b1_in, b1bc[:, 0:2048], 2048),
             (None, b1bc[:, 2048:4096], 2048),
             (b2_in, b2bc, C)]):
        def bb_ld(e, bi=bi, width=width):
            e.wait_ge(s_v, v_proj if bi == 0 else b_v[bi - 1])
            base = 2048 if bi == 1 else 0
            e.dma_start(out=wstage[:, 0:width],
                        in_=bass.AP((b1_in if bi < 2 else b2_in).tensor, base,
                                    [[0, 128], [1, width]])
                        ).then_inc(s_ld, 16)
            cnt["ld"] += 16
        SY(bb_ld)
        ldb = cnt["ld"]

        def bb_c(e, bi=bi, dst=dst, width=width, ldb=ldb):
            e.wait_ge(s_ld, ldb)
            e.tensor_copy(out=dst, in_=wstage[:, 0:width]).then_inc(s_v, 1)
            cnt["v"] += 1
        VE(bb_c)
        b_v[bi] = cnt["v"]
    v_bias = cnt["v"]

    def ln1_stats(e):
        e.wait_ge(s_v, v_bias)
        for tt in range(NTT):
            sl = sbf[:, C * tt:C * tt + C]
            e.reduce_sum(out=stat[:, 16 + tt:17 + tt], in_=sl, axis=AX.X)
            e.scalar_tensor_tensor(out=pt[0][:, 0:C], in0=sl, scalar=1.0,
                                   in1=sl, op0=ALU.mult, op1=ALU.mult,
                                   accum_out=stat[:, tt:tt + 1])
            e.reduce_max(out=stat[:, 96 + tt:97 + tt], in_=sl, axis=AX.X)
            li = e.tensor_reduce(out=stat[:, 112 + tt:113 + tt], in_=sl,
                                 axis=AX.X, op=ALU.min)
        li.then_inc(s_v, 1)
        cnt["v"] += 1
    VE(ln1_stats)
    v_st = cnt["v"]

    def ln1_c1(e, v_st=v_st):
        e.wait_ge(s_v, v_st)
        e.tensor_scalar(out=stat[:, 16:32], in0=stat[:, 16:32],
                        scalar1=float(np.float32(1.0 / C)), scalar2=None, op0=ALU.mult)
        e.tensor_scalar(out=stat[:, 0:16], in0=stat[:, 0:16],
                        scalar1=float(np.float32(1.0 / C)), scalar2=1e-5,
                        op0=ALU.mult, op1=ALU.add)
        e.tensor_tensor(out=stat[:, 112:128], in0=stat[:, 16:32],
                        in1=stat[:, 112:128], op=ALU.subtract)
        e.tensor_tensor(out=stat[:, 96:112], in0=stat[:, 96:112],
                        in1=stat[:, 16:32], op=ALU.subtract)
        e.tensor_tensor(out=stat[:, 80:96], in0=stat[:, 16:32],
                        in1=stat[:, 16:32], op=ALU.mult)
        e.tensor_tensor(out=stat[:, 96:112], in0=stat[:, 96:112],
                        in1=stat[:, 112:128], op=ALU.max)
        e.tensor_tensor(out=stat[:, 0:16], in0=stat[:, 0:16],
                        in1=stat[:, 80:96], op=ALU.subtract
                        ).then_inc(s_v, 1)
        cnt["v"] += 1
    VE(ln1_c1)
    v_c1 = cnt["v"]

    def ln1_s1(e, v_c1=v_c1):
        e.wait_ge(s_v, v_c1)
        e.activation(out=stat[:, 32:48], in_=stat[:, 0:16], func=AF.Sqrt,
                     bias=np.float32(0.0)).then_inc(s_s, 1)
        cnt["s"] += 1
    SE(ln1_s1)
    s_sq1 = cnt["s"]

    def ln1_c2(e, s_sq1=s_sq1):
        e.wait_ge(s_s, s_sq1)
        e.reciprocal(out=stat[:, 32:48], in_=stat[:, 32:48])
        e.tensor_scalar(out=stat[:, 112:128], in0=stat[:, 0:16],
                        scalar1=-1e-5, scalar2=None, op0=ALU.add)
        e.tensor_tensor(out=stat[:, 48:64], in0=stat[:, 16:32],
                        in1=stat[:, 32:48], op=ALU.mult)
        e.reciprocal(out=stat[:, 16:32], in_=stat[:, 0:16])
        e.tensor_scalar(out=stat[:, 48:64], in0=stat[:, 48:64],
                        scalar1=-1.0, scalar2=None, op0=ALU.mult)
        e.tensor_tensor(out=stat[:, 112:128], in0=stat[:, 112:128],
                        in1=stat[:, 16:32], op=ALU.mult
                        ).then_inc(s_v, 1)
        cnt["v"] += 1
    VE(ln1_c2)
    v_c2 = cnt["v"]

    def ln1_s2(e, v_c2=v_c2):
        e.wait_ge(s_v, v_c2)
        e.activation(out=stat[:, 16:32], in_=stat[:, 112:128], func=AF.Sqrt,
                     bias=np.float32(1e-5)).then_inc(s_s, 1)
        cnt["s"] += 1
    SE(ln1_s2)
    s_sq2 = cnt["s"]

    def ln1_c3(e, s_sq2=s_sq2):
        e.wait_ge(s_s, s_sq2)
        e.reciprocal(out=stat[:, 16:32], in_=stat[:, 16:32])
        e.tensor_tensor(out=stat[:, 96:112], in0=stat[:, 96:112],
                        in1=stat[:, 32:48], op=ALU.mult)
        VF(e, 1)
        e.tensor_tensor(out=stat[:, 112:128], in0=stat[:, 96:112],
                        in1=stat[:, 16:32], op=ALU.mult)
        VF(e)
        e.tensor_scalar(out=stat[:, 112:128], in0=stat[:, 112:128],
                        scalar1=1e-5, scalar2=None, op0=ALU.max)
        VF(e, 1)
        e.tensor_scalar(out=stat[:, 80:96], in0=stat[:, 112:128],
                        scalar1=wsc2[:, 0:1], scalar2=float(np.float32(1.0 / 127.0)),
                        op0=ALU.mult, op1=ALU.mult)
        e.reciprocal(out=stat[:, 112:128], in_=stat[:, 112:128])
        VF(e)
        e.tensor_scalar(out=stat[:, 112:128], in0=stat[:, 112:128],
                        scalar1=127.0, scalar2=None, op0=ALU.mult)
        VF(e)
        e.tensor_tensor(out=stat[:, 64:80], in0=stat[:, 16:32],
                        in1=stat[:, 112:128], op=ALU.mult)
        VF(e).then_inc(s_v, 1)
        cnt["v"] += 1
    VE(ln1_c3)
    v_c3 = cnt["v"]

    def ln1_apply(e, v_c3=v_c3):
        e.wait_ge(s_v, v_c3)
        for tt in range(NTT):
            li = e.activation(out=sbf[:, C * tt:C * tt + C],
                              in_=sbf[:, C * tt:C * tt + C], func=AF.Identity,
                              bias=stat[:, 48 + tt:49 + tt],
                              scale=stat[:, 32 + tt:33 + tt])
        li.then_inc(s_s, 1)
        cnt["s"] += 1
    SE(ln1_apply)
    s_y = cnt["s"]
    phase("ln1")

    ffn_v = {}
    ffn_t = {}
    out_sent = []

    for grp4 in range(4):
        for ti in range(4):
            tt = 4 * grp4 + ti

            def xquant(e, tt=tt):
                e.wait_ge(s_s, s_y)
                if ("xqfree",) in ffn_t:
                    e.wait_ge(s_t, ffn_t[("xqfree",)])
                e.tensor_scalar(out=wi32, in0=sbf[:, C * tt:C * tt + C],
                                scalar1=stat[:, 64 + tt:65 + tt],
                                scalar2=None, op0=ALU.mult)
                e.tensor_copy(out=xq, in_=wi32).then_inc(s_v, 1)
                cnt["v"] += 1
            VE(xquant)
            v_xq = cnt["v"]

            def xqt(e, v_xq=v_xq):
                e.wait_ge(s_v, v_xq)
                if ("xqTev",) in ffn_v:
                    e.wait_ge(s_v, ffn_v[("xqTev",)])
                for c8 in range(8):
                    li = e.transpose(out=tpb[:, 128 * c8:128 * c8 + 128],
                                     in_=xq[:, 128 * c8:128 * c8 + 128],
                                     identity=eyeb_s)
                li.then_inc(s_t, 1)
                cnt["t"] += 1
            TE(xqt)
            ffn_t[("xqfree",)] = cnt["t"]
            t_xq = cnt["t"]

            def xqev(e, t_xq=t_xq):
                e.wait_ge(s_t, t_xq)
                e.tensor_copy(out=xqT, in_=tpb).then_inc(s_v, 1)
                cnt["v"] += 1
            VE(xqev)
            ffn_v[("xqTev",)] = cnt["v"]
            v_xqT = cnt["v"]

            for fg in range(8):
                def f1mm(e, tt=tt, fg=fg, v_xqT=v_xqT):
                    e.wait_ge(s_v, v_xqT)
                    e.wait_ge(s_v, v_w1_done)
                    if ("f1ev", fg % 3) in ffn_v:
                        e.wait_ge(s_v, ffn_v[("f1ev", fg % 3)])
                    bank = f1b[:, 512 * (fg % 3):512 * (fg % 3) + 512]
                    for c8 in range(8):
                        sl = xqT[:, 128 * c8:128 * c8 + 128]
                        e.ldweights(sl)
                        li = e.matmul(out=bank, lhsT=sl,
                                      rhs=w1q[:, FF * c8 + 512 * fg:
                                              FF * c8 + 512 * fg + 512],
                                      start=(c8 == 0), stop=(c8 == 7))
                    li.then_inc(s_t, 1)
                    cnt["t"] += 1
                TE(f1mm)
                t_f1 = cnt["t"]

                def f1ev(e, tt=tt, fg=fg, t_f1=t_f1):
                    e.wait_ge(s_t, t_f1)
                    bank = f1b[:, 512 * (fg % 3):512 * (fg % 3) + 512]
                    e.scalar_tensor_tensor(
                        out=hbuf[:, 512 * fg:512 * fg + 512], in0=bank,
                        scalar=stat[:, 80 + tt:81 + tt],
                        in1=b1bc[:, 512 * fg:512 * fg + 512],
                        op0=ALU.mult, op1=ALU.add).then_inc(s_v, 1)
                    cnt["v"] += 1
                VE(f1ev)
                ffn_v[("f1ev", fg % 3)] = cnt["v"]
                v_h = cnt["v"]

                def f1g(e, fg=fg, v_h=v_h):
                    e.wait_ge(s_v, v_h)
                    e.activation(out=hbuf[:, 512 * fg:512 * fg + 512],
                                 in_=hbuf[:, 512 * fg:512 * fg + 512],
                                 func=AF.Gelu,
                                 bias=np.float32(0.0)).then_inc(s_s, 1)
                    cnt["s"] += 1
                SE(f1g)
                ffn_v[("gelu", tt, fg)] = cnt["s"]

            def gsq(e, tt=tt, ti=ti):
                e.wait_ge(s_s, ffn_v[("gelu", tt, 7)])
                if ("qgfree",) in ffn_t:
                    e.wait_ge(s_t, ffn_t[("qgfree",)])
                e.activation(out=qg, in_=hbuf, func=AF.Square,
                             bias=np.float32(0.0),
                             accum_out=stat[:, ti:ti + 1]).then_inc(s_s, 1)
                cnt["s"] += 1
            SE(gsq)
            s_sq = cnt["s"]

            def gst(e, ti=ti, s_sq=s_sq):
                e.wait_ge(s_s, s_sq)
                e.reduce_max(out=stat[:, 4 + ti:5 + ti], in_=hbuf, axis=AX.X,
                             apply_absolute_value=True)
                e.tensor_scalar(out=stat[:, ti:ti + 1], in0=stat[:, ti:ti + 1],
                                scalar1=float(np.float32(1.0 / FF)), scalar2=1e-5,
                                op0=ALU.mult, op1=ALU.add)
                VF(e, 1)
                e.tensor_scalar(out=stat[:, 12:13], in0=stat[:, 4 + ti:5 + ti],
                                scalar1=1e-12, scalar2=None, op0=ALU.max)
                VF(e)
                e.reciprocal(out=stat[:, 12:13], in_=stat[:, 12:13])
                VF(e)
                e.tensor_scalar(out=stat[:, 12:13], in0=stat[:, 12:13],
                                scalar1=127.0, scalar2=None,
                                op0=ALU.mult)
                VF(e).then_inc(s_v, 1)
                cnt["v"] += 1
            VE(gst)
            v_gs = cnt["v"]

            def gquant(e, v_gs=v_gs):
                e.wait_ge(s_v, v_gs)
                for q in range(4):
                    e.tensor_scalar(out=wi32,
                                    in0=hbuf[:, 1024 * q:1024 * q + 1024],
                                    scalar1=stat[:, 12:13], scalar2=None,
                                    op0=ALU.mult)
                    li = e.tensor_copy(out=qg[:, 1024 * q:1024 * q + 1024],
                                       in_=wi32)
                li.then_inc(s_v, 1)
                cnt["v"] += 1
            VE(gquant)
            v_qg = cnt["v"]

            for sub in range(2):
                def qgt(e, sub=sub, v_qg=v_qg):
                    e.wait_ge(s_v, v_qg)
                    if ("qgTev",) in ffn_v:
                        e.wait_ge(s_v, ffn_v[("qgTev",)])
                    for k16 in range(16):
                        bank = tpb if k16 < 8 else tpb2
                        li = e.transpose(out=bank[:, 128 * (k16 % 8):
                                                  128 * (k16 % 8) + 128],
                                         in_=qg[:, 2048 * sub + 128 * k16:
                                                2048 * sub + 128 * k16 + 128],
                                         identity=eyeb_s)
                    li.then_inc(s_t, 1)
                    cnt["t"] += 1
                TE(qgt)
                ffn_t[("qgfree",)] = cnt["t"]
                tq = cnt["t"]

                def qgev(e, sub=sub, tq=tq):
                    e.wait_ge(s_t, tq)
                    e.tensor_copy(out=qgT[:, 2048 * sub:2048 * sub + 1024],
                                  in_=tpb)
                    e.tensor_copy(out=qgT[:, 2048 * sub + 1024:2048 * sub + 2048],
                                  in_=tpb2).then_inc(s_v, 1)
                    cnt["v"] += 1
                VE(qgev)
                ffn_v[("qgTev",)] = cnt["v"]
            v_qgT = cnt["v"]

            for dg in range(2):
                def f2mm(e, tt=tt, dg=dg, v_qgT=v_qgT):
                    e.wait_ge(s_v, v_qgT)
                    e.wait_ge(s_v, v_w2_done)
                    if ("f2ev", dg) in ffn_v:
                        e.wait_ge(s_v, ffn_v[("f2ev", dg)])
                    bank = f2b[:, 512 * dg:512 * dg + 512]
                    for c32 in range(32):
                        sl = qgT[:, 128 * c32:128 * c32 + 128]
                        e.ldweights(sl)
                        li = e.matmul(out=bank, lhsT=sl,
                                      rhs=w2q[:, C * c32 + 512 * dg:
                                              C * c32 + 512 * dg + 512],
                                      start=(c32 == 0), stop=(c32 == 31))
                    li.then_inc(s_t, 1)
                    cnt["t"] += 1
                TE(f2mm)
                t_f2 = cnt["t"]

                def f2ev(e, tt=tt, ti=ti, dg=dg, t_f2=t_f2):
                    e.wait_ge(s_t, t_f2)
                    e.tensor_copy(out=h2raw[:, C * ti + 512 * dg:
                                            C * ti + 512 * dg + 512],
                                  in_=f2b[:, 512 * dg:512 * dg + 512]
                                  ).then_inc(s_v, 1)
                    cnt["v"] += 1
                VE(f2ev)
                ffn_v[("f2ev", dg)] = cnt["v"]
        v_h2 = cnt["v"]

        def ep_s1(e, v_h2=v_h2):
            e.wait_ge(s_v, v_h2)
            e.activation(out=stat[:, 8:12], in_=stat[:, 0:4], func=AF.Sqrt,
                         bias=np.float32(0.0)).then_inc(s_s, 1)
            cnt["s"] += 1
        SE(ep_s1)
        s_e1 = cnt["s"]

        def ep_v1(e, s_e1=s_e1):
            e.wait_ge(s_s, s_e1)
            e.reciprocal(out=stat[:, 8:12], in_=stat[:, 8:12])
            VF(e)
            e.tensor_tensor(out=stat[:, 8:12], in0=stat[:, 8:12],
                            in1=stat[:, 4:8], op=ALU.mult)
            VF(e)
            e.tensor_scalar(out=stat[:, 8:12], in0=stat[:, 8:12],
                            scalar1=1e-5, scalar2=wsc2[:, 3:4],
                            op0=ALU.max, op1=ALU.mult)
            VF(e).then_inc(s_v, 1)
            cnt["v"] += 1
        VE(ep_v1)
        v_sig = cnt["v"]

        for ti in range(4):
            tt = 4 * grp4 + ti

            def ep_z(e, tt=tt, ti=ti, v_sig=v_sig):
                e.wait_ge(s_v, v_sig)
                e.scalar_tensor_tensor(out=tscr,
                                       in0=h2raw[:, C * ti:C * ti + C],
                                       scalar=stat[:, 8 + ti:9 + ti],
                                       in1=b2bc, op0=ALU.mult, op1=ALU.add)
                e.tensor_tensor(out=sbf[:, C * tt:C * tt + C], in0=tscr,
                                in1=sbf[:, C * tt:C * tt + C], op=ALU.add)
                sl = sbf[:, C * tt:C * tt + C]
                e.reduce_sum(out=stat[:, 16 + ti:17 + ti], in_=sl, axis=AX.X)
                e.scalar_tensor_tensor(out=pt[1][:, 0:C], in0=sl, scalar=1.0,
                                       in1=sl, op0=ALU.mult, op1=ALU.mult,
                                       accum_out=stat[:, 20 + ti:21 + ti]
                                       ).then_inc(s_v, 1)
                cnt["v"] += 1
            VE(ep_z)
        v_z4 = cnt["v"]

        def ep_v2(e, v_z4=v_z4):
            e.wait_ge(s_v, v_z4)
            e.tensor_scalar(out=stat[:, 16:20], in0=stat[:, 16:20],
                            scalar1=float(np.float32(-1.0 / C)), scalar2=None,
                            op0=ALU.mult)
            VF(e, 1)
            e.tensor_scalar(out=stat[:, 20:24], in0=stat[:, 20:24],
                            scalar1=float(np.float32(1.0 / C)), scalar2=None, op0=ALU.mult)
            VF(e, 1)
            e.tensor_tensor(out=stat[:, 28:32], in0=stat[:, 16:20],
                            in1=stat[:, 16:20], op=ALU.mult)
            VF(e)
            e.tensor_tensor(out=stat[:, 20:24], in0=stat[:, 20:24],
                            in1=stat[:, 28:32], op=ALU.subtract)
            cnt["v"] += 1
            VF(e, 1).then_inc(s_v, 1)
        VE(ep_v2)
        v_l2 = cnt["v"]

        def ep_s2(e, v_l2=v_l2):
            e.wait_ge(s_v, v_l2)
            e.activation(out=stat[:, 24:28], in_=stat[:, 20:24], func=AF.Sqrt,
                         bias=np.float32(1e-5)).then_inc(s_s, 1)
            cnt["s"] += 1
        SE(ep_s2)
        s_l2 = cnt["s"]

        def ep_v3(e, s_l2=s_l2):
            e.wait_ge(s_s, s_l2)
            e.reciprocal(out=stat[:, 24:28], in_=stat[:, 24:28])
            VF(e)
            e.tensor_tensor(out=stat[:, 28:32], in0=stat[:, 16:20],
                            in1=stat[:, 24:28], op=ALU.mult)
            VF(e).then_inc(s_v, 1)
            cnt["v"] += 1
        VE(ep_v3)
        v_l2r = cnt["v"]

        for ti in range(4):
            tt = 4 * grp4 + ti

            def ep_out(e, tt=tt, ti=ti, v_l2r=v_l2r):
                e.wait_ge(s_v, v_l2r)
                if len(out_sent) >= 2:
                    e.wait_ge(s_out, out_sent[-2])
                e.activation(out=ostg[tt % 2], in_=sbf[:, C * tt:C * tt + C],
                             func=AF.Identity, bias=stat[:, 28 + ti:29 + ti],
                             scale=stat[:, 24 + ti:25 + ti]).then_inc(s_s, 1)
                cnt["s"] += 1
            SE(ep_out)
            s_o = cnt["s"]

            def ep_dma(e, tt=tt, s_o=s_o):
                e.wait_ge(s_s, s_o)
                e.dma_start(out=bass.AP(out_d.tensor, 128 * tt * C,
                                        [[C, 128], [1, C]]),
                            in_=ostg[tt % 2]).then_inc(s_out, 16)
                cnt["out"] += 16
            GP(ep_dma)
            out_sent.append(cnt["out"])

    if DEBUG_DUMP:
        dump_chunks = [
            [(0, 4, wsc2), (8, 16, wsc[0:1, 0:16]), (32, 128, stat)],
            [(0, 1024, w1q[:, 0:1024])],
            [(0, 1024, w1q[:, 4096 * 7 + 2048:4096 * 7 + 3072])],
            [(0, 1024, w2q[:, 0:1024])],
            [(0, 1024, sbf[:, 0:1024])],
            [(0, 1024, sbf[:, 15 * 1024:16 * 1024])],
            [(0, 1024, h2raw[:, 0:1024])],
            [(0, 1024, h2raw[:, 3 * 1024:4 * 1024])],
            [(0, 1024, hbuf[:, 0:1024])],
            [(0, 1024, hbuf[:, 2048:3072])],
            [(0, 1024, qg[:, 0:1024])],
            [(0, 1024, qgT[:, 0:1024])],
            [(0, 1024, xq)],
            [(0, 1024, xqT)],
            [(0, 1024, b2bc)],
            [(0, 1024, tscr)],
        ]

        dump_base = cnt["out"]
        dump_v = []

        def dump_cp(e):
            for sem, key in [(s_ld, "ld"), (s_gd, "gd"), (s_t, "t"),
                             (s_s, "s"), (s_cc, "cc"), (s_out, "out")]:
                e.wait_ge(sem, cnt[key])
            for i, grp in enumerate(dump_chunks):
                bb = i % 2
                if i >= 2:
                    e.wait_ge(s_out, dump_base + 16 * (i - 1))
                e.memset(ostg[bb], 0.0)
                li = None
                for lo, w, src in grp:
                    pp = 1 if (lo == 8) else 128
                    li = e.tensor_copy(out=ostg[bb][0:pp, lo:lo + w], in_=src)
                li.then_inc(s_v, 1)
                cnt["v"] += 1
                dump_v.append(cnt["v"])
        VE(dump_cp)

        def dump_dma(e):
            for i in range(len(dump_chunks)):
                e.wait_ge(s_v, dump_v[i])
                e.dma_start(out=bass.AP(dbg_d.tensor, 128 * i * C,
                                        [[C, 128], [1, C]]),
                            in_=ostg[i % 2]).then_inc(s_out, 16)
                cnt["out"] += 16
        GP(dump_dma)

    def fin(e):
        e.wait_ge(s_out, cnt["out"])
    GP(fin)

    for c in reversed(semctx):
        c.__exit__(None, None, None)
    ctx.__exit__(None, None, None)
    if upto is None:
        _CACHE["nc"] = nc
    return nc


def kernel(L0_attn_x, L0_attn_w_qkv, L0_attn_b_qkv, L0_attn_w_proj,
           L0_attn_b_proj, L0_attn_pos_independent, L0_ffn_w1, L0_ffn_b1,
           L0_ffn_w2, L0_ffn_b2):
    phi, psi, tri, eyef, eyeb = _host_tables()
    x = np.ascontiguousarray(np.asarray(L0_attn_x, dtype=np.float32))
    wqkv = np.asarray(L0_attn_w_qkv, dtype=np.float32)
    bqkv = np.asarray(L0_attn_b_qkv, dtype=np.float32)
    wproj = np.asarray(L0_attn_w_proj, dtype=np.float32)
    bproj = np.ascontiguousarray(np.asarray(L0_attn_b_proj, dtype=np.float32))
    pos_ind = np.ascontiguousarray(np.asarray(L0_attn_pos_independent,
                                              dtype=np.float32))
    w1 = np.asarray(L0_ffn_w1, dtype=np.float32)
    b1 = np.ascontiguousarray(np.asarray(L0_ffn_b1, dtype=np.float32))
    w2 = np.asarray(L0_ffn_w2, dtype=np.float32)
    b2 = np.ascontiguousarray(np.asarray(L0_ffn_b2, dtype=np.float32))

    zeros_tab = np.zeros_like(phi)
    zslab = np.zeros((8, 64), np.float32)
    pslab = np.ascontiguousarray(pos_ind.reshape(8, 64))
    nc = build_nc()

    in_maps = []
    for c in range(8):
        b, p = c // 2, c % 2
        perm = np.concatenate([np.arange(512) + 512 * p,
                               np.arange(512) + 512 * (1 - p)])
        im = dict(
            x_full=x[b],
            x_own=np.ascontiguousarray(x[b, 2048 * p:2048 * p + 2048]),
            wq_t=np.ascontiguousarray(wqkv[512 * p:512 * p + 512].T),
            wk_t=np.ascontiguousarray(
                wqkv[1024 + 512 * p:1024 + 512 * p + 512].T),
            wv_t=np.ascontiguousarray(
                wqkv[2048 + 512 * p:2048 + 512 * p + 512].T),
            bq_in=np.ascontiguousarray(bqkv[512 * p:512 * p + 512]),
            bk_in=np.ascontiguousarray(bqkv[1024 + 512 * p:1024 + 512 * p + 512]),
            bv_in=np.ascontiguousarray(bqkv[2048 + 512 * p:2048 + 512 * p + 512]),
            wproj_t=np.ascontiguousarray(wproj.T[perm, :]),
            bproj_in=bproj,
            w1_t=np.ascontiguousarray(w1.T),
            b1_in=b1,
            w2_t=np.ascontiguousarray(w2.T),
            b2_in=b2,
            pos_slab=(zslab if p == 0 else pslab),
            phi_in=(phi if p == 0 else zeros_tab),
            psi_in=(psi if p == 0 else zeros_tab),
            ebias_in=np.full(8, -(32.0 + MARGIN) if p == 0 else -MARGIN,
                             dtype=np.float32),
            tri_in=tri,
            eyef_in=eyef,
            eyeb_in=eyeb,
            psel=np.array([[p]], dtype=np.int32),
        )
        in_maps.append(im)

    res = run_bass_kernel_spmd(nc, in_maps, core_ids=list(range(8)))
    _LAST_RES.clear()
    _LAST_RES.extend(res.results if hasattr(res, "results") else res)
    out = np.empty((4, T, C), dtype=np.float32)
    for c in range(8):
        b, p = c // 2, c % 2
        out[b, 2048 * p:2048 * p + 2048] = np.asarray(res.results[c]["out"])
    return out


# version 03
# speedup vs baseline: 1.0109x; 1.0109x over previous
import math
import numpy as np
import ml_dtypes
import concourse.bass as bass
import concourse.mybir as mybir
from concourse.bass_utils import run_bass_kernel_spmd

F32 = mybir.dt.float32
BF16 = mybir.dt.bfloat16
I32 = mybir.dt.int32
F8 = mybir.dt.float8e4
AF = mybir.ActivationFunctionType
ALU = mybir.AluOpType
AX = mybir.AxisListType
f32 = np.float32

T = 4096
C = 1024
FF = 4096
NTT = 16
MARGIN = 10.0

_CACHE = {}
DEBUG_DUMP = False
_LAST_RES = []


def _host_tables():
    if "tabs" in _CACHE:
        return _CACHE["tabs"]
    dim = C // 2
    pos = np.arange(dim, dtype=np.float64)
    freqs = 1.0 / 10000.0 ** (pos / dim)
    active = int(np.sum(freqs * 2 * math.pi * T >= 1.0))
    active = min(active, dim - C // 4)
    assert active == 256
    fr = freqs[:active]
    th = np.arange(T, dtype=np.float64)[:, None] * fr[None, :]
    cos, sin = np.cos(th), np.sin(th)
    phi = np.zeros((8, 64, T), np.float64)
    psi = np.zeros((8, 64, T), np.float64)
    for h in range(8):
        sl = slice(32 * h, 32 * h + 32)
        phi[h, 0:32] = cos[:, sl].T
        phi[h, 32:64] = sin[:, sl].T
        psi[h, 0:32] = (cos[:, sl] - sin[:, sl]).T
        psi[h, 32:64] = (cos[:, sl] + sin[:, sl]).T
    kq = np.arange(128)
    tabs = (phi.astype(ml_dtypes.bfloat16), psi.astype(ml_dtypes.bfloat16),
            (kq[:, None] <= kq[None, :]).astype(ml_dtypes.bfloat16),
            np.eye(128, dtype=np.float32), np.eye(128, dtype=ml_dtypes.bfloat16))
    _CACHE["tabs"] = tabs
    return tabs


def _groups(ntk, w=3):
    return [list(range(i, min(i + w, ntk))) for i in range(0, ntk, w)]


class _StopEmission(Exception):
    pass


def build_nc(upto=None):
    if upto is None and "nc" in _CACHE:
        return _CACHE["nc"]
    nc = bass.Bass("TRN2", target_bir_lowering=False,
                   detect_race_conditions=False)
    dram = lambda n, s, d, k="ExternalInput": nc.dram_tensor(n, s, d, kind=k).ap()

    x_full = dram("x_full", [T, C], F32)
    x_own = dram("x_own", [2048, C], F32)
    wq_t = dram("wq_t", [C, 512], F32)
    wk_t = dram("wk_t", [C, 512], F32)
    wv_t = dram("wv_t", [C, 512], F32)
    bq_in = dram("bq_in", [512], F32)
    bk_in = dram("bk_in", [512], F32)
    bv_in = dram("bv_in", [512], F32)
    wproj_t = dram("wproj_t", [C, C], F32)
    bproj_in = dram("bproj_in", [C], F32)
    w1_t = dram("w1_t", [C, FF], F32)
    b1_in = dram("b1_in", [FF], F32)
    w2_t = dram("w2_t", [FF, C], F32)
    b2_in = dram("b2_in", [C], F32)
    pos_slab = dram("pos_slab", [8, 64], F32)
    phi_in = dram("phi_in", [8, 64, T], BF16)
    psi_in = dram("psi_in", [8, 64, T], BF16)
    ebias_in = dram("ebias_in", [8], F32)
    tri_in = dram("tri_in", [128, 128], BF16)
    eyef_in = dram("eyef_in", [128, 128], F32)
    eyeb_in = dram("eyeb_in", [128, 128], BF16)
    psel = dram("psel", [1, 1], I32)
    out_d = dram("out", [2048, C], F32, "ExternalOutput")
    if DEBUG_DUMP:
        dbg_d = dram("dbg", [2048, C], F32, "ExternalOutput")

    scr8 = nc.dram_tensor("scr8", [8], F32).ap()
    scr1 = nc.dram_tensor("scr1", [4], F32).ap()
    scr_rc = nc.dram_tensor("scr_rc", [2048], BF16).ap()
    gift = nc.dram_tensor("gift", [128, 4 * 2048], BF16).ap()
    gath = nc.dram_tensor("gath", [2, 128, 4 * 2048], BF16).ap()

    SB_BASE = 17152
    def at(name, shape, dt, off):
        nbytes = int(np.prod(shape[1:])) * mybir.dt.size(dt)
        assert SB_BASE + off + nbytes <= 229376, (name, off, nbytes)
        return nc.alloc_sbuf_tensor_at(name, shape, dt, offset=SB_BASE + off).ap()

    xT = at("xT", [128, 8 * T], BF16, 0)
    w1q = at("w1q", [128, 8 * FF], F8, 0)
    w2q = at("w2q", [128, 32 * C], F8, 32768)
    vbuf = at("vbuf", [128, 32 * 520], BF16, 65536)
    sbf = at("sbf", [128, NTT * C], BF16, 65536)
    kaug = [at(f"kaug{i}", [128, T], BF16, 98816 + 8192 * i) for i in range(3)]
    pstage = at("pstage", [128, 4 * 2048], BF16, 98816)
    hbuf = at("hbuf", [128, FF], F32, 98816)
    peerO = at("peerO", [128, 4 * 2048], BF16, 115200)
    ostg = [at(f"ostg{i}", [128, C], F32, 115200 + 4096 * i) for i in range(2)]
    qaug = [at(f"qaug{i}", [128, T], BF16, 123392 + 8192 * i) for i in range(3)]
    xres = [at(f"xres{i}", [128, C], F32, 131584 + 4096 * i) for i in range(2)]
    tscr = at("tscr", [128, C], F32, 139776)
    wi32 = at("wi32", [128, C], I32, 139776)
    bpbc = at("bpbc", [128, C], BF16, 143872)
    b2bc = at("b2bc", [128, C], BF16, 145920)
    onorm = at("onorm", [128, 4 * T], BF16, 147968)
    wvb = at("wvb", [128, FF], BF16, 147968)
    qg = at("qg", [128, FF], BF16, 147968)
    wpb = at("wpb", [128, 8 * C], BF16, 147968)
    qgT = at("qgT", [128, FF], BF16, 156160)
    xq = at("xq", [128, C], BF16, 164352)
    xqT = at("xqT", [128, C], BF16, 166400)
    b1bc = at("b1bc", [128, FF], BF16, 168448)
    h2raw = at("h2raw", [128, 4 * C], BF16, 176640)
    eyef_s = at("eyef_s", [128, 128], F32, 180736)
    pslab_s = at("pslab_s", [8, 64], F32, 181248)
    csum = at("csum", [8, 1], F32, 181504)
    ebrow = at("ebrow", [8, 1], F32, 181536)
    ebt = at("ebt", [8, 1], F32, 181568)
    wstage = at("wstage", [128, 2048], F32, 186880)
    wstgb = at("wstgb", [128, 2048], BF16, 195072)
    pt = [at("pt0", [128, 1536], BF16, 199424),
          at("pt1", [128, 1536], BF16, 202496)]
    rcbc = at("rcbc", [128, 2048], BF16, 205568)
    tri_s = at("tri_s", [128, 128], BF16, 209664)
    eyeb_s = at("eyeb_s", [128, 128], BF16, 209920)
    ebias_bc = at("ebias_bc", [128, 8], F32, 210176)
    bq8 = at("bq8", [128, 4], F32, 210208)
    bk_s = at("bk_s", [128, 4], F32, 210240)
    pb_s = at("pb_s", [1, 1], I32, 210272)
    bvbc = at("bvbc", [128, 512], BF16, 210304)
    stat = at("stat", [128, 128], F32, 211328)
    ones64 = at("ones64", [128, 64], F32, 211840)
    wsc = at("wsc", [1, 16], F32, 212096)
    wsc2 = at("wsc2", [128, 4], F32, 212160)
    vfill = at("vfill", [128, 64], F32, 181632)

    psum_ctx = [nc.psum_tensor("sh0", [128, 1536], F32),
                nc.psum_tensor("sh1", [128, 1536], F32),
                nc.psum_tensor("otb", [128, 512], F32),
                nc.psum_tensor("aux", [128, 512], F32)]
    sh0, sh1, otb, aux = [c.__enter__().ap() for c in psum_ctx]
    SHP = [sh0, sh1]

    ctx = nc.Block()
    block = ctx.__enter__()
    semctx = [nc.semaphore(n) for n in
              ["s_ld", "s_gd", "s_t", "s_v", "s_s", "s_cc", "s_out"]]
    s_ld, s_gd, s_t, s_v, s_s, s_cc, s_out = [c.__enter__() for c in semctx]
    cnt = dict(ld=0, gd=0, t=0, v=0, s=0, cc=0, out=0)

    stopped = [False]

    def VF(e, n=2):
        li = None
        for _ in range(n):
            li = e.tensor_scalar(out=vfill, in0=vfill, scalar1=1.0,
                                 scalar2=None, op0=ALU.mult)
        return li

    def SY(fn):
        if stopped[0]:
            return
        @block.sync
        def _(e):
            fn(e)

    def GP(fn):
        if stopped[0]:
            return
        @block.gpsimd
        def _(e):
            fn(e)

    def TE(fn):
        if stopped[0]:
            return
        @block.tensor
        def _(e):
            fn(e)

    def VE(fn):
        if stopped[0]:
            return
        @block.vector
        def _(e):
            fn(e)

    def SE(fn):
        if stopped[0]:
            return
        @block.scalar
        def _(e):
            fn(e)

    def phase(name):
        if stopped[0] or upto is None or name != upto:
            return
        def drain(e):
            for sem, key in [(s_ld, "ld"), (s_gd, "gd"), (s_t, "t"),
                             (s_v, "v"), (s_s, "s"), (s_cc, "cc")]:
                e.wait_ge(sem, cnt[key])
            e.memset(ostg[0], 0.0)
            e.dma_start(out=bass.AP(out_d.tensor, 0, [[C, 128], [1, C]]),
                        in_=ostg[0]).then_inc(s_out, 16)
            cnt["out"] += 16
            e.wait_ge(s_out, cnt["out"])
        GP(drain)
        stopped[0] = True

    def p0_sync(e):
        for dst, src in [(pb_s, psel), (tri_s, tri_in), (eyef_s, eyef_in),
                         (eyeb_s, eyeb_in), (pslab_s, pos_slab)]:
            e.dma_start(out=dst, in_=src).then_inc(s_ld, 16)
            cnt["ld"] += 16
        for dst, src in [(bq8, bq_in), (bk_s, bk_in)]:
            e.dma_start(out=dst,
                        in_=bass.AP(src.tensor, 0, [[1, 128], [128, 4]]),
                        allow_slow_non_contiguous=True).then_inc(s_ld, 16)
            cnt["ld"] += 16
        e.dma_start(out=ebrow,
                    in_=bass.AP(ebias_in.tensor, 0, [[1, 8], [1, 1]])
                    ).then_inc(s_ld, 16)
        cnt["ld"] += 16
        e.dma_start(out=wstage[:, 0:512],
                    in_=bass.AP(bv_in.tensor, 0, [[0, 128], [1, 512]])
                    ).then_inc(s_ld, 16)
        cnt["ld"] += 16
    SY(p0_sync)
    ld_p0 = cnt["ld"]

    def p0_vec(e):
        e.wait_ge(s_ld, ld_p0)
        e.reduce_sum(out=csum, in_=pslab_s, axis=AX.X)
        e.memset(bass.AP(vbuf.tensor, 64,
                         [[520 * 32, 128], [520, 32], [65, 8], [1, 1]]), 1.0)
        e.memset(ones64, 1.0)
        e.tensor_tensor(out=ebt, in0=ebrow, in1=csum, op=ALU.add)
        e.memset(stat, 0.0)
        e.memset(wsc, 0.0)
        e.tensor_tensor(out=ebrow, in0=ebt, in1=csum, op=ALU.subtract)
        e.memset(vfill, 0.0)
        e.tensor_scalar(out=bq8, in0=bq8, scalar1=0.125, scalar2=None,
                        op0=ALU.mult)
        e.tensor_copy(out=bvbc, in_=wstage[:, 0:512]).then_inc(s_v, 1)
        cnt["v"] += 1
    VE(p0_vec)
    v_p0 = cnt["v"]

    def p0_gp(e):
        e.wait_ge(s_v, v_p0)
        e.dma_start(out=scr8,
                    in_=bass.AP(ebrow.tensor, 0, [[1, 8], [1, 1]])
                    ).then_inc(s_gd, 16)
        cnt["gd"] += 16
        e.wait_ge(s_gd, cnt["gd"])
        for h in range(8):
            e.dma_start(out=ebias_bc[:, h:h + 1],
                        in_=bass.AP(scr8.tensor, h, [[0, 128], [1, 1]])
                        ).then_inc(s_gd, 16)
            cnt["gd"] += 16
    GP(p0_gp)
    phase("p0")

    xck_evac = [0] * 16
    for ck in range(16):
        def x_ld(e, ck=ck):
            if ck == 0:
                e.wait_ge(s_v, v_p0)
            else:
                e.wait_ge(s_v, xck_evac[ck - 1])
            e.dma_start(out=wstage,
                        in_=bass.AP(x_full.tensor, (256 * ck) * C,
                                    [[C, 128], [128 * C, 2], [1, C]])
                        ).then_inc(s_ld, 16)
            cnt["ld"] += 16
        SY(x_ld)
        ldk = cnt["ld"]

        def x_tp(e, ck=ck, ldk=ldk):
            e.wait_ge(s_ld, ldk)
            if ck >= 1:
                e.wait_ge(s_v, xck_evac[ck - 1])
            li = None
            for a in range(2):
                bank = sh0 if a == 0 else sh1
                for c8 in range(8):
                    li = e.transpose(out=bass.AP(bank.tensor, 128 * c8,
                                                 [[1536, 128], [1, 128]]),
                                     in_=wstage[:, 1024 * a + 128 * c8:
                                                1024 * a + 128 * c8 + 128],
                                     identity=eyef_s)
            li.then_inc(s_t, 1)
            cnt["t"] += 1
        TE(x_tp)
        t_tp = cnt["t"]

        def x_ev(e, ck=ck, t_tp=t_tp):
            e.wait_ge(s_t, t_tp)
            li = None
            for a in range(2):
                bank = sh0 if a == 0 else sh1
                for c8 in range(8):
                    li = e.tensor_copy(
                        out=xT[:, T * c8 + 256 * ck + 128 * a:
                               T * c8 + 256 * ck + 128 * a + 128],
                        in_=bank[:, 128 * c8:128 * c8 + 128])
            li.then_inc(s_v, 1)
            cnt["v"] += 1
        VE(x_ev)
        xck_evac[ck] = cnt["v"]
    phase("xT")

    wv_cast_v = [0, 0]
    for half in range(2):
        def wv_ld(e, half=half):
            e.wait_ge(s_v, xck_evac[15] if half == 0 else wv_cast_v[0])
            e.dma_start(out=wstage,
                        in_=bass.AP(wv_t.tensor, 256 * half,
                                    [[512, 128], [512 * 128, 8], [1, 256]])
                        ).then_inc(s_ld, 16)
            cnt["ld"] += 16
        SY(wv_ld)
        ldw = cnt["ld"]

        def wv_c(e, half=half, ldw=ldw):
            e.wait_ge(s_ld, ldw)
            e.tensor_copy(out=bass.AP(wvb.tensor, 256 * half,
                                      [[FF, 128], [512, 8], [1, 256]]),
                          in_=wstage).then_inc(s_v, 1)
            cnt["v"] += 1
        VE(wv_c)
        wv_cast_v[half] = cnt["v"]
    v_wvb = cnt["v"]

    vm_evac = [0] * 32
    for tt in range(32):
        def vmm(e, tt=tt):
            e.wait_ge(s_v, v_wvb)
            if tt >= 2:
                e.wait_ge(s_v, vm_evac[tt - 2])
            bank = otb if tt % 2 == 0 else aux
            for c8 in range(8):
                sl = xT[:, T * c8 + 128 * tt: T * c8 + 128 * tt + 128]
                e.ldweights(sl)
                li = e.matmul(out=bank, lhsT=sl,
                              rhs=wvb[:, 512 * c8:512 * c8 + 512],
                              start=(c8 == 0), stop=(c8 == 7))
            li.then_inc(s_t, 1)
            cnt["t"] += 1
        TE(vmm)
        t_vm = cnt["t"]

        def vev(e, tt=tt, t_vm=t_vm):
            e.wait_ge(s_t, t_vm)
            bank = otb if tt % 2 == 0 else aux
            e.scalar_tensor_tensor(
                out=bass.AP(vbuf.tensor, 520 * tt,
                            [[520 * 32, 128], [65, 8], [1, 64]]),
                in0=bass.AP(bank.tensor, 0, [[512, 128], [64, 8], [1, 64]]),
                scalar=1.0,
                in1=bass.AP(bvbc.tensor, 0, [[512, 128], [64, 8], [1, 64]]),
                op0=ALU.mult, op1=ALU.add).then_inc(s_v, 1)
            cnt["v"] += 1
        VE(vev)
        vm_evac[tt] = cnt["v"]
    phase("vmm")

    kq_ready = [0] * 8
    kq_aug_ld = [0] * 8
    exp_cp = {}
    mask_cp = {}
    pv_cp = {}
    sc_cp = {}
    head_last_sc = [0] * 8
    head_norm_v = [0] * 8
    stage_free = {"s": 0, "v": 0}

    def emit_pair_fill(hp, guard_t):
        b0, b1 = (2 * hp) % 3, (2 * hp + 1) % 3

        def w_ld(e, hp=hp):
            if hp == 0:
                e.wait_ge(s_v, v_wvb)
            else:
                e.wait_ge(s_v, stage_free["v"])
            e.dma_start(out=wstage[:, 0:1024],
                        in_=bass.AP(wk_t.tensor, 128 * hp,
                                    [[512, 128], [512 * 128, 8], [1, 128]])
                        ).then_inc(s_ld, 16)
            e.dma_start(out=wstage[:, 1024:2048],
                        in_=bass.AP(wq_t.tensor, 128 * hp,
                                    [[512, 128], [512 * 128, 8], [1, 128]])
                        ).then_inc(s_ld, 16)
            cnt["ld"] += 32
        SY(w_ld)
        ld_w = cnt["ld"]

        def w_cast(e, ld_w=ld_w):
            e.wait_ge(s_ld, ld_w)
            e.tensor_copy(out=wstgb, in_=wstage).then_inc(s_v, 1)
            cnt["v"] += 1
        VE(w_cast)
        v_cast = cnt["v"]

        def aug_ld(e, hp=hp, b0=b0, b1=b1, guard_t=guard_t):
            e.wait_ge(s_t, guard_t)
            for h_i, (kb, qb) in enumerate([(kaug[b0], qaug[b0]),
                                            (kaug[b1], qaug[b1])]):
                h = 2 * hp + h_i
                ab = 64 if h % 2 == 0 else 0
                e.dma_start(out=kb[ab:ab + 64, :], in_=psi_in[h, :, :]
                            ).then_inc(s_ld, 16)
                e.dma_start(out=qb[ab:ab + 64, :], in_=phi_in[h, :, :]
                            ).then_inc(s_ld, 16)
                cnt["ld"] += 32
        SY(aug_ld)
        ld_aug = cnt["ld"]

        evk = []
        for which in range(2):
            wbase = 1024 * which
            for st in range(8):
                def kqmm(e, which=which, wbase=wbase, st=st, v_cast=v_cast,
                         guard_t=guard_t):
                    e.wait_ge(s_v, v_cast)
                    e.wait_ge(s_t, guard_t)
                    if evk:
                        e.wait_ge(s_s, evk[-1])
                    for c8 in range(8):
                        sl = wstgb[:, wbase + 128 * c8: wbase + 128 * c8 + 128]
                        e.ldweights(sl)
                        li = e.matmul(out=aux, lhsT=sl,
                                      rhs=xT[:, T * c8 + 512 * st:
                                             T * c8 + 512 * st + 512],
                                      start=(c8 == 0), stop=(c8 == 7))
                    li.then_inc(s_t, 1)
                    cnt["t"] += 1
                TE(kqmm)
                t_mm = cnt["t"]

                def kqev(e, which=which, st=st, t_mm=t_mm, hp=hp, b0=b0, b1=b1):
                    e.wait_ge(s_t, t_mm)
                    for h_i in range(2):
                        h = 2 * hp + h_i
                        dst = (kaug if which == 0 else qaug)[b0 if h_i == 0 else b1]
                        db = 0 if h % 2 == 0 else 64
                        src = aux[64 * h_i:64 * h_i + 64, :]
                        if which == 0:
                            li = e.activation(
                                out=dst[db:db + 64, 512 * st:512 * st + 512],
                                in_=src, func=AF.Identity,
                                bias=bk_s[64 * h_i:64 * h_i + 64, hp:hp + 1])
                        else:
                            li = e.activation(
                                out=dst[db:db + 64, 512 * st:512 * st + 512],
                                in_=src, func=AF.Identity,
                                bias=bq8[64 * h_i:64 * h_i + 64, hp:hp + 1],
                                scale=np.float32(0.125))
                    li.then_inc(s_s, 1)
                    cnt["s"] += 1
                SE(kqev)
                evk.append(cnt["s"])
        stage_free["s"] = cnt["s"]
        stage_free["v"] = v_cast
        for h_i in range(2):
            kq_ready[2 * hp + h_i] = cnt["s"]
            kq_aug_ld[2 * hp + h_i] = ld_aug

    def emit_head(h):
        Ka, Qa = kaug[h % 3], qaug[h % 3]
        for st in range(8):
            ntk = 4 * st + 4
            grps = _groups(ntk)
            for g, kjs in enumerate(grps):
                par = g % 2

                def t_sc(e, h=h, st=st, g=g, kjs=kjs, par=par):
                    e.wait_ge(s_s, kq_ready[h])
                    e.wait_ge(s_ld, kq_aug_ld[h])
                    key = ("shfree", par)
                    if key in exp_cp:
                        e.wait_ge(s_s, exp_cp[key])
                    for i, kj in enumerate(kjs):
                        sl = Ka[:, 128 * kj:128 * kj + 128]
                        e.ldweights(sl)
                        li = e.matmul(out=SHP[par][:, 512 * i:512 * i + 512],
                                      lhsT=sl,
                                      rhs=Qa[:, 512 * st:512 * st + 512],
                                      start=True, stop=True)
                    li.then_inc(s_t, 1)
                    cnt["t"] += 1
                TE(t_sc)
                sc_cp[(h, st, g)] = cnt["t"]

                def s_exp(e, h=h, st=st, g=g, kjs=kjs, par=par):
                    e.wait_ge(s_t, sc_cp[(h, st, g)])
                    key = ("ptfree", par)
                    if key in pv_cp:
                        e.wait_ge(s_t, pv_cp[key])
                    w = 512 * len(kjs)
                    e.activation(out=pt[par][:, 0:w], in_=SHP[par][:, 0:w],
                                 func=AF.Exp,
                                 bias=ebias_bc[:, h:h + 1]).then_inc(s_s, 1)
                    cnt["s"] += 1
                SE(s_exp)
                exp_cp[(h, st, g)] = cnt["s"]
                exp_cp[("shfree", par)] = cnt["s"]

                def v_mask(e, h=h, st=st, g=g, kjs=kjs, par=par):
                    e.wait_ge(s_s, exp_cp[(h, st, g)])
                    li = None
                    for i, kj in enumerate(kjs):
                        d = kj - 4 * st
                        if 0 <= d <= 3:
                            sl = slice(512 * i + 128 * d, 512 * i + 128 * d + 128)
                            li = e.tensor_tensor(out=pt[par][:, sl],
                                                 in0=pt[par][:, sl],
                                                 in1=tri_s, op=ALU.mult)
                    if li is not None:
                        li.then_inc(s_v, 1)
                    else:
                        e.nop().then_inc(s_v, 1)
                    cnt["v"] += 1
                VE(v_mask)
                mask_cp[(h, st, g)] = cnt["v"]

                def t_pv(e, h=h, st=st, g=g, kjs=kjs, par=par, ntk=ntk):
                    e.wait_ge(s_v, mask_cp[(h, st, g)])
                    if g == 0 and ("otfree",) in pv_cp:
                        e.wait_ge(s_v, pv_cp[("otfree",)])
                    for i, kj in enumerate(kjs):
                        c0 = max(0, kj - 4 * st) * 128
                        vsl = vbuf[:, 520 * kj + 65 * h: 520 * kj + 65 * h + 65]
                        e.ldweights(vsl)
                        li = e.matmul(out=otb[0:65, c0:512], lhsT=vsl,
                                      rhs=pt[par][:, 512 * i + c0:512 * i + 512],
                                      start=(kj == 0), stop=(kj == ntk - 1))
                    li.then_inc(s_t, 1)
                    cnt["t"] += 1
                TE(t_pv)
                pv_cp[(h, st, g)] = cnt["t"]
                pv_cp[("ptfree", par)] = cnt["t"]

            def v_strip(e, h=h, st=st, ngr=len(grps)):
                e.wait_ge(s_t, pv_cp[(h, st, ngr - 1)])
                lo = 64 * (h % 2)
                e.tensor_copy(out=onorm[lo:lo + 64,
                                        T * (h // 2) + 512 * st:
                                        T * (h // 2) + 512 * st + 512],
                              in_=otb[0:64, :])
                e.tensor_copy(out=rcbc[0:1, 512 * (st % 4):512 * (st % 4) + 512],
                              in_=otb[64:65, :]).then_inc(s_v, 1)
                cnt["v"] += 1
            VE(v_strip)
            pv_cp[("otfree",)] = cnt["v"]

            if st in (3, 7):
                hh = (st - 3) // 4

                def v_recip(e):
                    e.wait_ge(s_v, pv_cp[("otfree",)])
                    VF(e)
                    with nc.allow_low_precision(reason="bf16 softmax sums"):
                        e.reciprocal(out=rcbc[0:1, :], in_=rcbc[0:1, :]
                                     ).then_inc(s_v, 1)
                    cnt["v"] += 1
                VE(v_recip)
                v_rc = cnt["v"]

                def g_bc(e, v_rc=v_rc):
                    e.wait_ge(s_v, v_rc)
                    e.dma_start(out=scr_rc, in_=rcbc[0:1, :]).then_inc(s_gd, 16)
                    cnt["gd"] += 16
                    e.wait_ge(s_gd, cnt["gd"])
                    e.dma_start(out=rcbc,
                                in_=bass.AP(scr_rc.tensor, 0,
                                            [[0, 128], [1, 2048]])
                                ).then_inc(s_gd, 16)
                    cnt["gd"] += 16
                GP(g_bc)
                gd_rc = cnt["gd"]

                def v_norm(e, h=h, hh=hh, gd_rc=gd_rc):
                    e.wait_ge(s_gd, gd_rc)
                    lo = 64 * (h % 2)
                    sl = slice(T * (h // 2) + 2048 * hh,
                               T * (h // 2) + 2048 * hh + 2048)
                    e.tensor_tensor(out=onorm[lo:lo + 64, sl],
                                    in0=onorm[lo:lo + 64, sl],
                                    in1=rcbc[lo:lo + 64, :],
                                    op=ALU.mult).then_inc(s_v, 1)
                    cnt["v"] += 1
                VE(v_norm)
        head_last_sc[h] = sc_cp[(h, 7, len(_groups(32)) - 1)]
        head_norm_v[h] = cnt["v"]

    def emit_wquant(w_in, wrows, wdst, wcols, m_col, race_t):
        nchunks = wrows // 128
        spl = 2 if wcols >= 2048 else 1
        tw = wcols // spl
        tiles = [(c, j) for c in range(nchunks) for j in range(spl)]
        absv = []
        for idx, (c, j) in enumerate(tiles):
            def wl(e, c=c, j=j, idx=idx):
                if idx == 0:
                    e.wait_ge(s_v, stage_free["v"])
                else:
                    e.wait_ge(s_v, absv[-1])
                e.dma_start(out=wstage[:, 0:tw],
                            in_=bass.AP(w_in.tensor, (128 * c) * wcols + tw * j,
                                        [[wcols, 128], [1, tw]])
                            ).then_inc(s_ld, 16)
                cnt["ld"] += 16
            SY(wl)
            ldw = cnt["ld"]

            def wa(e, idx=idx, ldw=ldw):
                e.wait_ge(s_ld, ldw)
                e.reduce_sum(out=stat[:, 64 + idx:65 + idx],
                             in_=wstage[:, 0:tw], axis=AX.X,
                             apply_absolute_value=True).then_inc(s_v, 1)
                cnt["v"] += 1
            VE(wa)
            absv.append(cnt["v"])
        ntile = len(tiles)

        def w_zero(e, ntile=ntile):
            e.wait_ge(s_v, absv[-1])
            if ntile < 64:
                e.memset(stat[:, 64 + ntile:128], 0.0).then_inc(s_v, 1)
            cnt["v"] += 1
        VE(w_zero)
        v_z = cnt["v"]

        def w_mm(e, v_z=v_z):
            e.wait_ge(s_v, v_z)
            e.wait_ge(s_s, stage_free["s"])
            e.matmul(out=aux[0:64, 0:64], lhsT=ones64, rhs=stat[:, 64:128],
                     start=True, stop=True).then_inc(s_t, 1)
            cnt["t"] += 1
        TE(w_mm)
        t_mm = cnt["t"]

        def w_chain(e, t_mm=t_mm, m_col=m_col):
            e.wait_ge(s_t, t_mm)
            e.reduce_sum(out=wsc[0:1, m_col:m_col + 1], in_=aux[0:1, 0:64],
                         axis=AX.X)
            VF(e)
            e.tensor_scalar(out=wsc[0:1, m_col:m_col + 1],
                            in0=wsc[0:1, m_col:m_col + 1],
                            scalar1=float(np.float32(1.0 / (wrows * wcols))), scalar2=1e-5,
                            op0=ALU.mult, op1=ALU.max)
            VF(e)
            e.reciprocal(out=wsc[0:1, m_col + 1:m_col + 2],
                         in_=wsc[0:1, m_col:m_col + 1]).then_inc(s_v, 1)
            cnt["v"] += 1
        VE(w_chain)
        v_ch = cnt["v"]

        def g_wbc(e, v_ch=v_ch, m_col=m_col):
            e.wait_ge(s_v, v_ch)
            e.dma_start(out=scr1, in_=wsc[0:1, m_col:m_col + 4]
                        ).then_inc(s_gd, 16)
            cnt["gd"] += 16
            e.wait_ge(s_gd, cnt["gd"])
            for k in range(2):
                e.dma_start(out=wsc2[:, m_col + k:m_col + k + 1],
                            in_=bass.AP(scr1.tensor, k, [[0, 128], [1, 1]])
                            ).then_inc(s_gd, 16)
                cnt["gd"] += 16
        GP(g_wbc)
        gd_bc = cnt["gd"]

        rl = []
        for idx, (c, j) in enumerate(tiles):
            def wl2(e, c=c, j=j, idx=idx):
                if rl:
                    e.wait_ge(s_v, rl[-1])
                else:
                    e.wait_ge(s_gd, gd_bc)
                e.dma_start(out=wstage[:, 0:tw],
                            in_=bass.AP(w_in.tensor, (128 * c) * wcols + tw * j,
                                        [[wcols, 128], [1, tw]])
                            ).then_inc(s_ld, 16)
                cnt["ld"] += 16
            SY(wl2)
            ldw = cnt["ld"]

            def wr(e, c=c, j=j, ldw=ldw, m_col=m_col, idx=idx):
                e.wait_ge(s_ld, ldw)
                if idx == 0:
                    e.wait_ge(s_gd, gd_bc)
                    e.wait_ge(s_t, race_t)
                for q in range(tw // 1024):
                    e.tensor_scalar(out=wi32,
                                    in0=wstage[:, 1024 * q:1024 * q + 1024],
                                    scalar1=wsc2[:, m_col + 1:m_col + 2],
                                    scalar2=None, op0=ALU.mult)
                    li = e.tensor_scalar(
                        out=wdst[:, wcols * c + tw * j + 1024 * q:
                                 wcols * c + tw * j + 1024 * q + 1024],
                        in0=wi32, scalar1=-1.0, scalar2=1.0,
                        op0=ALU.max, op1=ALU.min)
                li.then_inc(s_v, 1)
                cnt["v"] += 1
            VE(wr)
            rl.append(cnt["v"])
        stage_free["v"] = cnt["v"]
        return cnt["v"]

    emit_pair_fill(0, guard_t=0)
    phase("fill0")
    emit_head(0)
    phase("head0")
    emit_pair_fill(1, guard_t=head_last_sc[0])
    emit_head(1)
    emit_head(2)
    emit_pair_fill(2, guard_t=head_last_sc[2])
    emit_head(3)
    phase("head3")
    emit_head(4)
    emit_pair_fill(3, guard_t=head_last_sc[4])
    emit_head(5)
    phase("head5")
    v_w1_done = emit_wquant(w1_t, C, w1q, FF, 0, race_t=head_last_sc[5])
    phase("w1q")
    emit_head(6)
    v_w2_done = emit_wquant(w2_t, FF, w2q, C, 2, race_t=head_last_sc[6])
    emit_head(7)
    phase("attn")

    def g_exch(e):
        for h in range(8):
            e.wait_ge(s_v, head_norm_v[h])
        with e.register("rp") as rp:
            e.reg_load(rp, pb_s[0:1, 0:1])
            with e.If_eq(rp, 0):
                for u in range(4):
                    e.dma_start(out=gift[:, 2048 * u:2048 * u + 2048],
                                in_=onorm[:, T * u + 2048:T * u + 4096]
                                ).then_inc(s_gd, 16)
            with e.Else():
                for u in range(4):
                    e.dma_start(out=gift[:, 2048 * u:2048 * u + 2048],
                                in_=onorm[:, T * u:T * u + 2048]
                                ).then_inc(s_gd, 16)
            cnt["gd"] += 64
            e.wait_ge(s_gd, cnt["gd"])
            e.collective_compute(
                "AllGather", ALU.bypass,
                replica_groups=[[0, 1], [2, 3], [4, 5], [6, 7]],
                ins=[gift], outs=[gath]).then_inc(s_cc, 1)
            cnt["cc"] += 1
            e.wait_ge(s_cc, cnt["cc"])
            with e.If_eq(rp, 0):
                for u in range(4):
                    e.dma_start(out=pstage[:, 2048 * u:2048 * u + 2048],
                                in_=onorm[:, T * u:T * u + 2048]
                                ).then_inc(s_gd, 16)
                e.dma_start(out=peerO, in_=gath[1]).then_inc(s_gd, 16)
            with e.Else():
                for u in range(4):
                    e.dma_start(out=pstage[:, 2048 * u:2048 * u + 2048],
                                in_=onorm[:, T * u + 2048:T * u + 4096]
                                ).then_inc(s_gd, 16)
                e.dma_start(out=peerO, in_=gath[0]).then_inc(s_gd, 16)
            cnt["gd"] += 80
    GP(g_exch)
    gd_exch = cnt["gd"]
    phase("exch")

    def bp_ld(e):
        e.wait_ge(s_v, v_w2_done)
        e.dma_start(out=wstage[:, 0:1024],
                    in_=bass.AP(bproj_in.tensor, 0, [[0, 128], [1, C]])
                    ).then_inc(s_ld, 16)
        cnt["ld"] += 16
    SY(bp_ld)
    ld_bp = cnt["ld"]

    def bp_c(e, ld_bp=ld_bp):
        e.wait_ge(s_ld, ld_bp)
        e.tensor_scalar(out=wsc2[:, 3:4], in0=wsc2[:, 2:3],
                        scalar1=float(np.float32(1.0 / 127.0)), scalar2=None,
                        op0=ALU.mult)
        e.tensor_copy(out=bpbc, in_=wstage[:, 0:1024]).then_inc(s_v, 1)
        cnt["v"] += 1
    VE(bp_c)
    v_bp = cnt["v"]

    wp_cast = [0] * 4
    for q4 in range(4):
        def wp_ld(e, q4=q4):
            e.wait_ge(s_v, v_bp if q4 == 0 else wp_cast[q4 - 1])
            if q4 == 0:
                e.wait_ge(s_gd, gd_exch)
            e.dma_start(out=wstage,
                        in_=bass.AP(wproj_t.tensor, 256 * q4,
                                    [[C, 128], [128 * C, 8], [1, 256]])
                        ).then_inc(s_ld, 16)
            cnt["ld"] += 16
        SY(wp_ld)
        ldw = cnt["ld"]

        def wp_c(e, q4=q4, ldw=ldw):
            e.wait_ge(s_ld, ldw)
            if q4 == 0:
                e.wait_ge(s_gd, gd_exch)
            e.tensor_copy(out=bass.AP(wpb.tensor, 256 * q4,
                                      [[8 * C, 128], [C, 8], [1, 256]]),
                          in_=wstage).then_inc(s_v, 1)
            cnt["v"] += 1
        VE(wp_c)
        wp_cast[q4] = cnt["v"]
    v_wp = cnt["v"]
    phase("wp")

    proj_done_v = [0] * NTT
    for tt in range(NTT):
        def xr_ld(e, tt=tt):
            if tt >= 2:
                e.wait_ge(s_v, proj_done_v[tt - 2])
            e.dma_start(out=xres[tt % 2],
                        in_=bass.AP(x_own.tensor, 128 * tt * C,
                                    [[C, 128], [1, C]])).then_inc(s_gd, 16)
            cnt["gd"] += 16
        GP(xr_ld)
        ld_xr = cnt["gd"]

        def pmm(e, tt=tt):
            e.wait_ge(s_v, v_wp)
            e.wait_ge(s_gd, gd_exch)
            if tt >= 2:
                e.wait_ge(s_v, proj_done_v[tt - 2])
            bset = sh0 if tt % 2 == 0 else sh1
            for dgrp in range(2):
                bank = bset[:, 512 * dgrp:512 * dgrp + 512]
                for ci in range(8):
                    if ci < 4:
                        sl = pstage[:, 2048 * ci + 128 * tt:
                                    2048 * ci + 128 * tt + 128]
                    else:
                        sl = peerO[:, 2048 * (ci - 4) + 128 * tt:
                                   2048 * (ci - 4) + 128 * tt + 128]
                    e.ldweights(sl)
                    li = e.matmul(out=bank, lhsT=sl,
                                  rhs=bass.AP(wpb.tensor, C * ci + 512 * dgrp,
                                              [[8 * C, 128], [1, 512]]),
                                  start=(ci == 0), stop=(ci == 7))
            li.then_inc(s_t, 1)
            cnt["t"] += 1
        TE(pmm)
        t_p = cnt["t"]

        def pev(e, tt=tt, t_p=t_p, ld_xr=ld_xr):
            e.wait_ge(s_t, t_p)
            e.wait_ge(s_gd, ld_xr)
            bset = sh0 if tt % 2 == 0 else sh1
            for dgrp in range(2):
                e.scalar_tensor_tensor(
                    out=tscr[:, 512 * dgrp:512 * dgrp + 512],
                    in0=bset[:, 512 * dgrp:512 * dgrp + 512], scalar=1.0,
                    in1=bpbc[:, 512 * dgrp:512 * dgrp + 512],
                    op0=ALU.mult, op1=ALU.add)
            e.tensor_tensor(out=sbf[:, C * tt:C * tt + C], in0=tscr,
                            in1=xres[tt % 2], op=ALU.add).then_inc(s_v, 1)
            cnt["v"] += 1
        VE(pev)
        proj_done_v[tt] = cnt["v"]
    v_proj = cnt["v"]

    phase("proj")
    for c in reversed(psum_ctx):
        c.__exit__(None, None, None)
    psum_ctx2 = [nc.psum_tensor("f1b", [128, 1536], F32),
                 nc.psum_tensor("f2b", [128, 1536], F32),
                 nc.psum_tensor("tpb", [128, 1024], BF16),
                 nc.psum_tensor("tpb2", [128, 1024], BF16)]
    f1b, f2b, tpb, tpb2 = [c.__enter__().ap() for c in psum_ctx2]

    b_v = [0, 0, 0]
    for bi, (src, dst, width) in enumerate(
            [(b1_in, b1bc[:, 0:2048], 2048),
             (None, b1bc[:, 2048:4096], 2048),
             (b2_in, b2bc, C)]):
        def bb_ld(e, bi=bi, width=width):
            e.wait_ge(s_v, v_proj if bi == 0 else b_v[bi - 1])
            base = 2048 if bi == 1 else 0
            e.dma_start(out=wstage[:, 0:width],
                        in_=bass.AP((b1_in if bi < 2 else b2_in).tensor, base,
                                    [[0, 128], [1, width]])
                        ).then_inc(s_ld, 16)
            cnt["ld"] += 16
        SY(bb_ld)
        ldb = cnt["ld"]

        def bb_c(e, bi=bi, dst=dst, width=width, ldb=ldb):
            e.wait_ge(s_ld, ldb)
            e.tensor_copy(out=dst, in_=wstage[:, 0:width]).then_inc(s_v, 1)
            cnt["v"] += 1
        VE(bb_c)
        b_v[bi] = cnt["v"]
    v_bias = cnt["v"]

    def ln1_stats(e):
        e.wait_ge(s_v, v_bias)
        for tt in range(NTT):
            sl = sbf[:, C * tt:C * tt + C]
            e.reduce_sum(out=stat[:, 16 + tt:17 + tt], in_=sl, axis=AX.X)
            e.scalar_tensor_tensor(out=pt[0][:, 0:C], in0=sl, scalar=1.0,
                                   in1=sl, op0=ALU.mult, op1=ALU.mult,
                                   accum_out=stat[:, tt:tt + 1])
            e.reduce_max(out=stat[:, 96 + tt:97 + tt], in_=sl, axis=AX.X)
            li = e.tensor_reduce(out=stat[:, 112 + tt:113 + tt], in_=sl,
                                 axis=AX.X, op=ALU.min)
        li.then_inc(s_v, 1)
        cnt["v"] += 1
    VE(ln1_stats)
    v_st = cnt["v"]

    def ln1_c1(e, v_st=v_st):
        e.wait_ge(s_v, v_st)
        e.tensor_scalar(out=stat[:, 16:32], in0=stat[:, 16:32],
                        scalar1=float(np.float32(1.0 / C)), scalar2=None, op0=ALU.mult)
        e.tensor_scalar(out=stat[:, 0:16], in0=stat[:, 0:16],
                        scalar1=float(np.float32(1.0 / C)), scalar2=1e-5,
                        op0=ALU.mult, op1=ALU.add)
        e.tensor_tensor(out=stat[:, 112:128], in0=stat[:, 16:32],
                        in1=stat[:, 112:128], op=ALU.subtract)
        e.tensor_tensor(out=stat[:, 96:112], in0=stat[:, 96:112],
                        in1=stat[:, 16:32], op=ALU.subtract)
        e.tensor_tensor(out=stat[:, 80:96], in0=stat[:, 16:32],
                        in1=stat[:, 16:32], op=ALU.mult)
        e.tensor_tensor(out=stat[:, 96:112], in0=stat[:, 96:112],
                        in1=stat[:, 112:128], op=ALU.max)
        e.tensor_tensor(out=stat[:, 0:16], in0=stat[:, 0:16],
                        in1=stat[:, 80:96], op=ALU.subtract
                        ).then_inc(s_v, 1)
        cnt["v"] += 1
    VE(ln1_c1)
    v_c1 = cnt["v"]

    def ln1_s1(e, v_c1=v_c1):
        e.wait_ge(s_v, v_c1)
        e.activation(out=stat[:, 32:48], in_=stat[:, 0:16], func=AF.Sqrt,
                     bias=np.float32(0.0)).then_inc(s_s, 1)
        cnt["s"] += 1
    SE(ln1_s1)
    s_sq1 = cnt["s"]

    def ln1_c2(e, s_sq1=s_sq1):
        e.wait_ge(s_s, s_sq1)
        e.reciprocal(out=stat[:, 32:48], in_=stat[:, 32:48])
        e.tensor_scalar(out=stat[:, 112:128], in0=stat[:, 0:16],
                        scalar1=-1e-5, scalar2=None, op0=ALU.add)
        e.tensor_tensor(out=stat[:, 48:64], in0=stat[:, 16:32],
                        in1=stat[:, 32:48], op=ALU.mult)
        e.reciprocal(out=stat[:, 16:32], in_=stat[:, 0:16])
        e.tensor_scalar(out=stat[:, 48:64], in0=stat[:, 48:64],
                        scalar1=-1.0, scalar2=None, op0=ALU.mult)
        e.tensor_tensor(out=stat[:, 112:128], in0=stat[:, 112:128],
                        in1=stat[:, 16:32], op=ALU.mult
                        ).then_inc(s_v, 1)
        cnt["v"] += 1
    VE(ln1_c2)
    v_c2 = cnt["v"]

    def ln1_s2(e, v_c2=v_c2):
        e.wait_ge(s_v, v_c2)
        e.activation(out=stat[:, 16:32], in_=stat[:, 112:128], func=AF.Sqrt,
                     bias=np.float32(1e-5)).then_inc(s_s, 1)
        cnt["s"] += 1
    SE(ln1_s2)
    s_sq2 = cnt["s"]

    def ln1_c3(e, s_sq2=s_sq2):
        e.wait_ge(s_s, s_sq2)
        e.reciprocal(out=stat[:, 16:32], in_=stat[:, 16:32])
        e.tensor_tensor(out=stat[:, 96:112], in0=stat[:, 96:112],
                        in1=stat[:, 32:48], op=ALU.mult)
        VF(e, 1)
        e.tensor_tensor(out=stat[:, 112:128], in0=stat[:, 96:112],
                        in1=stat[:, 16:32], op=ALU.mult)
        VF(e)
        e.tensor_scalar(out=stat[:, 112:128], in0=stat[:, 112:128],
                        scalar1=1e-5, scalar2=None, op0=ALU.max)
        VF(e, 1)
        e.tensor_scalar(out=stat[:, 80:96], in0=stat[:, 112:128],
                        scalar1=wsc2[:, 0:1], scalar2=float(np.float32(1.0 / 127.0)),
                        op0=ALU.mult, op1=ALU.mult)
        e.reciprocal(out=stat[:, 112:128], in_=stat[:, 112:128])
        VF(e)
        e.tensor_scalar(out=stat[:, 112:128], in0=stat[:, 112:128],
                        scalar1=127.0, scalar2=None, op0=ALU.mult)
        VF(e)
        e.tensor_tensor(out=stat[:, 64:80], in0=stat[:, 16:32],
                        in1=stat[:, 112:128], op=ALU.mult)
        VF(e).then_inc(s_v, 1)
        cnt["v"] += 1
    VE(ln1_c3)
    v_c3 = cnt["v"]

    def ln1_apply(e, v_c3=v_c3):
        e.wait_ge(s_v, v_c3)
        for tt in range(NTT):
            li = e.activation(out=sbf[:, C * tt:C * tt + C],
                              in_=sbf[:, C * tt:C * tt + C], func=AF.Identity,
                              bias=stat[:, 48 + tt:49 + tt],
                              scale=stat[:, 32 + tt:33 + tt])
        li.then_inc(s_s, 1)
        cnt["s"] += 1
    SE(ln1_apply)
    s_y = cnt["s"]
    phase("ln1")

    ffn_v = {}
    ffn_t = {}
    out_sent = []

    for grp4 in range(4):
        for ti in range(4):
            tt = 4 * grp4 + ti

            def xquant(e, tt=tt):
                e.wait_ge(s_s, s_y)
                if ("xqfree",) in ffn_t:
                    e.wait_ge(s_t, ffn_t[("xqfree",)])
                e.tensor_scalar(out=wi32, in0=sbf[:, C * tt:C * tt + C],
                                scalar1=stat[:, 64 + tt:65 + tt],
                                scalar2=None, op0=ALU.mult)
                e.tensor_copy(out=xq, in_=wi32).then_inc(s_v, 1)
                cnt["v"] += 1
            VE(xquant)
            v_xq = cnt["v"]

            def xqt(e, v_xq=v_xq):
                e.wait_ge(s_v, v_xq)
                if ("xqTev",) in ffn_v:
                    e.wait_ge(s_v, ffn_v[("xqTev",)])
                for c8 in range(8):
                    li = e.transpose(out=tpb[:, 128 * c8:128 * c8 + 128],
                                     in_=xq[:, 128 * c8:128 * c8 + 128],
                                     identity=eyeb_s)
                li.then_inc(s_t, 1)
                cnt["t"] += 1
            TE(xqt)
            ffn_t[("xqfree",)] = cnt["t"]
            t_xq = cnt["t"]

            def xqev(e, t_xq=t_xq):
                e.wait_ge(s_t, t_xq)
                e.tensor_copy(out=xqT, in_=tpb).then_inc(s_v, 1)
                cnt["v"] += 1
            VE(xqev)
            ffn_v[("xqTev",)] = cnt["v"]
            v_xqT = cnt["v"]

            for fg in range(8):
                def f1mm(e, tt=tt, fg=fg, v_xqT=v_xqT):
                    e.wait_ge(s_v, v_xqT)
                    e.wait_ge(s_v, v_w1_done)
                    if ("f1ev", fg % 3) in ffn_v:
                        e.wait_ge(s_v, ffn_v[("f1ev", fg % 3)])
                    bank = f1b[:, 512 * (fg % 3):512 * (fg % 3) + 512]
                    for c8 in range(8):
                        sl = xqT[:, 128 * c8:128 * c8 + 128]
                        e.ldweights(sl)
                        li = e.matmul(out=bank, lhsT=sl,
                                      rhs=w1q[:, FF * c8 + 512 * fg:
                                              FF * c8 + 512 * fg + 512],
                                      start=(c8 == 0), stop=(c8 == 7))
                    li.then_inc(s_t, 1)
                    cnt["t"] += 1
                TE(f1mm)
                t_f1 = cnt["t"]

                def f1ev(e, tt=tt, fg=fg, t_f1=t_f1):
                    e.wait_ge(s_t, t_f1)
                    bank = f1b[:, 512 * (fg % 3):512 * (fg % 3) + 512]
                    e.scalar_tensor_tensor(
                        out=hbuf[:, 512 * fg:512 * fg + 512], in0=bank,
                        scalar=stat[:, 80 + tt:81 + tt],
                        in1=b1bc[:, 512 * fg:512 * fg + 512],
                        op0=ALU.mult, op1=ALU.add).then_inc(s_v, 1)
                    cnt["v"] += 1
                VE(f1ev)
                ffn_v[("f1ev", fg % 3)] = cnt["v"]
                v_h = cnt["v"]

                def f1g(e, fg=fg, v_h=v_h):
                    e.wait_ge(s_v, v_h)
                    e.activation(out=hbuf[:, 512 * fg:512 * fg + 512],
                                 in_=hbuf[:, 512 * fg:512 * fg + 512],
                                 func=AF.Gelu,
                                 bias=np.float32(0.0)).then_inc(s_s, 1)
                    cnt["s"] += 1
                SE(f1g)
                ffn_v[("gelu", tt, fg)] = cnt["s"]

            def gsq(e, tt=tt, ti=ti):
                e.wait_ge(s_s, ffn_v[("gelu", tt, 7)])
                if ("qgfree",) in ffn_t:
                    e.wait_ge(s_t, ffn_t[("qgfree",)])
                e.activation(out=qg, in_=hbuf, func=AF.Square,
                             bias=np.float32(0.0),
                             accum_out=stat[:, ti:ti + 1]).then_inc(s_s, 1)
                cnt["s"] += 1
            SE(gsq)
            s_sq = cnt["s"]

            def gst(e, ti=ti, s_sq=s_sq):
                e.wait_ge(s_s, s_sq)
                e.reduce_max(out=stat[:, 4 + ti:5 + ti], in_=hbuf, axis=AX.X,
                             apply_absolute_value=True)
                e.tensor_scalar(out=stat[:, ti:ti + 1], in0=stat[:, ti:ti + 1],
                                scalar1=float(np.float32(1.0 / FF)), scalar2=1e-5,
                                op0=ALU.mult, op1=ALU.add)
                VF(e, 1)
                e.tensor_scalar(out=stat[:, 12:13], in0=stat[:, 4 + ti:5 + ti],
                                scalar1=1e-12, scalar2=None, op0=ALU.max)
                VF(e)
                e.reciprocal(out=stat[:, 12:13], in_=stat[:, 12:13])
                VF(e)
                e.tensor_scalar(out=stat[:, 12:13], in0=stat[:, 12:13],
                                scalar1=127.0, scalar2=None,
                                op0=ALU.mult)
                VF(e).then_inc(s_v, 1)
                cnt["v"] += 1
            VE(gst)
            v_gs = cnt["v"]

            def gquant(e, v_gs=v_gs):
                e.wait_ge(s_v, v_gs)
                for q in range(4):
                    e.tensor_scalar(out=wi32,
                                    in0=hbuf[:, 1024 * q:1024 * q + 1024],
                                    scalar1=stat[:, 12:13], scalar2=None,
                                    op0=ALU.mult)
                    li = e.tensor_copy(out=qg[:, 1024 * q:1024 * q + 1024],
                                       in_=wi32)
                li.then_inc(s_v, 1)
                cnt["v"] += 1
            VE(gquant)
            v_qg = cnt["v"]

            for sub in range(2):
                def qgt(e, sub=sub, v_qg=v_qg):
                    e.wait_ge(s_v, v_qg)
                    if ("qgTev",) in ffn_v:
                        e.wait_ge(s_v, ffn_v[("qgTev",)])
                    for k16 in range(16):
                        bank = tpb if k16 < 8 else tpb2
                        li = e.transpose(out=bank[:, 128 * (k16 % 8):
                                                  128 * (k16 % 8) + 128],
                                         in_=qg[:, 2048 * sub + 128 * k16:
                                                2048 * sub + 128 * k16 + 128],
                                         identity=eyeb_s)
                    li.then_inc(s_t, 1)
                    cnt["t"] += 1
                TE(qgt)
                ffn_t[("qgfree",)] = cnt["t"]
                tq = cnt["t"]

                def qgev(e, sub=sub, tq=tq):
                    e.wait_ge(s_t, tq)
                    e.tensor_copy(out=qgT[:, 2048 * sub:2048 * sub + 1024],
                                  in_=tpb)
                    e.tensor_copy(out=qgT[:, 2048 * sub + 1024:2048 * sub + 2048],
                                  in_=tpb2).then_inc(s_v, 1)
                    cnt["v"] += 1
                VE(qgev)
                ffn_v[("qgTev",)] = cnt["v"]
            v_qgT = cnt["v"]

            for dg in range(2):
                def f2mm(e, tt=tt, dg=dg, v_qgT=v_qgT):
                    e.wait_ge(s_v, v_qgT)
                    e.wait_ge(s_v, v_w2_done)
                    if ("f2ev", dg) in ffn_v:
                        e.wait_ge(s_v, ffn_v[("f2ev", dg)])
                    bank = f2b[:, 512 * dg:512 * dg + 512]
                    for c32 in range(32):
                        sl = qgT[:, 128 * c32:128 * c32 + 128]
                        e.ldweights(sl)
                        li = e.matmul(out=bank, lhsT=sl,
                                      rhs=w2q[:, C * c32 + 512 * dg:
                                              C * c32 + 512 * dg + 512],
                                      start=(c32 == 0), stop=(c32 == 31))
                    li.then_inc(s_t, 1)
                    cnt["t"] += 1
                TE(f2mm)
                t_f2 = cnt["t"]

                def f2ev(e, tt=tt, ti=ti, dg=dg, t_f2=t_f2):
                    e.wait_ge(s_t, t_f2)
                    e.tensor_copy(out=h2raw[:, C * ti + 512 * dg:
                                            C * ti + 512 * dg + 512],
                                  in_=f2b[:, 512 * dg:512 * dg + 512]
                                  ).then_inc(s_v, 1)
                    cnt["v"] += 1
                VE(f2ev)
                ffn_v[("f2ev", dg)] = cnt["v"]
        v_h2 = cnt["v"]

        def ep_s1(e, v_h2=v_h2):
            e.wait_ge(s_v, v_h2)
            e.activation(out=stat[:, 8:12], in_=stat[:, 0:4], func=AF.Sqrt,
                         bias=np.float32(0.0)).then_inc(s_s, 1)
            cnt["s"] += 1
        SE(ep_s1)
        s_e1 = cnt["s"]

        def ep_v1(e, s_e1=s_e1):
            e.wait_ge(s_s, s_e1)
            e.reciprocal(out=stat[:, 8:12], in_=stat[:, 8:12])
            VF(e)
            e.tensor_tensor(out=stat[:, 8:12], in0=stat[:, 8:12],
                            in1=stat[:, 4:8], op=ALU.mult)
            VF(e)
            e.tensor_scalar(out=stat[:, 8:12], in0=stat[:, 8:12],
                            scalar1=1e-5, scalar2=wsc2[:, 3:4],
                            op0=ALU.max, op1=ALU.mult)
            VF(e).then_inc(s_v, 1)
            cnt["v"] += 1
        VE(ep_v1)
        v_sig = cnt["v"]

        for ti in range(4):
            tt = 4 * grp4 + ti

            def ep_z(e, tt=tt, ti=ti, v_sig=v_sig):
                e.wait_ge(s_v, v_sig)
                e.scalar_tensor_tensor(out=tscr,
                                       in0=h2raw[:, C * ti:C * ti + C],
                                       scalar=stat[:, 8 + ti:9 + ti],
                                       in1=b2bc, op0=ALU.mult, op1=ALU.add)
                e.tensor_tensor(out=sbf[:, C * tt:C * tt + C], in0=tscr,
                                in1=sbf[:, C * tt:C * tt + C], op=ALU.add)
                sl = sbf[:, C * tt:C * tt + C]
                e.reduce_sum(out=stat[:, 16 + ti:17 + ti], in_=sl, axis=AX.X)
                e.scalar_tensor_tensor(out=pt[1][:, 0:C], in0=sl, scalar=1.0,
                                       in1=sl, op0=ALU.mult, op1=ALU.mult,
                                       accum_out=stat[:, 20 + ti:21 + ti]
                                       ).then_inc(s_v, 1)
                cnt["v"] += 1
            VE(ep_z)
        v_z4 = cnt["v"]

        def ep_v2(e, v_z4=v_z4):
            e.wait_ge(s_v, v_z4)
            e.tensor_scalar(out=stat[:, 16:20], in0=stat[:, 16:20],
                            scalar1=float(np.float32(-1.0 / C)), scalar2=None,
                            op0=ALU.mult)
            VF(e, 1)
            e.tensor_scalar(out=stat[:, 20:24], in0=stat[:, 20:24],
                            scalar1=float(np.float32(1.0 / C)), scalar2=None, op0=ALU.mult)
            VF(e, 1)
            e.tensor_tensor(out=stat[:, 28:32], in0=stat[:, 16:20],
                            in1=stat[:, 16:20], op=ALU.mult)
            VF(e)
            e.tensor_tensor(out=stat[:, 20:24], in0=stat[:, 20:24],
                            in1=stat[:, 28:32], op=ALU.subtract)
            cnt["v"] += 1
            VF(e, 1).then_inc(s_v, 1)
        VE(ep_v2)
        v_l2 = cnt["v"]

        def ep_s2(e, v_l2=v_l2):
            e.wait_ge(s_v, v_l2)
            e.activation(out=stat[:, 24:28], in_=stat[:, 20:24], func=AF.Sqrt,
                         bias=np.float32(1e-5)).then_inc(s_s, 1)
            cnt["s"] += 1
        SE(ep_s2)
        s_l2 = cnt["s"]

        def ep_v3(e, s_l2=s_l2):
            e.wait_ge(s_s, s_l2)
            e.reciprocal(out=stat[:, 24:28], in_=stat[:, 24:28])
            VF(e)
            e.tensor_tensor(out=stat[:, 28:32], in0=stat[:, 16:20],
                            in1=stat[:, 24:28], op=ALU.mult)
            VF(e).then_inc(s_v, 1)
            cnt["v"] += 1
        VE(ep_v3)
        v_l2r = cnt["v"]

        for ti in range(4):
            tt = 4 * grp4 + ti

            def ep_out(e, tt=tt, ti=ti, v_l2r=v_l2r):
                e.wait_ge(s_v, v_l2r)
                if len(out_sent) >= 2:
                    e.wait_ge(s_out, out_sent[-2])
                e.activation(out=ostg[tt % 2], in_=sbf[:, C * tt:C * tt + C],
                             func=AF.Identity, bias=stat[:, 28 + ti:29 + ti],
                             scale=stat[:, 24 + ti:25 + ti]).then_inc(s_s, 1)
                cnt["s"] += 1
            SE(ep_out)
            s_o = cnt["s"]

            def ep_dma(e, tt=tt, s_o=s_o):
                e.wait_ge(s_s, s_o)
                e.dma_start(out=bass.AP(out_d.tensor, 128 * tt * C,
                                        [[C, 128], [1, C]]),
                            in_=ostg[tt % 2]).then_inc(s_out, 16)
                cnt["out"] += 16
            GP(ep_dma)
            out_sent.append(cnt["out"])

    if DEBUG_DUMP:
        dump_chunks = [
            [(0, 4, wsc2), (8, 16, wsc[0:1, 0:16]), (32, 128, stat)],
            [(0, 1024, w1q[:, 0:1024])],
            [(0, 1024, w1q[:, 4096 * 7 + 2048:4096 * 7 + 3072])],
            [(0, 1024, w2q[:, 0:1024])],
            [(0, 1024, sbf[:, 0:1024])],
            [(0, 1024, sbf[:, 15 * 1024:16 * 1024])],
            [(0, 1024, h2raw[:, 0:1024])],
            [(0, 1024, h2raw[:, 3 * 1024:4 * 1024])],
            [(0, 1024, hbuf[:, 0:1024])],
            [(0, 1024, hbuf[:, 2048:3072])],
            [(0, 1024, qg[:, 0:1024])],
            [(0, 1024, qgT[:, 0:1024])],
            [(0, 1024, xq)],
            [(0, 1024, xqT)],
            [(0, 1024, b2bc)],
            [(0, 1024, tscr)],
        ]

        dump_base = cnt["out"]
        dump_v = []

        def dump_cp(e):
            for sem, key in [(s_ld, "ld"), (s_gd, "gd"), (s_t, "t"),
                             (s_s, "s"), (s_cc, "cc"), (s_out, "out")]:
                e.wait_ge(sem, cnt[key])
            for i, grp in enumerate(dump_chunks):
                bb = i % 2
                if i >= 2:
                    e.wait_ge(s_out, dump_base + 16 * (i - 1))
                e.memset(ostg[bb], 0.0)
                li = None
                for lo, w, src in grp:
                    pp = 1 if (lo == 8) else 128
                    li = e.tensor_copy(out=ostg[bb][0:pp, lo:lo + w], in_=src)
                li.then_inc(s_v, 1)
                cnt["v"] += 1
                dump_v.append(cnt["v"])
        VE(dump_cp)

        def dump_dma(e):
            for i in range(len(dump_chunks)):
                e.wait_ge(s_v, dump_v[i])
                e.dma_start(out=bass.AP(dbg_d.tensor, 128 * i * C,
                                        [[C, 128], [1, C]]),
                            in_=ostg[i % 2]).then_inc(s_out, 16)
                cnt["out"] += 16
        GP(dump_dma)

    def fin(e):
        e.wait_ge(s_out, cnt["out"])
    GP(fin)

    for c in reversed(semctx):
        c.__exit__(None, None, None)
    ctx.__exit__(None, None, None)
    if upto is None:
        _CACHE["nc"] = nc
    return nc


def kernel(L0_attn_x, L0_attn_w_qkv, L0_attn_b_qkv, L0_attn_w_proj,
           L0_attn_b_proj, L0_attn_pos_independent, L0_ffn_w1, L0_ffn_b1,
           L0_ffn_w2, L0_ffn_b2):
    phi, psi, tri, eyef, eyeb = _host_tables()
    x = np.ascontiguousarray(np.asarray(L0_attn_x, dtype=np.float32))
    wqkv = np.asarray(L0_attn_w_qkv, dtype=np.float32)
    bqkv = np.asarray(L0_attn_b_qkv, dtype=np.float32)
    wproj = np.asarray(L0_attn_w_proj, dtype=np.float32)
    bproj = np.ascontiguousarray(np.asarray(L0_attn_b_proj, dtype=np.float32))
    pos_ind = np.ascontiguousarray(np.asarray(L0_attn_pos_independent,
                                              dtype=np.float32))
    w1 = np.asarray(L0_ffn_w1, dtype=np.float32)
    b1 = np.ascontiguousarray(np.asarray(L0_ffn_b1, dtype=np.float32))
    w2 = np.asarray(L0_ffn_w2, dtype=np.float32)
    b2 = np.ascontiguousarray(np.asarray(L0_ffn_b2, dtype=np.float32))

    zeros_tab = np.zeros_like(phi)
    zslab = np.zeros((8, 64), np.float32)
    pslab = np.ascontiguousarray(pos_ind.reshape(8, 64))
    nc = build_nc()

    in_maps = []
    for c in range(8):
        b, p = c // 2, c % 2
        perm = np.concatenate([np.arange(512) + 512 * p,
                               np.arange(512) + 512 * (1 - p)])
        im = dict(
            x_full=x[b],
            x_own=np.ascontiguousarray(x[b, 2048 * p:2048 * p + 2048]),
            wq_t=np.ascontiguousarray(wqkv[512 * p:512 * p + 512].T),
            wk_t=np.ascontiguousarray(
                wqkv[1024 + 512 * p:1024 + 512 * p + 512].T),
            wv_t=np.ascontiguousarray(
                wqkv[2048 + 512 * p:2048 + 512 * p + 512].T),
            bq_in=np.ascontiguousarray(bqkv[512 * p:512 * p + 512]),
            bk_in=np.ascontiguousarray(bqkv[1024 + 512 * p:1024 + 512 * p + 512]),
            bv_in=np.ascontiguousarray(bqkv[2048 + 512 * p:2048 + 512 * p + 512]),
            wproj_t=np.ascontiguousarray(wproj.T[perm, :]),
            bproj_in=bproj,
            w1_t=np.ascontiguousarray(w1.T),
            b1_in=b1,
            w2_t=np.ascontiguousarray(w2.T),
            b2_in=b2,
            pos_slab=(zslab if p == 0 else pslab),
            phi_in=(phi if p == 0 else zeros_tab),
            psi_in=(psi if p == 0 else zeros_tab),
            ebias_in=np.full(8, -(32.0 + MARGIN) if p == 0 else -MARGIN,
                             dtype=np.float32),
            tri_in=tri,
            eyef_in=eyef,
            eyeb_in=eyeb,
            psel=np.array([[p]], dtype=np.int32),
        )
        in_maps.append(im)

    res = run_bass_kernel_spmd(nc, in_maps, core_ids=list(range(8)))
    _LAST_RES.clear()
    _LAST_RES.extend(res.results if hasattr(res, "results") else res)
    out = np.empty((4, T, C), dtype=np.float32)
    for c in range(8):
        b, p = c // 2, c % 2
        out[b, 2048 * p:2048 * p + 2048] = np.asarray(res.results[c]["out"])
    return out


# version 04
# speedup vs baseline: 1.2021x; 1.1891x over previous
import math
import numpy as np
import ml_dtypes
import concourse.bass as bass
import concourse.mybir as mybir
from concourse.bass_utils import run_bass_kernel_spmd

F32 = mybir.dt.float32
BF16 = mybir.dt.bfloat16
I32 = mybir.dt.int32
F8 = mybir.dt.float8e4
AF = mybir.ActivationFunctionType
ALU = mybir.AluOpType
AX = mybir.AxisListType
f32 = np.float32

T = 4096
C = 1024
FF = 4096
NTT = 16
MARGIN = 10.0

_CACHE = {}
DEBUG_DUMP = False
_LAST_RES = []


def _host_tables():
    if "tabs" in _CACHE:
        return _CACHE["tabs"]
    dim = C // 2
    pos = np.arange(dim, dtype=np.float64)
    freqs = 1.0 / 10000.0 ** (pos / dim)
    active = int(np.sum(freqs * 2 * math.pi * T >= 1.0))
    active = min(active, dim - C // 4)
    assert active == 256
    fr = freqs[:active]
    th = np.arange(T, dtype=np.float64)[:, None] * fr[None, :]
    cos, sin = np.cos(th), np.sin(th)
    phi = np.zeros((8, 64, T), np.float64)
    psi = np.zeros((8, 64, T), np.float64)
    for h in range(8):
        sl = slice(32 * h, 32 * h + 32)
        phi[h, 0:32] = cos[:, sl].T
        phi[h, 32:64] = sin[:, sl].T
        psi[h, 0:32] = (cos[:, sl] - sin[:, sl]).T
        psi[h, 32:64] = (cos[:, sl] + sin[:, sl]).T
    kq = np.arange(128)
    tabs = (phi.astype(ml_dtypes.bfloat16), psi.astype(ml_dtypes.bfloat16),
            (kq[:, None] <= kq[None, :]).astype(ml_dtypes.bfloat16),
            np.eye(128, dtype=np.float32), np.eye(128, dtype=ml_dtypes.bfloat16))
    _CACHE["tabs"] = tabs
    return tabs


def _groups(ntk, w=3):
    return [list(range(i, min(i + w, ntk))) for i in range(0, ntk, w)]


class _StopEmission(Exception):
    pass


def build_nc(upto=None):
    if upto is None and "nc" in _CACHE:
        return _CACHE["nc"]
    nc = bass.Bass("TRN2", target_bir_lowering=False,
                   detect_race_conditions=False)
    dram = lambda n, s, d, k="ExternalInput": nc.dram_tensor(n, s, d, kind=k).ap()

    x_full = dram("x_full", [T, C], F32)
    x_own = dram("x_own", [2048, C], F32)
    wq_t = dram("wq_t", [C, 512], F32)
    wk_t = dram("wk_t", [C, 512], F32)
    wv_t = dram("wv_t", [C, 512], F32)
    bq_in = dram("bq_in", [512], F32)
    bk_in = dram("bk_in", [512], F32)
    bv_in = dram("bv_in", [512], F32)
    wproj_t = dram("wproj_t", [C, C], F32)
    bproj_in = dram("bproj_in", [C], F32)
    w1_t = dram("w1_t", [C, FF], F32)
    b1_in = dram("b1_in", [FF], F32)
    w2_t = dram("w2_t", [FF, C], F32)
    b2_in = dram("b2_in", [C], F32)
    pos_slab = dram("pos_slab", [8, 64], F32)
    phi_in = dram("phi_in", [8, 64, T], BF16)
    psi_in = dram("psi_in", [8, 64, T], BF16)
    ebias_in = dram("ebias_in", [8], F32)
    tri_in = dram("tri_in", [128, 128], BF16)
    eyef_in = dram("eyef_in", [128, 128], F32)
    eyeb_in = dram("eyeb_in", [128, 128], BF16)
    psel = dram("psel", [1, 1], I32)
    out_d = dram("out", [2048, C], F32, "ExternalOutput")
    if DEBUG_DUMP:
        dbg_d = dram("dbg", [2048, C], F32, "ExternalOutput")

    scr8 = nc.dram_tensor("scr8", [8], F32).ap()
    scr1 = nc.dram_tensor("scr1", [4], F32).ap()
    scr_rc = nc.dram_tensor("scr_rc", [2048], BF16).ap()
    gift = nc.dram_tensor("gift", [128, 4 * 2048], BF16).ap()
    gath = nc.dram_tensor("gath", [2, 128, 4 * 2048], BF16).ap()

    SB_BASE = 17152
    def at(name, shape, dt, off):
        nbytes = int(np.prod(shape[1:])) * mybir.dt.size(dt)
        assert SB_BASE + off + nbytes <= 229376, (name, off, nbytes)
        return nc.alloc_sbuf_tensor_at(name, shape, dt, offset=SB_BASE + off).ap()

    xT = at("xT", [128, 8 * T], BF16, 0)
    w1q = at("w1q", [128, 8 * FF], F8, 0)
    w2q = at("w2q", [128, 32 * C], F8, 32768)
    vbuf = at("vbuf", [128, 32 * 520], BF16, 65536)
    sbf = at("sbf", [128, NTT * C], BF16, 65536)
    kaug = [at(f"kaug{i}", [128, T], BF16, 98816 + 8192 * i) for i in range(3)]
    pstage = at("pstage", [128, 4 * 2048], BF16, 98816)
    hbuf = at("hbuf", [128, FF], F32, 98816)
    peerO = at("peerO", [128, 4 * 2048], BF16, 115200)
    ostg = [at(f"ostg{i}", [128, C], F32, 115200 + 4096 * i) for i in range(2)]
    qaug = [at(f"qaug{i}", [128, T], BF16, 123392 + 8192 * i) for i in range(3)]
    xres = [at(f"xres{i}", [128, C], F32, 131584 + 4096 * i) for i in range(2)]
    tscr = at("tscr", [128, C], F32, 139776)
    wi32 = at("wi32", [128, C], I32, 139776)
    bpbc = at("bpbc", [128, C], BF16, 143872)
    b2bc = at("b2bc", [128, C], BF16, 145920)
    onorm = at("onorm", [128, 4 * T], BF16, 147968)
    wvb = at("wvb", [128, FF], BF16, 147968)
    qg = at("qg", [128, FF], BF16, 147968)
    wpb = at("wpb", [128, 8 * C], BF16, 147968)
    qgT = at("qgT", [128, FF], BF16, 156160)
    xq = at("xq", [128, C], BF16, 164352)
    xqT = at("xqT", [128, C], BF16, 166400)
    b1bc = at("b1bc", [128, FF], BF16, 168448)
    h2raw = at("h2raw", [128, 4 * C], BF16, 176640)
    eyef_s = at("eyef_s", [128, 128], F32, 180736)
    pslab_s = at("pslab_s", [8, 64], F32, 181248)
    csum = at("csum", [8, 1], F32, 181504)
    ebrow = at("ebrow", [8, 1], F32, 181536)
    ebt = at("ebt", [8, 1], F32, 181568)
    wstage = at("wstage", [128, 2048], F32, 186880)
    wstgb = at("wstgb", [128, 2048], BF16, 195072)
    pt = [at("pt0", [128, 1536], BF16, 199424),
          at("pt1", [128, 1536], BF16, 202496)]
    rcbc = at("rcbc", [128, 2048], BF16, 205568)
    tri_s = at("tri_s", [128, 128], BF16, 209664)
    eyeb_s = at("eyeb_s", [128, 128], BF16, 209920)
    ebias_bc = at("ebias_bc", [128, 8], F32, 210176)
    bq8 = at("bq8", [128, 4], F32, 210208)
    bk_s = at("bk_s", [128, 4], F32, 210240)
    pb_s = at("pb_s", [1, 1], I32, 210272)
    bvbc = at("bvbc", [128, 512], BF16, 210304)
    stat = at("stat", [128, 128], F32, 211328)
    ones64 = at("ones64", [128, 64], F32, 211840)
    wsc = at("wsc", [1, 16], F32, 212096)
    wsc2 = at("wsc2", [128, 4], F32, 212160)
    vfill = at("vfill", [128, 64], F32, 181632)

    psum_ctx = [nc.psum_tensor("sh0", [128, 1536], F32),
                nc.psum_tensor("sh1", [128, 1536], F32),
                nc.psum_tensor("otb", [128, 512], F32),
                nc.psum_tensor("aux", [128, 512], F32)]
    sh0, sh1, otb, aux = [c.__enter__().ap() for c in psum_ctx]
    SHP = [sh0, sh1]

    ctx = nc.Block()
    block = ctx.__enter__()
    semctx = [nc.semaphore(n) for n in
              ["s_ld", "s_gd", "s_t", "s_v", "s_s", "s_cc", "s_out"]]
    s_ld, s_gd, s_t, s_v, s_s, s_cc, s_out = [c.__enter__() for c in semctx]
    cnt = dict(ld=0, gd=0, t=0, v=0, s=0, cc=0, out=0)

    stopped = [False]

    def VF(e, n=2):
        li = None
        for _ in range(n):
            li = e.tensor_scalar(out=vfill, in0=vfill, scalar1=1.0,
                                 scalar2=None, op0=ALU.mult)
        return li

    def SY(fn):
        if stopped[0]:
            return
        @block.sync
        def _(e):
            fn(e)

    def GP(fn):
        if stopped[0]:
            return
        @block.gpsimd
        def _(e):
            fn(e)

    def TE(fn):
        if stopped[0]:
            return
        @block.tensor
        def _(e):
            fn(e)

    def VE(fn):
        if stopped[0]:
            return
        @block.vector
        def _(e):
            fn(e)

    def SE(fn):
        if stopped[0]:
            return
        @block.scalar
        def _(e):
            fn(e)

    def phase(name):
        if stopped[0] or upto is None or name != upto:
            return
        def drain(e):
            for sem, key in [(s_ld, "ld"), (s_gd, "gd"), (s_t, "t"),
                             (s_v, "v"), (s_s, "s"), (s_cc, "cc")]:
                e.wait_ge(sem, cnt[key])
            e.memset(ostg[0], 0.0)
            e.dma_start(out=bass.AP(out_d.tensor, 0, [[C, 128], [1, C]]),
                        in_=ostg[0]).then_inc(s_out, 16)
            cnt["out"] += 16
            e.wait_ge(s_out, cnt["out"])
        GP(drain)
        stopped[0] = True

    def p0_sync(e):
        for dst, src in [(pb_s, psel), (tri_s, tri_in), (eyef_s, eyef_in),
                         (eyeb_s, eyeb_in), (pslab_s, pos_slab)]:
            e.dma_start(out=dst, in_=src).then_inc(s_ld, 16)
            cnt["ld"] += 16
        for dst, src in [(bq8, bq_in), (bk_s, bk_in)]:
            e.dma_start(out=dst,
                        in_=bass.AP(src.tensor, 0, [[1, 128], [128, 4]]),
                        allow_slow_non_contiguous=True).then_inc(s_ld, 16)
            cnt["ld"] += 16
        e.dma_start(out=ebrow,
                    in_=bass.AP(ebias_in.tensor, 0, [[1, 8], [1, 1]])
                    ).then_inc(s_ld, 16)
        cnt["ld"] += 16
        e.dma_start(out=wstage[:, 0:512],
                    in_=bass.AP(bv_in.tensor, 0, [[0, 128], [1, 512]])
                    ).then_inc(s_ld, 16)
        cnt["ld"] += 16
    SY(p0_sync)
    ld_p0 = cnt["ld"]

    def p0_vec(e):
        e.wait_ge(s_ld, ld_p0)
        e.reduce_sum(out=csum, in_=pslab_s, axis=AX.X)
        e.memset(bass.AP(vbuf.tensor, 64,
                         [[520 * 32, 128], [520, 32], [65, 8], [1, 1]]), 1.0)
        e.memset(ones64, 1.0)
        e.tensor_tensor(out=ebt, in0=ebrow, in1=csum, op=ALU.add)
        e.memset(stat, 0.0)
        e.memset(wsc, 0.0)
        e.tensor_tensor(out=ebrow, in0=ebt, in1=csum, op=ALU.subtract)
        e.memset(vfill, 0.0)
        e.tensor_scalar(out=bq8, in0=bq8, scalar1=0.125, scalar2=None,
                        op0=ALU.mult)
        e.tensor_copy(out=bvbc, in_=wstage[:, 0:512]).then_inc(s_v, 1)
        cnt["v"] += 1
    VE(p0_vec)
    v_p0 = cnt["v"]

    def p0_gp(e):
        e.wait_ge(s_v, v_p0)
        e.dma_start(out=scr8,
                    in_=bass.AP(ebrow.tensor, 0, [[1, 8], [1, 1]])
                    ).then_inc(s_gd, 16)
        cnt["gd"] += 16
        e.wait_ge(s_gd, cnt["gd"])
        for h in range(8):
            e.dma_start(out=ebias_bc[:, h:h + 1],
                        in_=bass.AP(scr8.tensor, h, [[0, 128], [1, 1]])
                        ).then_inc(s_gd, 16)
            cnt["gd"] += 16
    GP(p0_gp)
    phase("p0")

    xck_evac = [0] * 16
    for ck in range(16):
        def x_ld(e, ck=ck):
            if ck == 0:
                e.wait_ge(s_v, v_p0)
            else:
                e.wait_ge(s_v, xck_evac[ck - 1])
            e.dma_start(out=wstage,
                        in_=bass.AP(x_full.tensor, (256 * ck) * C,
                                    [[C, 128], [128 * C, 2], [1, C]])
                        ).then_inc(s_ld, 16)
            cnt["ld"] += 16
        SY(x_ld)
        ldk = cnt["ld"]

        def x_tp(e, ck=ck, ldk=ldk):
            e.wait_ge(s_ld, ldk)
            if ck >= 1:
                e.wait_ge(s_v, xck_evac[ck - 1])
            li = None
            for a in range(2):
                bank = sh0 if a == 0 else sh1
                for c8 in range(8):
                    li = e.transpose(out=bass.AP(bank.tensor, 128 * c8,
                                                 [[1536, 128], [1, 128]]),
                                     in_=wstage[:, 1024 * a + 128 * c8:
                                                1024 * a + 128 * c8 + 128],
                                     identity=eyef_s)
            li.then_inc(s_t, 1)
            cnt["t"] += 1
        TE(x_tp)
        t_tp = cnt["t"]

        def x_ev(e, ck=ck, t_tp=t_tp):
            e.wait_ge(s_t, t_tp)
            li = None
            for a in range(2):
                bank = sh0 if a == 0 else sh1
                for c8 in range(8):
                    li = e.tensor_copy(
                        out=xT[:, T * c8 + 256 * ck + 128 * a:
                               T * c8 + 256 * ck + 128 * a + 128],
                        in_=bank[:, 128 * c8:128 * c8 + 128])
            li.then_inc(s_v, 1)
            cnt["v"] += 1
        VE(x_ev)
        xck_evac[ck] = cnt["v"]
    phase("xT")

    wv_cast_v = [0, 0]
    for half in range(2):
        def wv_ld(e, half=half):
            e.wait_ge(s_v, xck_evac[15] if half == 0 else wv_cast_v[0])
            e.dma_start(out=wstage,
                        in_=bass.AP(wv_t.tensor, 256 * half,
                                    [[512, 128], [512 * 128, 8], [1, 256]])
                        ).then_inc(s_ld, 16)
            cnt["ld"] += 16
        SY(wv_ld)
        ldw = cnt["ld"]

        def wv_c(e, half=half, ldw=ldw):
            e.wait_ge(s_ld, ldw)
            e.tensor_copy(out=bass.AP(wvb.tensor, 256 * half,
                                      [[FF, 128], [512, 8], [1, 256]]),
                          in_=wstage).then_inc(s_v, 1)
            cnt["v"] += 1
        VE(wv_c)
        wv_cast_v[half] = cnt["v"]
    v_wvb = cnt["v"]

    vm_evac = [0] * 32
    for tt in range(32):
        def vmm(e, tt=tt):
            e.wait_ge(s_v, v_wvb)
            if tt >= 2:
                e.wait_ge(s_v, vm_evac[tt - 2])
            bank = otb if tt % 2 == 0 else aux
            for c8 in range(8):
                sl = xT[:, T * c8 + 128 * tt: T * c8 + 128 * tt + 128]
                e.ldweights(sl)
                li = e.matmul(out=bank, lhsT=sl,
                              rhs=wvb[:, 512 * c8:512 * c8 + 512],
                              start=(c8 == 0), stop=(c8 == 7))
            li.then_inc(s_t, 1)
            cnt["t"] += 1
        TE(vmm)
        t_vm = cnt["t"]

        def vev(e, tt=tt, t_vm=t_vm):
            e.wait_ge(s_t, t_vm)
            bank = otb if tt % 2 == 0 else aux
            e.scalar_tensor_tensor(
                out=bass.AP(vbuf.tensor, 520 * tt,
                            [[520 * 32, 128], [65, 8], [1, 64]]),
                in0=bass.AP(bank.tensor, 0, [[512, 128], [64, 8], [1, 64]]),
                scalar=1.0,
                in1=bass.AP(bvbc.tensor, 0, [[512, 128], [64, 8], [1, 64]]),
                op0=ALU.mult, op1=ALU.add).then_inc(s_v, 1)
            cnt["v"] += 1
        VE(vev)
        vm_evac[tt] = cnt["v"]
    phase("vmm")

    kq_ready = [0] * 8
    kq_aug_ld = [0] * 8
    exp_cp = {}
    mask_cp = {}
    pv_cp = {}
    sc_cp = {}
    head_last_sc = [0] * 8
    head_norm_v = [0] * 8
    stage_free = {"s": 0, "v": 0}

    def emit_pair_fill(hp, guard_t):
        b0, b1 = (2 * hp) % 3, (2 * hp + 1) % 3

        def w_ld(e, hp=hp):
            if hp == 0:
                e.wait_ge(s_v, v_wvb)
            else:
                e.wait_ge(s_v, stage_free["v"])
            e.dma_start(out=wstage[:, 0:1024],
                        in_=bass.AP(wk_t.tensor, 128 * hp,
                                    [[512, 128], [512 * 128, 8], [1, 128]])
                        ).then_inc(s_ld, 16)
            e.dma_start(out=wstage[:, 1024:2048],
                        in_=bass.AP(wq_t.tensor, 128 * hp,
                                    [[512, 128], [512 * 128, 8], [1, 128]])
                        ).then_inc(s_ld, 16)
            cnt["ld"] += 32
        SY(w_ld)
        ld_w = cnt["ld"]

        def w_cast(e, ld_w=ld_w):
            e.wait_ge(s_ld, ld_w)
            e.tensor_copy(out=wstgb, in_=wstage).then_inc(s_v, 1)
            cnt["v"] += 1
        VE(w_cast)
        v_cast = cnt["v"]

        def aug_ld(e, hp=hp, b0=b0, b1=b1, guard_t=guard_t):
            e.wait_ge(s_t, guard_t)
            for h_i, (kb, qb) in enumerate([(kaug[b0], qaug[b0]),
                                            (kaug[b1], qaug[b1])]):
                h = 2 * hp + h_i
                ab = 64 if h % 2 == 0 else 0
                e.dma_start(out=kb[ab:ab + 64, :], in_=psi_in[h, :, :]
                            ).then_inc(s_ld, 16)
                e.dma_start(out=qb[ab:ab + 64, :], in_=phi_in[h, :, :]
                            ).then_inc(s_ld, 16)
                cnt["ld"] += 32
        SY(aug_ld)
        ld_aug = cnt["ld"]

        evk = []
        for which in range(2):
            wbase = 1024 * which
            for st in range(8):
                def kqmm(e, which=which, wbase=wbase, st=st, v_cast=v_cast,
                         guard_t=guard_t):
                    e.wait_ge(s_v, v_cast)
                    e.wait_ge(s_t, guard_t)
                    if evk:
                        e.wait_ge(s_s, evk[-1])
                    for c8 in range(8):
                        sl = wstgb[:, wbase + 128 * c8: wbase + 128 * c8 + 128]
                        e.ldweights(sl)
                        li = e.matmul(out=aux, lhsT=sl,
                                      rhs=xT[:, T * c8 + 512 * st:
                                             T * c8 + 512 * st + 512],
                                      start=(c8 == 0), stop=(c8 == 7))
                    li.then_inc(s_t, 1)
                    cnt["t"] += 1
                TE(kqmm)
                t_mm = cnt["t"]

                def kqev(e, which=which, st=st, t_mm=t_mm, hp=hp, b0=b0, b1=b1):
                    e.wait_ge(s_t, t_mm)
                    for h_i in range(2):
                        h = 2 * hp + h_i
                        dst = (kaug if which == 0 else qaug)[b0 if h_i == 0 else b1]
                        db = 0 if h % 2 == 0 else 64
                        src = aux[64 * h_i:64 * h_i + 64, :]
                        if which == 0:
                            li = e.activation(
                                out=dst[db:db + 64, 512 * st:512 * st + 512],
                                in_=src, func=AF.Identity,
                                bias=bk_s[64 * h_i:64 * h_i + 64, hp:hp + 1])
                        else:
                            li = e.activation(
                                out=dst[db:db + 64, 512 * st:512 * st + 512],
                                in_=src, func=AF.Identity,
                                bias=bq8[64 * h_i:64 * h_i + 64, hp:hp + 1],
                                scale=np.float32(0.125))
                    li.then_inc(s_s, 1)
                    cnt["s"] += 1
                SE(kqev)
                evk.append(cnt["s"])
        stage_free["s"] = cnt["s"]
        stage_free["v"] = v_cast
        for h_i in range(2):
            kq_ready[2 * hp + h_i] = cnt["s"]
            kq_aug_ld[2 * hp + h_i] = ld_aug

    def emit_head(h):
        Ka, Qa = kaug[h % 3], qaug[h % 3]
        for st in range(8):
            ntk = 4 * st + 4
            grps = _groups(ntk)
            for g, kjs in enumerate(grps):
                par = g % 2

                def t_sc(e, h=h, st=st, g=g, kjs=kjs, par=par):
                    e.wait_ge(s_s, kq_ready[h])
                    e.wait_ge(s_ld, kq_aug_ld[h])
                    key = ("shfree", par)
                    if key in exp_cp:
                        e.wait_ge(s_s, exp_cp[key])
                    for i, kj in enumerate(kjs):
                        sl = Ka[:, 128 * kj:128 * kj + 128]
                        e.ldweights(sl)
                        li = e.matmul(out=SHP[par][:, 512 * i:512 * i + 512],
                                      lhsT=sl,
                                      rhs=Qa[:, 512 * st:512 * st + 512],
                                      start=True, stop=True)
                    li.then_inc(s_t, 1)
                    cnt["t"] += 1
                TE(t_sc)
                sc_cp[(h, st, g)] = cnt["t"]

                def s_exp(e, h=h, st=st, g=g, kjs=kjs, par=par):
                    e.wait_ge(s_t, sc_cp[(h, st, g)])
                    key = ("ptfree", par)
                    if key in pv_cp:
                        e.wait_ge(s_t, pv_cp[key])
                    w = 512 * len(kjs)
                    e.activation(out=pt[par][:, 0:w], in_=SHP[par][:, 0:w],
                                 func=AF.Exp,
                                 bias=ebias_bc[:, h:h + 1]).then_inc(s_s, 1)
                    cnt["s"] += 1
                SE(s_exp)
                exp_cp[(h, st, g)] = cnt["s"]
                exp_cp[("shfree", par)] = cnt["s"]

                def v_mask(e, h=h, st=st, g=g, kjs=kjs, par=par):
                    e.wait_ge(s_s, exp_cp[(h, st, g)])
                    li = None
                    for i, kj in enumerate(kjs):
                        d = kj - 4 * st
                        if 0 <= d <= 3:
                            sl = slice(512 * i + 128 * d, 512 * i + 128 * d + 128)
                            li = e.tensor_tensor(out=pt[par][:, sl],
                                                 in0=pt[par][:, sl],
                                                 in1=tri_s, op=ALU.mult)
                    if li is not None:
                        li.then_inc(s_v, 1)
                    else:
                        e.nop().then_inc(s_v, 1)
                    cnt["v"] += 1
                VE(v_mask)
                mask_cp[(h, st, g)] = cnt["v"]

                def t_pv(e, h=h, st=st, g=g, kjs=kjs, par=par, ntk=ntk):
                    e.wait_ge(s_v, mask_cp[(h, st, g)])
                    if g == 0 and ("otfree",) in pv_cp:
                        e.wait_ge(s_v, pv_cp[("otfree",)])
                    for i, kj in enumerate(kjs):
                        c0 = max(0, kj - 4 * st) * 128
                        vsl = vbuf[:, 520 * kj + 65 * h: 520 * kj + 65 * h + 65]
                        e.ldweights(vsl)
                        li = e.matmul(out=otb[0:65, c0:512], lhsT=vsl,
                                      rhs=pt[par][:, 512 * i + c0:512 * i + 512],
                                      start=(kj == 0), stop=(kj == ntk - 1))
                    li.then_inc(s_t, 1)
                    cnt["t"] += 1
                TE(t_pv)
                pv_cp[(h, st, g)] = cnt["t"]
                pv_cp[("ptfree", par)] = cnt["t"]

            def v_strip(e, h=h, st=st, ngr=len(grps)):
                e.wait_ge(s_t, pv_cp[(h, st, ngr - 1)])
                lo = 64 * (h % 2)
                e.tensor_copy(out=onorm[lo:lo + 64,
                                        T * (h // 2) + 512 * st:
                                        T * (h // 2) + 512 * st + 512],
                              in_=otb[0:64, :])
                e.tensor_copy(out=rcbc[0:1, 512 * (st % 4):512 * (st % 4) + 512],
                              in_=otb[64:65, :]).then_inc(s_v, 1)
                cnt["v"] += 1
            VE(v_strip)
            pv_cp[("otfree",)] = cnt["v"]

            if st in (3, 7):
                hh = (st - 3) // 4

                def v_recip(e):
                    e.wait_ge(s_v, pv_cp[("otfree",)])
                    VF(e)
                    with nc.allow_low_precision(reason="bf16 softmax sums"):
                        e.reciprocal(out=rcbc[0:1, :], in_=rcbc[0:1, :]
                                     ).then_inc(s_v, 1)
                    cnt["v"] += 1
                VE(v_recip)
                v_rc = cnt["v"]

                def g_bc(e, v_rc=v_rc):
                    e.wait_ge(s_v, v_rc)
                    e.dma_start(out=scr_rc, in_=rcbc[0:1, :]).then_inc(s_gd, 16)
                    cnt["gd"] += 16
                    e.wait_ge(s_gd, cnt["gd"])
                    e.dma_start(out=rcbc,
                                in_=bass.AP(scr_rc.tensor, 0,
                                            [[0, 128], [1, 2048]])
                                ).then_inc(s_gd, 16)
                    cnt["gd"] += 16
                GP(g_bc)
                gd_rc = cnt["gd"]

                def v_norm(e, h=h, hh=hh, gd_rc=gd_rc):
                    e.wait_ge(s_gd, gd_rc)
                    lo = 64 * (h % 2)
                    sl = slice(T * (h // 2) + 2048 * hh,
                               T * (h // 2) + 2048 * hh + 2048)
                    e.tensor_tensor(out=onorm[lo:lo + 64, sl],
                                    in0=onorm[lo:lo + 64, sl],
                                    in1=rcbc[lo:lo + 64, :],
                                    op=ALU.mult).then_inc(s_v, 1)
                    cnt["v"] += 1
                VE(v_norm)
        head_last_sc[h] = sc_cp[(h, 7, len(_groups(32)) - 1)]
        head_norm_v[h] = cnt["v"]

    def emit_wquant(w_in, wrows, wdst, wcols, m_col, race_t):
        nchunks = wrows // 128
        spl = 2 if wcols >= 2048 else 1
        tw = wcols // spl
        tiles = [(c, j) for c in range(nchunks) for j in range(spl)]
        absv = []
        for idx, (c, j) in enumerate(tiles):
            def wl(e, c=c, j=j, idx=idx):
                if idx == 0:
                    e.wait_ge(s_v, stage_free["v"])
                else:
                    e.wait_ge(s_v, absv[-1])
                e.dma_start(out=wstage[:, 0:tw],
                            in_=bass.AP(w_in.tensor, (128 * c) * wcols + tw * j,
                                        [[wcols, 128], [1, tw]])
                            ).then_inc(s_ld, 16)
                cnt["ld"] += 16
            SY(wl)
            ldw = cnt["ld"]

            def wa(e, idx=idx, ldw=ldw):
                e.wait_ge(s_ld, ldw)
                e.reduce_sum(out=stat[:, 64 + idx:65 + idx],
                             in_=wstage[:, 0:tw], axis=AX.X,
                             apply_absolute_value=True).then_inc(s_v, 1)
                cnt["v"] += 1
            VE(wa)
            absv.append(cnt["v"])
        ntile = len(tiles)

        def w_zero(e, ntile=ntile):
            e.wait_ge(s_v, absv[-1])
            if ntile < 64:
                e.memset(stat[:, 64 + ntile:128], 0.0).then_inc(s_v, 1)
            cnt["v"] += 1
        VE(w_zero)
        v_z = cnt["v"]

        def w_mm(e, v_z=v_z):
            e.wait_ge(s_v, v_z)
            e.wait_ge(s_s, stage_free["s"])
            e.matmul(out=aux[0:64, 0:64], lhsT=ones64, rhs=stat[:, 64:128],
                     start=True, stop=True).then_inc(s_t, 1)
            cnt["t"] += 1
        TE(w_mm)
        t_mm = cnt["t"]

        def w_chain(e, t_mm=t_mm, m_col=m_col):
            e.wait_ge(s_t, t_mm)
            e.reduce_sum(out=wsc[0:1, m_col:m_col + 1], in_=aux[0:1, 0:64],
                         axis=AX.X)
            VF(e)
            e.tensor_scalar(out=wsc[0:1, m_col:m_col + 1],
                            in0=wsc[0:1, m_col:m_col + 1],
                            scalar1=float(np.float32(1.0 / (wrows * wcols))), scalar2=1e-5,
                            op0=ALU.mult, op1=ALU.max)
            VF(e)
            e.reciprocal(out=wsc[0:1, m_col + 1:m_col + 2],
                         in_=wsc[0:1, m_col:m_col + 1]).then_inc(s_v, 1)
            cnt["v"] += 1
        VE(w_chain)
        v_ch = cnt["v"]

        def g_wbc(e, v_ch=v_ch, m_col=m_col):
            e.wait_ge(s_v, v_ch)
            e.dma_start(out=scr1, in_=wsc[0:1, m_col:m_col + 4]
                        ).then_inc(s_gd, 16)
            cnt["gd"] += 16
            e.wait_ge(s_gd, cnt["gd"])
            for k in range(2):
                e.dma_start(out=wsc2[:, m_col + k:m_col + k + 1],
                            in_=bass.AP(scr1.tensor, k, [[0, 128], [1, 1]])
                            ).then_inc(s_gd, 16)
                cnt["gd"] += 16
        GP(g_wbc)
        gd_bc = cnt["gd"]

        rl = []
        for idx, (c, j) in enumerate(tiles):
            def wl2(e, c=c, j=j, idx=idx):
                if rl:
                    e.wait_ge(s_v, rl[-1])
                else:
                    e.wait_ge(s_gd, gd_bc)
                e.dma_start(out=wstage[:, 0:tw],
                            in_=bass.AP(w_in.tensor, (128 * c) * wcols + tw * j,
                                        [[wcols, 128], [1, tw]])
                            ).then_inc(s_ld, 16)
                cnt["ld"] += 16
            SY(wl2)
            ldw = cnt["ld"]

            def wr(e, c=c, j=j, ldw=ldw, m_col=m_col, idx=idx):
                e.wait_ge(s_ld, ldw)
                if idx == 0:
                    e.wait_ge(s_gd, gd_bc)
                    e.wait_ge(s_t, race_t)
                for q in range(tw // 1024):
                    e.tensor_scalar(out=wi32,
                                    in0=wstage[:, 1024 * q:1024 * q + 1024],
                                    scalar1=wsc2[:, m_col + 1:m_col + 2],
                                    scalar2=None, op0=ALU.mult)
                    li = e.tensor_scalar(
                        out=wdst[:, wcols * c + tw * j + 1024 * q:
                                 wcols * c + tw * j + 1024 * q + 1024],
                        in0=wi32, scalar1=-1.0, scalar2=1.0,
                        op0=ALU.max, op1=ALU.min)
                li.then_inc(s_v, 1)
                cnt["v"] += 1
            VE(wr)
            rl.append(cnt["v"])
        stage_free["v"] = cnt["v"]
        return cnt["v"]

    emit_pair_fill(0, guard_t=0)
    phase("fill0")
    emit_head(0)
    phase("head0")
    emit_pair_fill(1, guard_t=head_last_sc[0])
    emit_head(1)
    emit_head(2)
    emit_pair_fill(2, guard_t=head_last_sc[2])
    emit_head(3)
    phase("head3")
    emit_head(4)
    emit_pair_fill(3, guard_t=head_last_sc[4])
    emit_head(5)
    phase("head5")
    v_w1_done = emit_wquant(w1_t, C, w1q, FF, 0, race_t=head_last_sc[5])
    phase("w1q")
    emit_head(6)
    v_w2_done = emit_wquant(w2_t, FF, w2q, C, 2, race_t=head_last_sc[6])
    emit_head(7)
    phase("attn")

    def g_exch(e):
        for h in range(8):
            e.wait_ge(s_v, head_norm_v[h])
        with e.register("rp") as rp:
            e.reg_load(rp, pb_s[0:1, 0:1])
            with e.If_eq(rp, 0):
                for u in range(4):
                    e.dma_start(out=gift[:, 2048 * u:2048 * u + 2048],
                                in_=onorm[:, T * u + 2048:T * u + 4096]
                                ).then_inc(s_gd, 16)
            with e.Else():
                for u in range(4):
                    e.dma_start(out=gift[:, 2048 * u:2048 * u + 2048],
                                in_=onorm[:, T * u:T * u + 2048]
                                ).then_inc(s_gd, 16)
            cnt["gd"] += 64
            e.wait_ge(s_gd, cnt["gd"])
            e.collective_compute(
                "AllGather", ALU.bypass,
                replica_groups=[[0, 1], [2, 3], [4, 5], [6, 7]],
                ins=[gift], outs=[gath]).then_inc(s_cc, 1)
            cnt["cc"] += 1
            e.wait_ge(s_cc, cnt["cc"])
            with e.If_eq(rp, 0):
                for u in range(4):
                    e.dma_start(out=pstage[:, 2048 * u:2048 * u + 2048],
                                in_=onorm[:, T * u:T * u + 2048]
                                ).then_inc(s_gd, 16)
                e.dma_start(out=peerO, in_=gath[1]).then_inc(s_gd, 16)
            with e.Else():
                for u in range(4):
                    e.dma_start(out=pstage[:, 2048 * u:2048 * u + 2048],
                                in_=onorm[:, T * u + 2048:T * u + 4096]
                                ).then_inc(s_gd, 16)
                e.dma_start(out=peerO, in_=gath[0]).then_inc(s_gd, 16)
            cnt["gd"] += 80
    GP(g_exch)
    gd_exch = cnt["gd"]
    phase("exch")

    def bp_ld(e):
        e.wait_ge(s_v, v_w2_done)
        e.dma_start(out=wstage[:, 0:1024],
                    in_=bass.AP(bproj_in.tensor, 0, [[0, 128], [1, C]])
                    ).then_inc(s_ld, 16)
        cnt["ld"] += 16
    SY(bp_ld)
    ld_bp = cnt["ld"]

    def bp_c(e, ld_bp=ld_bp):
        e.wait_ge(s_ld, ld_bp)
        e.tensor_scalar(out=wsc2[:, 3:4], in0=wsc2[:, 2:3],
                        scalar1=float(np.float32(1.0 / 127.0)), scalar2=None,
                        op0=ALU.mult)
        e.tensor_copy(out=bpbc, in_=wstage[:, 0:1024]).then_inc(s_v, 1)
        cnt["v"] += 1
    VE(bp_c)
    v_bp = cnt["v"]

    wp_cast = [0] * 4
    for q4 in range(4):
        def wp_ld(e, q4=q4):
            e.wait_ge(s_v, v_bp if q4 == 0 else wp_cast[q4 - 1])
            if q4 == 0:
                e.wait_ge(s_gd, gd_exch)
            e.dma_start(out=wstage,
                        in_=bass.AP(wproj_t.tensor, 256 * q4,
                                    [[C, 128], [128 * C, 8], [1, 256]])
                        ).then_inc(s_ld, 16)
            cnt["ld"] += 16
        SY(wp_ld)
        ldw = cnt["ld"]

        def wp_c(e, q4=q4, ldw=ldw):
            e.wait_ge(s_ld, ldw)
            if q4 == 0:
                e.wait_ge(s_gd, gd_exch)
            e.tensor_copy(out=bass.AP(wpb.tensor, 256 * q4,
                                      [[8 * C, 128], [C, 8], [1, 256]]),
                          in_=wstage).then_inc(s_v, 1)
            cnt["v"] += 1
        VE(wp_c)
        wp_cast[q4] = cnt["v"]
    v_wp = cnt["v"]
    phase("wp")

    proj_done_v = [0] * NTT
    for tt in range(NTT):
        def xr_ld(e, tt=tt):
            if tt >= 2:
                e.wait_ge(s_v, proj_done_v[tt - 2])
            e.dma_start(out=xres[tt % 2],
                        in_=bass.AP(x_own.tensor, 128 * tt * C,
                                    [[C, 128], [1, C]])).then_inc(s_gd, 16)
            cnt["gd"] += 16
        GP(xr_ld)
        ld_xr = cnt["gd"]

        def pmm(e, tt=tt):
            e.wait_ge(s_v, v_wp)
            e.wait_ge(s_gd, gd_exch)
            if tt >= 2:
                e.wait_ge(s_v, proj_done_v[tt - 2])
            bset = sh0 if tt % 2 == 0 else sh1
            for dgrp in range(2):
                bank = bset[:, 512 * dgrp:512 * dgrp + 512]
                for ci in range(8):
                    if ci < 4:
                        sl = pstage[:, 2048 * ci + 128 * tt:
                                    2048 * ci + 128 * tt + 128]
                    else:
                        sl = peerO[:, 2048 * (ci - 4) + 128 * tt:
                                   2048 * (ci - 4) + 128 * tt + 128]
                    e.ldweights(sl)
                    li = e.matmul(out=bank, lhsT=sl,
                                  rhs=bass.AP(wpb.tensor, C * ci + 512 * dgrp,
                                              [[8 * C, 128], [1, 512]]),
                                  start=(ci == 0), stop=(ci == 7))
            li.then_inc(s_t, 1)
            cnt["t"] += 1
        TE(pmm)
        t_p = cnt["t"]

        def pev(e, tt=tt, t_p=t_p, ld_xr=ld_xr):
            e.wait_ge(s_t, t_p)
            e.wait_ge(s_gd, ld_xr)
            bset = sh0 if tt % 2 == 0 else sh1
            for dgrp in range(2):
                e.scalar_tensor_tensor(
                    out=tscr[:, 512 * dgrp:512 * dgrp + 512],
                    in0=bset[:, 512 * dgrp:512 * dgrp + 512], scalar=1.0,
                    in1=bpbc[:, 512 * dgrp:512 * dgrp + 512],
                    op0=ALU.mult, op1=ALU.add)
            e.tensor_tensor(out=sbf[:, C * tt:C * tt + C], in0=tscr,
                            in1=xres[tt % 2], op=ALU.add).then_inc(s_v, 1)
            cnt["v"] += 1
        VE(pev)
        proj_done_v[tt] = cnt["v"]
    v_proj = cnt["v"]

    phase("proj")
    for c in reversed(psum_ctx):
        c.__exit__(None, None, None)
    psum_ctx2 = [nc.psum_tensor("f1b", [128, 1536], F32),
                 nc.psum_tensor("f2b", [128, 1536], F32),
                 nc.psum_tensor("tpb", [128, 1024], BF16),
                 nc.psum_tensor("tpb2", [128, 1024], BF16)]
    f1b, f2b, tpb, tpb2 = [c.__enter__().ap() for c in psum_ctx2]

    b_v = [0, 0, 0]
    for bi, (src, dst, width) in enumerate(
            [(b1_in, b1bc[:, 0:2048], 2048),
             (None, b1bc[:, 2048:4096], 2048),
             (b2_in, b2bc, C)]):
        def bb_ld(e, bi=bi, width=width):
            e.wait_ge(s_v, v_proj if bi == 0 else b_v[bi - 1])
            base = 2048 if bi == 1 else 0
            e.dma_start(out=wstage[:, 0:width],
                        in_=bass.AP((b1_in if bi < 2 else b2_in).tensor, base,
                                    [[0, 128], [1, width]])
                        ).then_inc(s_ld, 16)
            cnt["ld"] += 16
        SY(bb_ld)
        ldb = cnt["ld"]

        def bb_c(e, bi=bi, dst=dst, width=width, ldb=ldb):
            e.wait_ge(s_ld, ldb)
            e.tensor_copy(out=dst, in_=wstage[:, 0:width]).then_inc(s_v, 1)
            cnt["v"] += 1
        VE(bb_c)
        b_v[bi] = cnt["v"]
    v_bias = cnt["v"]

    def ln1_stats(e):
        e.wait_ge(s_v, v_bias)
        for tt in range(NTT):
            sl = sbf[:, C * tt:C * tt + C]
            e.reduce_sum(out=stat[:, 16 + tt:17 + tt], in_=sl, axis=AX.X)
            e.scalar_tensor_tensor(out=pt[0][:, 0:C], in0=sl, scalar=1.0,
                                   in1=sl, op0=ALU.mult, op1=ALU.mult,
                                   accum_out=stat[:, tt:tt + 1])
            e.reduce_max(out=stat[:, 96 + tt:97 + tt], in_=sl, axis=AX.X)
            li = e.tensor_reduce(out=stat[:, 112 + tt:113 + tt], in_=sl,
                                 axis=AX.X, op=ALU.min)
        li.then_inc(s_v, 1)
        cnt["v"] += 1
    VE(ln1_stats)
    v_st = cnt["v"]

    def ln1_c1(e, v_st=v_st):
        e.wait_ge(s_v, v_st)
        e.tensor_scalar(out=stat[:, 16:32], in0=stat[:, 16:32],
                        scalar1=float(np.float32(1.0 / C)), scalar2=None, op0=ALU.mult)
        e.tensor_scalar(out=stat[:, 0:16], in0=stat[:, 0:16],
                        scalar1=float(np.float32(1.0 / C)), scalar2=1e-5,
                        op0=ALU.mult, op1=ALU.add)
        e.tensor_tensor(out=stat[:, 112:128], in0=stat[:, 16:32],
                        in1=stat[:, 112:128], op=ALU.subtract)
        e.tensor_tensor(out=stat[:, 96:112], in0=stat[:, 96:112],
                        in1=stat[:, 16:32], op=ALU.subtract)
        e.tensor_tensor(out=stat[:, 80:96], in0=stat[:, 16:32],
                        in1=stat[:, 16:32], op=ALU.mult)
        e.tensor_tensor(out=stat[:, 96:112], in0=stat[:, 96:112],
                        in1=stat[:, 112:128], op=ALU.max)
        e.tensor_tensor(out=stat[:, 0:16], in0=stat[:, 0:16],
                        in1=stat[:, 80:96], op=ALU.subtract
                        ).then_inc(s_v, 1)
        cnt["v"] += 1
    VE(ln1_c1)
    v_c1 = cnt["v"]

    def ln1_s1(e, v_c1=v_c1):
        e.wait_ge(s_v, v_c1)
        e.activation(out=stat[:, 32:48], in_=stat[:, 0:16], func=AF.Sqrt,
                     bias=np.float32(0.0)).then_inc(s_s, 1)
        cnt["s"] += 1
    SE(ln1_s1)
    s_sq1 = cnt["s"]

    def ln1_c2(e, s_sq1=s_sq1):
        e.wait_ge(s_s, s_sq1)
        e.reciprocal(out=stat[:, 32:48], in_=stat[:, 32:48])
        e.tensor_scalar(out=stat[:, 112:128], in0=stat[:, 0:16],
                        scalar1=-1e-5, scalar2=None, op0=ALU.add)
        e.tensor_tensor(out=stat[:, 48:64], in0=stat[:, 16:32],
                        in1=stat[:, 32:48], op=ALU.mult)
        e.reciprocal(out=stat[:, 16:32], in_=stat[:, 0:16])
        e.tensor_scalar(out=stat[:, 48:64], in0=stat[:, 48:64],
                        scalar1=-1.0, scalar2=None, op0=ALU.mult)
        e.tensor_tensor(out=stat[:, 112:128], in0=stat[:, 112:128],
                        in1=stat[:, 16:32], op=ALU.mult
                        ).then_inc(s_v, 1)
        cnt["v"] += 1
    VE(ln1_c2)
    v_c2 = cnt["v"]

    def ln1_s2(e, v_c2=v_c2):
        e.wait_ge(s_v, v_c2)
        e.activation(out=stat[:, 16:32], in_=stat[:, 112:128], func=AF.Sqrt,
                     bias=np.float32(1e-5)).then_inc(s_s, 1)
        cnt["s"] += 1
    SE(ln1_s2)
    s_sq2 = cnt["s"]

    def ln1_c3(e, s_sq2=s_sq2):
        e.wait_ge(s_s, s_sq2)
        e.reciprocal(out=stat[:, 16:32], in_=stat[:, 16:32])
        e.tensor_tensor(out=stat[:, 96:112], in0=stat[:, 96:112],
                        in1=stat[:, 32:48], op=ALU.mult)
        VF(e, 1)
        e.tensor_tensor(out=stat[:, 112:128], in0=stat[:, 96:112],
                        in1=stat[:, 16:32], op=ALU.mult)
        VF(e)
        e.tensor_scalar(out=stat[:, 112:128], in0=stat[:, 112:128],
                        scalar1=1e-5, scalar2=None, op0=ALU.max)
        VF(e, 1)
        e.tensor_scalar(out=stat[:, 80:96], in0=stat[:, 112:128],
                        scalar1=wsc2[:, 0:1], scalar2=float(np.float32(1.0 / 127.0)),
                        op0=ALU.mult, op1=ALU.mult)
        e.reciprocal(out=stat[:, 112:128], in_=stat[:, 112:128])
        VF(e)
        e.tensor_scalar(out=stat[:, 112:128], in0=stat[:, 112:128],
                        scalar1=127.0, scalar2=None, op0=ALU.mult)
        VF(e)
        e.tensor_tensor(out=stat[:, 64:80], in0=stat[:, 16:32],
                        in1=stat[:, 112:128], op=ALU.mult)
        VF(e).then_inc(s_v, 1)
        cnt["v"] += 1
    VE(ln1_c3)
    v_c3 = cnt["v"]

    def ln1_apply(e, v_c3=v_c3):
        e.wait_ge(s_v, v_c3)
        for tt in range(NTT):
            li = e.activation(out=sbf[:, C * tt:C * tt + C],
                              in_=sbf[:, C * tt:C * tt + C], func=AF.Identity,
                              bias=stat[:, 48 + tt:49 + tt],
                              scale=stat[:, 32 + tt:33 + tt])
        li.then_inc(s_s, 1)
        cnt["s"] += 1
    SE(ln1_apply)
    s_y = cnt["s"]
    phase("ln1")

    ffn_v = {}
    ffn_t = {}
    out_sent = []

    for grp4 in range(4):
        for ti in range(4):
            tt = 4 * grp4 + ti

            def xquant(e, tt=tt):
                e.wait_ge(s_s, s_y)
                if ("xqfree",) in ffn_t:
                    e.wait_ge(s_t, ffn_t[("xqfree",)])
                e.tensor_scalar(out=wi32, in0=sbf[:, C * tt:C * tt + C],
                                scalar1=stat[:, 64 + tt:65 + tt],
                                scalar2=None, op0=ALU.mult)
                e.tensor_copy(out=xq, in_=wi32).then_inc(s_v, 1)
                cnt["v"] += 1
            VE(xquant)
            v_xq = cnt["v"]

            def xqt(e, v_xq=v_xq):
                e.wait_ge(s_v, v_xq)
                if ("xqTev",) in ffn_v:
                    e.wait_ge(s_v, ffn_v[("xqTev",)])
                for c8 in range(8):
                    li = e.transpose(out=tpb[:, 128 * c8:128 * c8 + 128],
                                     in_=xq[:, 128 * c8:128 * c8 + 128],
                                     identity=eyeb_s)
                li.then_inc(s_t, 1)
                cnt["t"] += 1
            TE(xqt)
            ffn_t[("xqfree",)] = cnt["t"]
            t_xq = cnt["t"]

            def xqev(e, t_xq=t_xq):
                e.wait_ge(s_t, t_xq)
                e.tensor_copy(out=xqT, in_=tpb).then_inc(s_v, 1)
                cnt["v"] += 1
            VE(xqev)
            ffn_v[("xqTev",)] = cnt["v"]
            v_xqT = cnt["v"]

            for fg in range(8):
                def f1mm(e, tt=tt, fg=fg, v_xqT=v_xqT):
                    e.wait_ge(s_v, v_xqT)
                    e.wait_ge(s_v, v_w1_done)
                    if ("f1ev", fg % 3) in ffn_v:
                        e.wait_ge(s_v, ffn_v[("f1ev", fg % 3)])
                    bank = f1b[:, 512 * (fg % 3):512 * (fg % 3) + 512]
                    for c8 in range(8):
                        sl = xqT[:, 128 * c8:128 * c8 + 128]
                        e.ldweights(sl)
                        li = e.matmul(out=bank, lhsT=sl,
                                      rhs=w1q[:, FF * c8 + 512 * fg:
                                              FF * c8 + 512 * fg + 512],
                                      start=(c8 == 0), stop=(c8 == 7))
                    li.then_inc(s_t, 1)
                    cnt["t"] += 1
                TE(f1mm)
                t_f1 = cnt["t"]

                def f1ev(e, tt=tt, fg=fg, t_f1=t_f1):
                    e.wait_ge(s_t, t_f1)
                    bank = f1b[:, 512 * (fg % 3):512 * (fg % 3) + 512]
                    e.scalar_tensor_tensor(
                        out=hbuf[:, 512 * fg:512 * fg + 512], in0=bank,
                        scalar=stat[:, 80 + tt:81 + tt],
                        in1=b1bc[:, 512 * fg:512 * fg + 512],
                        op0=ALU.mult, op1=ALU.add).then_inc(s_v, 1)
                    cnt["v"] += 1
                VE(f1ev)
                ffn_v[("f1ev", fg % 3)] = cnt["v"]
                v_h = cnt["v"]

                def f1g(e, fg=fg, v_h=v_h):
                    e.wait_ge(s_v, v_h)
                    e.activation(out=hbuf[:, 512 * fg:512 * fg + 512],
                                 in_=hbuf[:, 512 * fg:512 * fg + 512],
                                 func=AF.Gelu,
                                 bias=np.float32(0.0)).then_inc(s_s, 1)
                    cnt["s"] += 1
                SE(f1g)
                ffn_v[("gelu", tt, fg)] = cnt["s"]

            def gsq(e, tt=tt, ti=ti):
                e.wait_ge(s_s, ffn_v[("gelu", tt, 7)])
                if ("qgfree",) in ffn_t:
                    e.wait_ge(s_t, ffn_t[("qgfree",)])
                e.activation(out=qg, in_=hbuf, func=AF.Square,
                             bias=np.float32(0.0),
                             accum_out=stat[:, ti:ti + 1]).then_inc(s_s, 1)
                cnt["s"] += 1
            SE(gsq)
            s_sq = cnt["s"]

            def gst(e, ti=ti, s_sq=s_sq):
                e.wait_ge(s_s, s_sq)
                e.reduce_max(out=stat[:, 4 + ti:5 + ti], in_=hbuf, axis=AX.X,
                             apply_absolute_value=True)
                e.tensor_scalar(out=stat[:, ti:ti + 1], in0=stat[:, ti:ti + 1],
                                scalar1=float(np.float32(1.0 / FF)), scalar2=1e-5,
                                op0=ALU.mult, op1=ALU.add)
                VF(e, 1)
                e.tensor_scalar(out=stat[:, 12:13], in0=stat[:, 4 + ti:5 + ti],
                                scalar1=1e-12, scalar2=None, op0=ALU.max)
                VF(e)
                e.reciprocal(out=stat[:, 12:13], in_=stat[:, 12:13])
                VF(e)
                e.tensor_scalar(out=stat[:, 12:13], in0=stat[:, 12:13],
                                scalar1=127.0, scalar2=None,
                                op0=ALU.mult)
                VF(e).then_inc(s_v, 1)
                cnt["v"] += 1
            VE(gst)
            v_gs = cnt["v"]

            def gquant(e, v_gs=v_gs):
                e.wait_ge(s_v, v_gs)
                for q in range(4):
                    e.tensor_scalar(out=wi32,
                                    in0=hbuf[:, 1024 * q:1024 * q + 1024],
                                    scalar1=stat[:, 12:13], scalar2=None,
                                    op0=ALU.mult)
                    li = e.tensor_copy(out=qg[:, 1024 * q:1024 * q + 1024],
                                       in_=wi32)
                li.then_inc(s_v, 1)
                cnt["v"] += 1
            VE(gquant)
            v_qg = cnt["v"]

            for sub in range(2):
                def qgt(e, sub=sub, v_qg=v_qg):
                    e.wait_ge(s_v, v_qg)
                    if ("qgTev",) in ffn_v:
                        e.wait_ge(s_v, ffn_v[("qgTev",)])
                    for k16 in range(16):
                        bank = tpb if k16 < 8 else tpb2
                        li = e.transpose(out=bank[:, 128 * (k16 % 8):
                                                  128 * (k16 % 8) + 128],
                                         in_=qg[:, 2048 * sub + 128 * k16:
                                                2048 * sub + 128 * k16 + 128],
                                         identity=eyeb_s)
                    li.then_inc(s_t, 1)
                    cnt["t"] += 1
                TE(qgt)
                ffn_t[("qgfree",)] = cnt["t"]
                tq = cnt["t"]

                def qgev(e, sub=sub, tq=tq):
                    e.wait_ge(s_t, tq)
                    e.tensor_copy(out=qgT[:, 2048 * sub:2048 * sub + 1024],
                                  in_=tpb)
                    e.tensor_copy(out=qgT[:, 2048 * sub + 1024:2048 * sub + 2048],
                                  in_=tpb2).then_inc(s_v, 1)
                    cnt["v"] += 1
                VE(qgev)
                ffn_v[("qgTev",)] = cnt["v"]
            v_qgT = cnt["v"]

            for dg in range(2):
                def f2mm(e, tt=tt, dg=dg, v_qgT=v_qgT):
                    e.wait_ge(s_v, v_qgT)
                    e.wait_ge(s_v, v_w2_done)
                    if ("f2ev", dg) in ffn_v:
                        e.wait_ge(s_v, ffn_v[("f2ev", dg)])
                    bank = f2b[:, 512 * dg:512 * dg + 512]
                    for c32 in range(32):
                        sl = qgT[:, 128 * c32:128 * c32 + 128]
                        e.ldweights(sl)
                        li = e.matmul(out=bank, lhsT=sl,
                                      rhs=w2q[:, C * c32 + 512 * dg:
                                              C * c32 + 512 * dg + 512],
                                      start=(c32 == 0), stop=(c32 == 31))
                    li.then_inc(s_t, 1)
                    cnt["t"] += 1
                TE(f2mm)
                t_f2 = cnt["t"]

                def f2ev(e, tt=tt, ti=ti, dg=dg, t_f2=t_f2):
                    e.wait_ge(s_t, t_f2)
                    e.tensor_copy(out=h2raw[:, C * ti + 512 * dg:
                                            C * ti + 512 * dg + 512],
                                  in_=f2b[:, 512 * dg:512 * dg + 512]
                                  ).then_inc(s_v, 1)
                    cnt["v"] += 1
                VE(f2ev)
                ffn_v[("f2ev", dg)] = cnt["v"]
        v_h2 = cnt["v"]

        def ep_s1(e, v_h2=v_h2):
            e.wait_ge(s_v, v_h2)
            e.activation(out=stat[:, 8:12], in_=stat[:, 0:4], func=AF.Sqrt,
                         bias=np.float32(0.0)).then_inc(s_s, 1)
            cnt["s"] += 1
        SE(ep_s1)
        s_e1 = cnt["s"]

        def ep_v1(e, s_e1=s_e1):
            e.wait_ge(s_s, s_e1)
            e.reciprocal(out=stat[:, 8:12], in_=stat[:, 8:12])
            VF(e)
            e.tensor_tensor(out=stat[:, 8:12], in0=stat[:, 8:12],
                            in1=stat[:, 4:8], op=ALU.mult)
            VF(e)
            e.tensor_scalar(out=stat[:, 8:12], in0=stat[:, 8:12],
                            scalar1=1e-5, scalar2=wsc2[:, 3:4],
                            op0=ALU.max, op1=ALU.mult)
            VF(e).then_inc(s_v, 1)
            cnt["v"] += 1
        VE(ep_v1)
        v_sig = cnt["v"]

        for ti in range(4):
            tt = 4 * grp4 + ti

            def ep_z(e, tt=tt, ti=ti, v_sig=v_sig):
                e.wait_ge(s_v, v_sig)
                e.scalar_tensor_tensor(out=tscr,
                                       in0=h2raw[:, C * ti:C * ti + C],
                                       scalar=stat[:, 8 + ti:9 + ti],
                                       in1=b2bc, op0=ALU.mult, op1=ALU.add)
                e.tensor_tensor(out=sbf[:, C * tt:C * tt + C], in0=tscr,
                                in1=sbf[:, C * tt:C * tt + C], op=ALU.add)
                sl = sbf[:, C * tt:C * tt + C]
                e.reduce_sum(out=stat[:, 16 + ti:17 + ti], in_=sl, axis=AX.X)
                e.scalar_tensor_tensor(out=pt[1][:, 0:C], in0=sl, scalar=1.0,
                                       in1=sl, op0=ALU.mult, op1=ALU.mult,
                                       accum_out=stat[:, 20 + ti:21 + ti]
                                       ).then_inc(s_v, 1)
                cnt["v"] += 1
            VE(ep_z)
        v_z4 = cnt["v"]

        def ep_v2(e, v_z4=v_z4):
            e.wait_ge(s_v, v_z4)
            e.tensor_scalar(out=stat[:, 16:20], in0=stat[:, 16:20],
                            scalar1=float(np.float32(-1.0 / C)), scalar2=None,
                            op0=ALU.mult)
            VF(e, 1)
            e.tensor_scalar(out=stat[:, 20:24], in0=stat[:, 20:24],
                            scalar1=float(np.float32(1.0 / C)), scalar2=None, op0=ALU.mult)
            VF(e, 1)
            e.tensor_tensor(out=stat[:, 28:32], in0=stat[:, 16:20],
                            in1=stat[:, 16:20], op=ALU.mult)
            VF(e)
            e.tensor_tensor(out=stat[:, 20:24], in0=stat[:, 20:24],
                            in1=stat[:, 28:32], op=ALU.subtract)
            cnt["v"] += 1
            VF(e, 1).then_inc(s_v, 1)
        VE(ep_v2)
        v_l2 = cnt["v"]

        def ep_s2(e, v_l2=v_l2):
            e.wait_ge(s_v, v_l2)
            e.activation(out=stat[:, 24:28], in_=stat[:, 20:24], func=AF.Sqrt,
                         bias=np.float32(1e-5)).then_inc(s_s, 1)
            cnt["s"] += 1
        SE(ep_s2)
        s_l2 = cnt["s"]

        def ep_v3(e, s_l2=s_l2):
            e.wait_ge(s_s, s_l2)
            e.reciprocal(out=stat[:, 24:28], in_=stat[:, 24:28])
            VF(e)
            e.tensor_tensor(out=stat[:, 28:32], in0=stat[:, 16:20],
                            in1=stat[:, 24:28], op=ALU.mult)
            VF(e).then_inc(s_v, 1)
            cnt["v"] += 1
        VE(ep_v3)
        v_l2r = cnt["v"]

        for ti in range(4):
            tt = 4 * grp4 + ti

            def ep_out(e, tt=tt, ti=ti, v_l2r=v_l2r):
                e.wait_ge(s_v, v_l2r)
                if len(out_sent) >= 2:
                    e.wait_ge(s_out, out_sent[-2])
                e.activation(out=ostg[tt % 2], in_=sbf[:, C * tt:C * tt + C],
                             func=AF.Identity, bias=stat[:, 28 + ti:29 + ti],
                             scale=stat[:, 24 + ti:25 + ti]).then_inc(s_s, 1)
                cnt["s"] += 1
            SE(ep_out)
            s_o = cnt["s"]

            def ep_dma(e, tt=tt, s_o=s_o):
                e.wait_ge(s_s, s_o)
                e.dma_start(out=bass.AP(out_d.tensor, 128 * tt * C,
                                        [[C, 128], [1, C]]),
                            in_=ostg[tt % 2]).then_inc(s_out, 16)
                cnt["out"] += 16
            GP(ep_dma)
            out_sent.append(cnt["out"])

    if DEBUG_DUMP:
        dump_chunks = [
            [(0, 4, wsc2), (8, 16, wsc[0:1, 0:16]), (32, 128, stat)],
            [(0, 1024, w1q[:, 0:1024])],
            [(0, 1024, w1q[:, 4096 * 7 + 2048:4096 * 7 + 3072])],
            [(0, 1024, w2q[:, 0:1024])],
            [(0, 1024, sbf[:, 0:1024])],
            [(0, 1024, sbf[:, 15 * 1024:16 * 1024])],
            [(0, 1024, h2raw[:, 0:1024])],
            [(0, 1024, h2raw[:, 3 * 1024:4 * 1024])],
            [(0, 1024, hbuf[:, 0:1024])],
            [(0, 1024, hbuf[:, 2048:3072])],
            [(0, 1024, qg[:, 0:1024])],
            [(0, 1024, qgT[:, 0:1024])],
            [(0, 1024, xq)],
            [(0, 1024, xqT)],
            [(0, 1024, b2bc)],
            [(0, 1024, tscr)],
        ]

        dump_base = cnt["out"]
        dump_v = []

        def dump_cp(e):
            for sem, key in [(s_ld, "ld"), (s_gd, "gd"), (s_t, "t"),
                             (s_s, "s"), (s_cc, "cc"), (s_out, "out")]:
                e.wait_ge(sem, cnt[key])
            for i, grp in enumerate(dump_chunks):
                bb = i % 2
                if i >= 2:
                    e.wait_ge(s_out, dump_base + 16 * (i - 1))
                e.memset(ostg[bb], 0.0)
                li = None
                for lo, w, src in grp:
                    pp = 1 if (lo == 8) else 128
                    li = e.tensor_copy(out=ostg[bb][0:pp, lo:lo + w], in_=src)
                li.then_inc(s_v, 1)
                cnt["v"] += 1
                dump_v.append(cnt["v"])
        VE(dump_cp)

        def dump_dma(e):
            for i in range(len(dump_chunks)):
                e.wait_ge(s_v, dump_v[i])
                e.dma_start(out=bass.AP(dbg_d.tensor, 128 * i * C,
                                        [[C, 128], [1, C]]),
                            in_=ostg[i % 2]).then_inc(s_out, 16)
                cnt["out"] += 16
        GP(dump_dma)

    def fin(e):
        e.wait_ge(s_out, cnt["out"])
    GP(fin)

    for c in reversed(semctx):
        c.__exit__(None, None, None)
    ctx.__exit__(None, None, None)

    f = nc.m.functions[0]
    order = {"main": 0, "SP": 1, "PE": 2, "Pool": 3, "DVE": 4, "Activation": 5}

    def bgrp(b):
        nm = b.name
        if nm == "main":
            return 0
        if nm.startswith("g_exch_if_"):
            return order["Pool"]
        parts = nm.split("_")
        if nm.startswith("block_") and len(parts) > 2 and parts[2] in order:
            return order[parts[2]]
        return 9
    f.blocks = sorted(f.blocks, key=bgrp)

    if upto is None:
        _CACHE["nc"] = nc
    return nc


def kernel(L0_attn_x, L0_attn_w_qkv, L0_attn_b_qkv, L0_attn_w_proj,
           L0_attn_b_proj, L0_attn_pos_independent, L0_ffn_w1, L0_ffn_b1,
           L0_ffn_w2, L0_ffn_b2):
    phi, psi, tri, eyef, eyeb = _host_tables()
    x = np.ascontiguousarray(np.asarray(L0_attn_x, dtype=np.float32))
    wqkv = np.asarray(L0_attn_w_qkv, dtype=np.float32)
    bqkv = np.asarray(L0_attn_b_qkv, dtype=np.float32)
    wproj = np.asarray(L0_attn_w_proj, dtype=np.float32)
    bproj = np.ascontiguousarray(np.asarray(L0_attn_b_proj, dtype=np.float32))
    pos_ind = np.ascontiguousarray(np.asarray(L0_attn_pos_independent,
                                              dtype=np.float32))
    w1 = np.asarray(L0_ffn_w1, dtype=np.float32)
    b1 = np.ascontiguousarray(np.asarray(L0_ffn_b1, dtype=np.float32))
    w2 = np.asarray(L0_ffn_w2, dtype=np.float32)
    b2 = np.ascontiguousarray(np.asarray(L0_ffn_b2, dtype=np.float32))

    zeros_tab = np.zeros_like(phi)
    zslab = np.zeros((8, 64), np.float32)
    pslab = np.ascontiguousarray(pos_ind.reshape(8, 64))
    nc = build_nc()

    in_maps = []
    for c in range(8):
        b, p = c // 2, c % 2
        perm = np.concatenate([np.arange(512) + 512 * p,
                               np.arange(512) + 512 * (1 - p)])
        im = dict(
            x_full=x[b],
            x_own=np.ascontiguousarray(x[b, 2048 * p:2048 * p + 2048]),
            wq_t=np.ascontiguousarray(wqkv[512 * p:512 * p + 512].T),
            wk_t=np.ascontiguousarray(
                wqkv[1024 + 512 * p:1024 + 512 * p + 512].T),
            wv_t=np.ascontiguousarray(
                wqkv[2048 + 512 * p:2048 + 512 * p + 512].T),
            bq_in=np.ascontiguousarray(bqkv[512 * p:512 * p + 512]),
            bk_in=np.ascontiguousarray(bqkv[1024 + 512 * p:1024 + 512 * p + 512]),
            bv_in=np.ascontiguousarray(bqkv[2048 + 512 * p:2048 + 512 * p + 512]),
            wproj_t=np.ascontiguousarray(wproj.T[perm, :]),
            bproj_in=bproj,
            w1_t=np.ascontiguousarray(w1.T),
            b1_in=b1,
            w2_t=np.ascontiguousarray(w2.T),
            b2_in=b2,
            pos_slab=(zslab if p == 0 else pslab),
            phi_in=(phi if p == 0 else zeros_tab),
            psi_in=(psi if p == 0 else zeros_tab),
            ebias_in=np.full(8, -(32.0 + MARGIN) if p == 0 else -MARGIN,
                             dtype=np.float32),
            tri_in=tri,
            eyef_in=eyef,
            eyeb_in=eyeb,
            psel=np.array([[p]], dtype=np.int32),
        )
        in_maps.append(im)

    res = run_bass_kernel_spmd(nc, in_maps, core_ids=list(range(8)))
    _LAST_RES.clear()
    _LAST_RES.extend(res.results if hasattr(res, "results") else res)
    out = np.empty((4, T, C), dtype=np.float32)
    for c in range(8):
        b, p = c // 2, c % 2
        out[b, 2048 * p:2048 * p + 2048] = np.asarray(res.results[c]["out"])
    return out


# version 05
# speedup vs baseline: 1.5029x; 1.2503x over previous
import math
import numpy as np
import ml_dtypes
import concourse.bass as bass
import concourse.mybir as mybir
from concourse.bass_utils import run_bass_kernel_spmd

F32 = mybir.dt.float32
BF16 = mybir.dt.bfloat16
I32 = mybir.dt.int32
F8 = mybir.dt.float8e4
AF = mybir.ActivationFunctionType
ALU = mybir.AluOpType
AX = mybir.AxisListType
f32 = np.float32

T = 4096
C = 1024
FF = 4096
NTT = 16
MARGIN = 10.0

_CACHE = {}
DEBUG_DUMP = False
_LAST_RES = []


def _host_tables():
    if "tabs" in _CACHE:
        return _CACHE["tabs"]
    dim = C // 2
    pos = np.arange(dim, dtype=np.float64)
    freqs = 1.0 / 10000.0 ** (pos / dim)
    active = int(np.sum(freqs * 2 * math.pi * T >= 1.0))
    active = min(active, dim - C // 4)
    assert active == 256
    fr = freqs[:active]
    th = np.arange(T, dtype=np.float64)[:, None] * fr[None, :]
    cos, sin = np.cos(th), np.sin(th)
    phi = np.zeros((8, 64, T), np.float64)
    psi = np.zeros((8, 64, T), np.float64)
    for h in range(8):
        sl = slice(32 * h, 32 * h + 32)
        phi[h, 0:32] = cos[:, sl].T
        phi[h, 32:64] = sin[:, sl].T
        psi[h, 0:32] = (cos[:, sl] - sin[:, sl]).T
        psi[h, 32:64] = (cos[:, sl] + sin[:, sl]).T
    kq = np.arange(128)
    tabs = (phi.astype(ml_dtypes.bfloat16), psi.astype(ml_dtypes.bfloat16),
            (kq[:, None] <= kq[None, :]).astype(ml_dtypes.bfloat16),
            np.eye(128, dtype=np.float32), np.eye(128, dtype=ml_dtypes.bfloat16))
    _CACHE["tabs"] = tabs
    return tabs


def _groups(ntk, w=3):
    return [list(range(i, min(i + w, ntk))) for i in range(0, ntk, w)]


class _StopEmission(Exception):
    pass


def build_nc(upto=None):
    if upto is None and "nc" in _CACHE:
        return _CACHE["nc"]
    nc = bass.Bass("TRN2", target_bir_lowering=False,
                   detect_race_conditions=False)
    dram = lambda n, s, d, k="ExternalInput": nc.dram_tensor(n, s, d, kind=k).ap()

    x_full = dram("x_full", [T, C], F32)
    x_own = dram("x_own", [2048, C], F32)
    wq_t = dram("wq_t", [C, 512], F32)
    wk_t = dram("wk_t", [C, 512], F32)
    wv_t = dram("wv_t", [C, 512], F32)
    bq_in = dram("bq_in", [512], F32)
    bk_in = dram("bk_in", [512], F32)
    bv_in = dram("bv_in", [512], F32)
    wproj_t = dram("wproj_t", [C, C], F32)
    bproj_in = dram("bproj_in", [C], F32)
    w1_t = dram("w1_t", [C, FF], F32)
    b1_in = dram("b1_in", [FF], F32)
    w2_t = dram("w2_t", [FF, C], F32)
    b2_in = dram("b2_in", [C], F32)
    pos_slab = dram("pos_slab", [8, 64], F32)
    phi_in = dram("phi_in", [8, 64, T], BF16)
    psi_in = dram("psi_in", [8, 64, T], BF16)
    ebias_in = dram("ebias_in", [8], F32)
    tri_in = dram("tri_in", [128, 128], BF16)
    eyef_in = dram("eyef_in", [128, 128], F32)
    eyeb_in = dram("eyeb_in", [128, 128], BF16)
    psel = dram("psel", [1, 1], I32)
    out_d = dram("out", [2048, C], F32, "ExternalOutput")
    if DEBUG_DUMP:
        dbg_d = dram("dbg", [2048, C], F32, "ExternalOutput")

    scr8 = nc.dram_tensor("scr8", [8], F32).ap()
    scr1 = nc.dram_tensor("scr1", [4], F32).ap()
    scr_rc = nc.dram_tensor("scr_rc", [2048], BF16).ap()
    gift = nc.dram_tensor("gift", [128, 4 * 2048], BF16).ap()
    gath = nc.dram_tensor("gath", [2, 128, 4 * 2048], BF16).ap()

    SB_BASE = 17152
    def at(name, shape, dt, off):
        nbytes = int(np.prod(shape[1:])) * mybir.dt.size(dt)
        assert SB_BASE + off + nbytes <= 229376, (name, off, nbytes)
        return nc.alloc_sbuf_tensor_at(name, shape, dt, offset=SB_BASE + off).ap()

    xT = at("xT", [128, 8 * T], BF16, 0)
    w1q = at("w1q", [128, 8 * FF], F8, 0)
    w2q = at("w2q", [128, 32 * C], F8, 32768)
    vbuf = at("vbuf", [128, 32 * 520], BF16, 65536)
    sbf = at("sbf", [128, NTT * C], BF16, 65536)
    kaug = [at(f"kaug{i}", [128, T], BF16, 98816 + 8192 * i) for i in range(3)]
    pstage = at("pstage", [128, 4 * 2048], BF16, 98816)
    hbuf = at("hbuf", [128, FF], F32, 98816)
    peerO = at("peerO", [128, 4 * 2048], BF16, 115200)
    ostg = [at(f"ostg{i}", [128, C], F32, 115200 + 4096 * i) for i in range(2)]
    qaug = [at(f"qaug{i}", [128, T], BF16, 123392 + 8192 * i) for i in range(3)]
    xres = [at(f"xres{i}", [128, C], F32, 131584 + 4096 * i) for i in range(2)]
    tscr = at("tscr", [128, C], F32, 139776)
    wi32 = at("wi32", [128, C], I32, 139776)
    bpbc = at("bpbc", [128, C], BF16, 143872)
    b2bc = at("b2bc", [128, C], BF16, 145920)
    onorm = at("onorm", [128, 4 * T], BF16, 147968)
    wvb = at("wvb", [128, FF], BF16, 147968)
    qg = at("qg", [128, FF], BF16, 147968)
    wpb = at("wpb", [128, 8 * C], BF16, 147968)
    qgT = at("qgT", [128, FF], BF16, 156160)
    xq = at("xq", [128, C], BF16, 164352)
    xqT = at("xqT", [128, C], BF16, 166400)
    b1bc = at("b1bc", [128, FF], BF16, 168448)
    h2raw = at("h2raw", [128, 4 * C], BF16, 176640)
    eyef_s = at("eyef_s", [128, 128], F32, 180736)
    pslab_s = at("pslab_s", [8, 64], F32, 181248)
    csum = at("csum", [8, 1], F32, 181504)
    ebrow = at("ebrow", [8, 1], F32, 181536)
    ebt = at("ebt", [8, 1], F32, 181568)
    wstage = at("wstage", [128, 2048], F32, 186880)
    wstgb = at("wstgb", [128, 2048], BF16, 195072)
    pt = [at("pt0", [128, 1536], BF16, 199424),
          at("pt1", [128, 1536], BF16, 202496)]
    rcbc = at("rcbc", [128, 2048], BF16, 205568)
    tri_s = at("tri_s", [128, 128], BF16, 209664)
    eyeb_s = at("eyeb_s", [128, 128], BF16, 209920)
    ebias_bc = at("ebias_bc", [128, 8], F32, 210176)
    bq8 = at("bq8", [128, 4], F32, 210208)
    bk_s = at("bk_s", [128, 4], F32, 210240)
    pb_s = at("pb_s", [1, 1], I32, 210272)
    bvbc = at("bvbc", [128, 512], BF16, 210304)
    stat = at("stat", [128, 128], F32, 211328)
    ones64 = at("ones64", [128, 64], F32, 211840)
    wsc = at("wsc", [1, 16], F32, 212096)
    wsc2 = at("wsc2", [128, 4], F32, 212160)
    vfill = at("vfill", [128, 64], F32, 181632)

    psum_ctx = [nc.psum_tensor("sh0", [128, 1536], F32),
                nc.psum_tensor("sh1", [128, 1536], F32),
                nc.psum_tensor("otb", [128, 512], F32),
                nc.psum_tensor("aux", [128, 512], F32)]
    sh0, sh1, otb, aux = [c.__enter__().ap() for c in psum_ctx]
    SHP = [sh0, sh1]

    ctx = nc.Block()
    block = ctx.__enter__()
    semctx = [nc.semaphore(n) for n in
              ["s_ld", "s_gd", "s_t", "s_v", "s_s", "s_cc", "s_out"]]
    s_ld, s_gd, s_t, s_v, s_s, s_cc, s_out = [c.__enter__() for c in semctx]
    cnt = dict(ld=0, gd=0, t=0, v=0, s=0, cc=0, out=0)

    stopped = [False]

    def VF(e, n=2):
        li = None
        for _ in range(n):
            li = e.tensor_scalar(out=vfill, in0=vfill, scalar1=1.0,
                                 scalar2=None, op0=ALU.mult)
        return li

    def SY(fn):
        if stopped[0]:
            return
        @block.sync
        def _(e):
            fn(e)

    def GP(fn):
        if stopped[0]:
            return
        @block.gpsimd
        def _(e):
            fn(e)

    def TE(fn):
        if stopped[0]:
            return
        @block.tensor
        def _(e):
            fn(e)

    def VE(fn):
        if stopped[0]:
            return
        @block.vector
        def _(e):
            fn(e)

    def SE(fn):
        if stopped[0]:
            return
        @block.scalar
        def _(e):
            fn(e)

    def phase(name):
        if stopped[0] or upto is None or name != upto:
            return
        def drain(e):
            for sem, key in [(s_ld, "ld"), (s_gd, "gd"), (s_t, "t"),
                             (s_v, "v"), (s_s, "s"), (s_cc, "cc")]:
                e.wait_ge(sem, cnt[key])
            e.memset(ostg[0], 0.0)
            e.dma_start(out=bass.AP(out_d.tensor, 0, [[C, 128], [1, C]]),
                        in_=ostg[0]).then_inc(s_out, 16)
            cnt["out"] += 16
            e.wait_ge(s_out, cnt["out"])
        GP(drain)
        stopped[0] = True

    def p0_sync(e):
        for dst, src in [(pb_s, psel), (tri_s, tri_in), (eyef_s, eyef_in),
                         (eyeb_s, eyeb_in), (pslab_s, pos_slab)]:
            e.dma_start(out=dst, in_=src).then_inc(s_ld, 16)
            cnt["ld"] += 16
        for dst, src in [(bq8, bq_in), (bk_s, bk_in)]:
            e.dma_start(out=dst,
                        in_=bass.AP(src.tensor, 0, [[1, 128], [128, 4]]),
                        allow_slow_non_contiguous=True).then_inc(s_ld, 16)
            cnt["ld"] += 16
        e.dma_start(out=ebrow,
                    in_=bass.AP(ebias_in.tensor, 0, [[1, 8], [1, 1]])
                    ).then_inc(s_ld, 16)
        cnt["ld"] += 16
        e.dma_start(out=wstage[:, 0:512],
                    in_=bass.AP(bv_in.tensor, 0, [[0, 128], [1, 512]])
                    ).then_inc(s_ld, 16)
        cnt["ld"] += 16
    SY(p0_sync)
    ld_p0 = cnt["ld"]

    def p0_vec(e):
        e.wait_ge(s_ld, ld_p0)
        e.reduce_sum(out=csum, in_=pslab_s, axis=AX.X)
        e.memset(bass.AP(vbuf.tensor, 64,
                         [[520 * 32, 128], [520, 32], [65, 8], [1, 1]]), 1.0)
        e.memset(ones64, 1.0)
        e.tensor_tensor(out=ebt, in0=ebrow, in1=csum, op=ALU.add)
        e.memset(stat, 0.0)
        e.memset(wsc, 0.0)
        e.tensor_tensor(out=ebrow, in0=ebt, in1=csum, op=ALU.subtract)
        e.memset(vfill, 0.0)
        e.tensor_scalar(out=bq8, in0=bq8, scalar1=0.125, scalar2=None,
                        op0=ALU.mult)
        e.tensor_copy(out=bvbc, in_=wstage[:, 0:512]).then_inc(s_v, 1)
        cnt["v"] += 1
    VE(p0_vec)
    v_p0 = cnt["v"]

    def p0_gp(e):
        e.wait_ge(s_v, v_p0)
        e.dma_start(out=scr8,
                    in_=bass.AP(ebrow.tensor, 0, [[1, 8], [1, 1]])
                    ).then_inc(s_gd, 16)
        cnt["gd"] += 16
        e.wait_ge(s_gd, cnt["gd"])
        for h in range(8):
            e.dma_start(out=ebias_bc[:, h:h + 1],
                        in_=bass.AP(scr8.tensor, h, [[0, 128], [1, 1]])
                        ).then_inc(s_gd, 16)
            cnt["gd"] += 16
    GP(p0_gp)
    phase("p0")

    xck_evac = [0] * 16
    for ck in range(16):
        def x_ld(e, ck=ck):
            if ck == 0:
                e.wait_ge(s_v, v_p0)
            else:
                e.wait_ge(s_v, xck_evac[ck - 1])
            e.dma_start(out=wstage,
                        in_=bass.AP(x_full.tensor, (256 * ck) * C,
                                    [[C, 128], [128 * C, 2], [1, C]])
                        ).then_inc(s_ld, 16)
            cnt["ld"] += 16
        SY(x_ld)
        ldk = cnt["ld"]

        def x_tp(e, ck=ck, ldk=ldk):
            e.wait_ge(s_ld, ldk)
            if ck >= 1:
                e.wait_ge(s_v, xck_evac[ck - 1])
            li = None
            for a in range(2):
                bank = sh0 if a == 0 else sh1
                for c8 in range(8):
                    li = e.transpose(out=bass.AP(bank.tensor, 128 * c8,
                                                 [[1536, 128], [1, 128]]),
                                     in_=wstage[:, 1024 * a + 128 * c8:
                                                1024 * a + 128 * c8 + 128],
                                     identity=eyef_s)
            li.then_inc(s_t, 1)
            cnt["t"] += 1
        TE(x_tp)
        t_tp = cnt["t"]

        def x_ev(e, ck=ck, t_tp=t_tp):
            e.wait_ge(s_t, t_tp)
            li = None
            for a in range(2):
                bank = sh0 if a == 0 else sh1
                for c8 in range(8):
                    li = e.tensor_copy(
                        out=xT[:, T * c8 + 256 * ck + 128 * a:
                               T * c8 + 256 * ck + 128 * a + 128],
                        in_=bank[:, 128 * c8:128 * c8 + 128])
            li.then_inc(s_v, 1)
            cnt["v"] += 1
        VE(x_ev)
        xck_evac[ck] = cnt["v"]
    phase("xT")

    wv_cast_v = [0, 0]
    for half in range(2):
        def wv_ld(e, half=half):
            e.wait_ge(s_v, xck_evac[15] if half == 0 else wv_cast_v[0])
            e.dma_start(out=wstage,
                        in_=bass.AP(wv_t.tensor, 256 * half,
                                    [[512, 128], [512 * 128, 8], [1, 256]])
                        ).then_inc(s_ld, 16)
            cnt["ld"] += 16
        SY(wv_ld)
        ldw = cnt["ld"]

        def wv_c(e, half=half, ldw=ldw):
            e.wait_ge(s_ld, ldw)
            e.tensor_copy(out=bass.AP(wvb.tensor, 256 * half,
                                      [[FF, 128], [512, 8], [1, 256]]),
                          in_=wstage).then_inc(s_v, 1)
            cnt["v"] += 1
        VE(wv_c)
        wv_cast_v[half] = cnt["v"]
    v_wvb = cnt["v"]

    vm_evac = [0] * 32
    for tt in range(32):
        def vmm(e, tt=tt):
            e.wait_ge(s_v, v_wvb)
            if tt >= 2:
                e.wait_ge(s_v, vm_evac[tt - 2])
            bank = otb if tt % 2 == 0 else aux
            for c8 in range(8):
                sl = xT[:, T * c8 + 128 * tt: T * c8 + 128 * tt + 128]
                e.ldweights(sl)
                li = e.matmul(out=bank, lhsT=sl,
                              rhs=wvb[:, 512 * c8:512 * c8 + 512],
                              start=(c8 == 0), stop=(c8 == 7))
            li.then_inc(s_t, 1)
            cnt["t"] += 1
        TE(vmm)
        t_vm = cnt["t"]

        def vev(e, tt=tt, t_vm=t_vm):
            e.wait_ge(s_t, t_vm)
            bank = otb if tt % 2 == 0 else aux
            e.scalar_tensor_tensor(
                out=bass.AP(vbuf.tensor, 520 * tt,
                            [[520 * 32, 128], [65, 8], [1, 64]]),
                in0=bass.AP(bank.tensor, 0, [[512, 128], [64, 8], [1, 64]]),
                scalar=1.0,
                in1=bass.AP(bvbc.tensor, 0, [[512, 128], [64, 8], [1, 64]]),
                op0=ALU.mult, op1=ALU.add).then_inc(s_v, 1)
            cnt["v"] += 1
        VE(vev)
        vm_evac[tt] = cnt["v"]
    phase("vmm")

    kq_ready = [0] * 8
    kq_aug_ld = [0] * 8
    exp_cp = {}
    mask_cp = {}
    pv_cp = {}
    sc_cp = {}
    head_last_sc = [0] * 8
    head_norm_v = [0] * 8
    stage_free = {"s": 0, "v": 0}

    def emit_pair_fill(hp, guard_t):
        b0, b1 = (2 * hp) % 3, (2 * hp + 1) % 3

        def w_ld(e, hp=hp):
            if hp == 0:
                e.wait_ge(s_v, v_wvb)
            else:
                e.wait_ge(s_v, stage_free["v"])
            e.dma_start(out=wstage[:, 0:1024],
                        in_=bass.AP(wk_t.tensor, 128 * hp,
                                    [[512, 128], [512 * 128, 8], [1, 128]])
                        ).then_inc(s_ld, 16)
            e.dma_start(out=wstage[:, 1024:2048],
                        in_=bass.AP(wq_t.tensor, 128 * hp,
                                    [[512, 128], [512 * 128, 8], [1, 128]])
                        ).then_inc(s_ld, 16)
            cnt["ld"] += 32
        SY(w_ld)
        ld_w = cnt["ld"]

        def w_cast(e, ld_w=ld_w):
            e.wait_ge(s_ld, ld_w)
            e.tensor_copy(out=wstgb, in_=wstage).then_inc(s_v, 1)
            cnt["v"] += 1
        VE(w_cast)
        v_cast = cnt["v"]

        def aug_ld(e, hp=hp, b0=b0, b1=b1, guard_t=guard_t):
            e.wait_ge(s_t, guard_t)
            for h_i, (kb, qb) in enumerate([(kaug[b0], qaug[b0]),
                                            (kaug[b1], qaug[b1])]):
                h = 2 * hp + h_i
                ab = 64 if h % 2 == 0 else 0
                e.dma_start(out=kb[ab:ab + 64, :], in_=psi_in[h, :, :]
                            ).then_inc(s_ld, 16)
                e.dma_start(out=qb[ab:ab + 64, :], in_=phi_in[h, :, :]
                            ).then_inc(s_ld, 16)
                cnt["ld"] += 32
        SY(aug_ld)
        ld_aug = cnt["ld"]

        evk = []
        for which in range(2):
            wbase = 1024 * which
            for st in range(8):
                def kqmm(e, which=which, wbase=wbase, st=st, v_cast=v_cast,
                         guard_t=guard_t):
                    e.wait_ge(s_v, v_cast)
                    e.wait_ge(s_t, guard_t)
                    if evk:
                        e.wait_ge(s_s, evk[-1])
                    for c8 in range(8):
                        sl = wstgb[:, wbase + 128 * c8: wbase + 128 * c8 + 128]
                        e.ldweights(sl)
                        li = e.matmul(out=aux, lhsT=sl,
                                      rhs=xT[:, T * c8 + 512 * st:
                                             T * c8 + 512 * st + 512],
                                      start=(c8 == 0), stop=(c8 == 7))
                    li.then_inc(s_t, 1)
                    cnt["t"] += 1
                TE(kqmm)
                t_mm = cnt["t"]

                def kqev(e, which=which, st=st, t_mm=t_mm, hp=hp, b0=b0, b1=b1):
                    e.wait_ge(s_t, t_mm)
                    for h_i in range(2):
                        h = 2 * hp + h_i
                        dst = (kaug if which == 0 else qaug)[b0 if h_i == 0 else b1]
                        db = 0 if h % 2 == 0 else 64
                        src = aux[64 * h_i:64 * h_i + 64, :]
                        if which == 0:
                            li = e.activation(
                                out=dst[db:db + 64, 512 * st:512 * st + 512],
                                in_=src, func=AF.Identity,
                                bias=bk_s[64 * h_i:64 * h_i + 64, hp:hp + 1])
                        else:
                            li = e.activation(
                                out=dst[db:db + 64, 512 * st:512 * st + 512],
                                in_=src, func=AF.Identity,
                                bias=bq8[64 * h_i:64 * h_i + 64, hp:hp + 1],
                                scale=np.float32(0.125))
                    li.then_inc(s_s, 1)
                    cnt["s"] += 1
                SE(kqev)
                evk.append(cnt["s"])
        stage_free["s"] = cnt["s"]
        stage_free["v"] = v_cast
        for h_i in range(2):
            kq_ready[2 * hp + h_i] = cnt["s"]
            kq_aug_ld[2 * hp + h_i] = ld_aug

    def emit_head(h):
        Ka, Qa = kaug[h % 3], qaug[h % 3]
        items = []
        for st in range(8):
            ntk = 4 * st + 4
            grps = _groups(ntk)
            for g, kjs in enumerate(grps):
                items.append((st, g, kjs, ntk, len(grps)))

        def emit_pv(pend):
            st, g, kjs, ntk, glen, par = pend

            def t_pv(e, h=h, st=st, g=g, kjs=kjs, par=par, ntk=ntk):
                e.wait_ge(s_v, mask_cp[(h, st, g)])
                if g == 0 and ("otfree",) in pv_cp:
                    e.wait_ge(s_v, pv_cp[("otfree",)])
                for i, kj in enumerate(kjs):
                    c0 = max(0, kj - 4 * st) * 128
                    vsl = vbuf[:, 520 * kj + 65 * h: 520 * kj + 65 * h + 65]
                    e.ldweights(vsl)
                    li = e.matmul(out=otb[0:65, c0:512], lhsT=vsl,
                                  rhs=pt[par][:, 512 * i + c0:512 * i + 512],
                                  start=(kj == 0), stop=(kj == ntk - 1))
                li.then_inc(s_t, 1)
                cnt["t"] += 1
            TE(t_pv)
            pv_cp[(h, st, g)] = cnt["t"]
            pv_cp[("ptfree", par)] = cnt["t"]

            if g != glen - 1:
                return

            def v_strip(e, h=h, st=st, ngr=glen):
                e.wait_ge(s_t, pv_cp[(h, st, ngr - 1)])
                lo = 64 * (h % 2)
                e.tensor_copy(out=onorm[lo:lo + 64,
                                        T * (h // 2) + 512 * st:
                                        T * (h // 2) + 512 * st + 512],
                              in_=otb[0:64, :])
                e.tensor_copy(out=rcbc[0:1, 512 * (st % 4):512 * (st % 4) + 512],
                              in_=otb[64:65, :]).then_inc(s_v, 1)
                cnt["v"] += 1
            VE(v_strip)
            pv_cp[("otfree",)] = cnt["v"]

            if st in (3, 7):
                hh = (st - 3) // 4
                v_sums = cnt["v"]

                def g_bc(e, v_sums=v_sums):
                    e.wait_ge(s_v, v_sums)
                    e.dma_start(out=scr_rc, in_=rcbc[0:1, :]).then_inc(s_gd, 16)
                    cnt["gd"] += 16
                    e.wait_ge(s_gd, cnt["gd"])
                    e.dma_start(out=rcbc,
                                in_=bass.AP(scr_rc.tensor, 0,
                                            [[0, 128], [1, 2048]])
                                ).then_inc(s_gd, 16)
                    cnt["gd"] += 16
                GP(g_bc)
                gd_rc = cnt["gd"]

                def v_norm(e, h=h, hh=hh, gd_rc=gd_rc):
                    e.wait_ge(s_gd, gd_rc)
                    with nc.allow_low_precision(reason="bf16 softmax sums"):
                        e.reciprocal(out=rcbc, in_=rcbc)
                    lo = 64 * (h % 2)
                    sl = slice(T * (h // 2) + 2048 * hh,
                               T * (h // 2) + 2048 * hh + 2048)
                    e.tensor_tensor(out=onorm[lo:lo + 64, sl],
                                    in0=onorm[lo:lo + 64, sl],
                                    in1=rcbc[lo:lo + 64, :],
                                    op=ALU.mult).then_inc(s_v, 1)
                    cnt["v"] += 1
                VE(v_norm)

        pend = None
        for j, (st, g, kjs, ntk, glen) in enumerate(items):
            par = j % 2

            def t_sc(e, h=h, st=st, g=g, kjs=kjs, par=par):
                e.wait_ge(s_s, kq_ready[h])
                e.wait_ge(s_ld, kq_aug_ld[h])
                key = ("shfree", par)
                if key in exp_cp:
                    e.wait_ge(s_s, exp_cp[key])
                for i, kj in enumerate(kjs):
                    sl = Ka[:, 128 * kj:128 * kj + 128]
                    e.ldweights(sl)
                    li = e.matmul(out=SHP[par][:, 512 * i:512 * i + 512],
                                  lhsT=sl,
                                  rhs=Qa[:, 512 * st:512 * st + 512],
                                  start=True, stop=True)
                li.then_inc(s_t, 1)
                cnt["t"] += 1
            TE(t_sc)
            sc_cp[(h, st, g)] = cnt["t"]

            if pend is not None:
                emit_pv(pend)

            def s_exp(e, h=h, st=st, g=g, kjs=kjs, par=par):
                e.wait_ge(s_t, sc_cp[(h, st, g)])
                key = ("ptfree", par)
                if key in pv_cp:
                    e.wait_ge(s_t, pv_cp[key])
                w = 512 * len(kjs)
                e.activation(out=pt[par][:, 0:w], in_=SHP[par][:, 0:w],
                             func=AF.Exp,
                             bias=ebias_bc[:, h:h + 1]).then_inc(s_s, 1)
                cnt["s"] += 1
            SE(s_exp)
            exp_cp[(h, st, g)] = cnt["s"]
            exp_cp[("shfree", par)] = cnt["s"]

            def v_mask(e, h=h, st=st, g=g, kjs=kjs, par=par):
                e.wait_ge(s_s, exp_cp[(h, st, g)])
                li = None
                for i, kj in enumerate(kjs):
                    d = kj - 4 * st
                    if 0 <= d <= 3:
                        sl = slice(512 * i + 128 * d, 512 * i + 128 * d + 128)
                        li = e.tensor_tensor(out=pt[par][:, sl],
                                             in0=pt[par][:, sl],
                                             in1=tri_s, op=ALU.mult)
                if li is not None:
                    li.then_inc(s_v, 1)
                else:
                    e.nop().then_inc(s_v, 1)
                cnt["v"] += 1
            VE(v_mask)
            mask_cp[(h, st, g)] = cnt["v"]
            pend = (st, g, kjs, ntk, glen, par)
        emit_pv(pend)
        head_last_sc[h] = sc_cp[(h, 7, len(_groups(32)) - 1)]
        head_norm_v[h] = cnt["v"]

    def emit_wquant(w_in, wrows, wdst, wcols, m_col, race_t):
        nchunks = wrows // 128
        spl = 2 if wcols >= 2048 else 1
        tw = wcols // spl
        tiles = [(c, j) for c in range(nchunks) for j in range(spl)]
        absv = []
        for idx, (c, j) in enumerate(tiles):
            def wl(e, c=c, j=j, idx=idx):
                if idx == 0:
                    e.wait_ge(s_v, stage_free["v"])
                else:
                    e.wait_ge(s_v, absv[-1])
                e.dma_start(out=wstage[:, 0:tw],
                            in_=bass.AP(w_in.tensor, (128 * c) * wcols + tw * j,
                                        [[wcols, 128], [1, tw]])
                            ).then_inc(s_ld, 16)
                cnt["ld"] += 16
            SY(wl)
            ldw = cnt["ld"]

            def wa(e, idx=idx, ldw=ldw):
                e.wait_ge(s_ld, ldw)
                e.reduce_sum(out=stat[:, 64 + idx:65 + idx],
                             in_=wstage[:, 0:tw], axis=AX.X,
                             apply_absolute_value=True).then_inc(s_v, 1)
                cnt["v"] += 1
            VE(wa)
            absv.append(cnt["v"])
        ntile = len(tiles)

        def w_zero(e, ntile=ntile):
            e.wait_ge(s_v, absv[-1])
            if ntile < 64:
                e.memset(stat[:, 64 + ntile:128], 0.0).then_inc(s_v, 1)
            cnt["v"] += 1
        VE(w_zero)
        v_z = cnt["v"]

        def w_mm(e, v_z=v_z):
            e.wait_ge(s_v, v_z)
            e.wait_ge(s_s, stage_free["s"])
            e.matmul(out=aux[0:64, 0:64], lhsT=ones64, rhs=stat[:, 64:128],
                     start=True, stop=True).then_inc(s_t, 1)
            cnt["t"] += 1
        TE(w_mm)
        t_mm = cnt["t"]

        def w_chain(e, t_mm=t_mm, m_col=m_col):
            e.wait_ge(s_t, t_mm)
            e.reduce_sum(out=wsc[0:1, m_col:m_col + 1], in_=aux[0:1, 0:64],
                         axis=AX.X)
            VF(e)
            e.tensor_scalar(out=wsc[0:1, m_col:m_col + 1],
                            in0=wsc[0:1, m_col:m_col + 1],
                            scalar1=float(np.float32(1.0 / (wrows * wcols))), scalar2=1e-5,
                            op0=ALU.mult, op1=ALU.max)
            VF(e)
            e.reciprocal(out=wsc[0:1, m_col + 1:m_col + 2],
                         in_=wsc[0:1, m_col:m_col + 1]).then_inc(s_v, 1)
            cnt["v"] += 1
        VE(w_chain)
        v_ch = cnt["v"]

        def g_wbc(e, v_ch=v_ch, m_col=m_col):
            e.wait_ge(s_v, v_ch)
            e.dma_start(out=scr1, in_=wsc[0:1, m_col:m_col + 4]
                        ).then_inc(s_gd, 16)
            cnt["gd"] += 16
            e.wait_ge(s_gd, cnt["gd"])
            for k in range(2):
                e.dma_start(out=wsc2[:, m_col + k:m_col + k + 1],
                            in_=bass.AP(scr1.tensor, k, [[0, 128], [1, 1]])
                            ).then_inc(s_gd, 16)
                cnt["gd"] += 16
        GP(g_wbc)
        gd_bc = cnt["gd"]

        rl = []
        for idx, (c, j) in enumerate(tiles):
            def wl2(e, c=c, j=j, idx=idx):
                if rl:
                    e.wait_ge(s_v, rl[-1])
                else:
                    e.wait_ge(s_gd, gd_bc)
                e.dma_start(out=wstage[:, 0:tw],
                            in_=bass.AP(w_in.tensor, (128 * c) * wcols + tw * j,
                                        [[wcols, 128], [1, tw]])
                            ).then_inc(s_ld, 16)
                cnt["ld"] += 16
            SY(wl2)
            ldw = cnt["ld"]

            def wr(e, c=c, j=j, ldw=ldw, m_col=m_col, idx=idx):
                e.wait_ge(s_ld, ldw)
                if idx == 0:
                    e.wait_ge(s_gd, gd_bc)
                    e.wait_ge(s_t, race_t)
                for q in range(tw // 1024):
                    e.tensor_scalar(out=wi32,
                                    in0=wstage[:, 1024 * q:1024 * q + 1024],
                                    scalar1=wsc2[:, m_col + 1:m_col + 2],
                                    scalar2=None, op0=ALU.mult)
                    li = e.tensor_scalar(
                        out=wdst[:, wcols * c + tw * j + 1024 * q:
                                 wcols * c + tw * j + 1024 * q + 1024],
                        in0=wi32, scalar1=-1.0, scalar2=1.0,
                        op0=ALU.max, op1=ALU.min)
                li.then_inc(s_v, 1)
                cnt["v"] += 1
            VE(wr)
            rl.append(cnt["v"])
        stage_free["v"] = cnt["v"]
        return cnt["v"]

    emit_pair_fill(0, guard_t=0)
    phase("fill0")
    emit_head(0)
    phase("head0")
    emit_pair_fill(1, guard_t=head_last_sc[0])
    emit_head(1)
    emit_head(2)
    emit_pair_fill(2, guard_t=head_last_sc[2])
    emit_head(3)
    phase("head3")
    emit_head(4)
    emit_pair_fill(3, guard_t=head_last_sc[4])
    emit_head(5)
    phase("head5")
    v_w1_done = emit_wquant(w1_t, C, w1q, FF, 0, race_t=head_last_sc[5])
    phase("w1q")
    emit_head(6)
    v_w2_done = emit_wquant(w2_t, FF, w2q, C, 2, race_t=head_last_sc[6])
    emit_head(7)
    phase("attn")

    def g_exch(e):
        for h in range(8):
            e.wait_ge(s_v, head_norm_v[h])
        with e.register("rp") as rp:
            e.reg_load(rp, pb_s[0:1, 0:1])
            with e.If_eq(rp, 0):
                for u in range(4):
                    e.dma_start(out=gift[:, 2048 * u:2048 * u + 2048],
                                in_=onorm[:, T * u + 2048:T * u + 4096]
                                ).then_inc(s_gd, 16)
            with e.Else():
                for u in range(4):
                    e.dma_start(out=gift[:, 2048 * u:2048 * u + 2048],
                                in_=onorm[:, T * u:T * u + 2048]
                                ).then_inc(s_gd, 16)
            cnt["gd"] += 64
            e.wait_ge(s_gd, cnt["gd"])
            e.collective_compute(
                "AllGather", ALU.bypass,
                replica_groups=[[0, 1], [2, 3], [4, 5], [6, 7]],
                ins=[gift], outs=[gath]).then_inc(s_cc, 1)
            cnt["cc"] += 1
            e.wait_ge(s_cc, cnt["cc"])
            with e.If_eq(rp, 0):
                for u in range(4):
                    e.dma_start(out=pstage[:, 2048 * u:2048 * u + 2048],
                                in_=onorm[:, T * u:T * u + 2048]
                                ).then_inc(s_gd, 16)
                e.dma_start(out=peerO, in_=gath[1]).then_inc(s_gd, 16)
            with e.Else():
                for u in range(4):
                    e.dma_start(out=pstage[:, 2048 * u:2048 * u + 2048],
                                in_=onorm[:, T * u + 2048:T * u + 4096]
                                ).then_inc(s_gd, 16)
                e.dma_start(out=peerO, in_=gath[0]).then_inc(s_gd, 16)
            cnt["gd"] += 80
    GP(g_exch)
    gd_exch = cnt["gd"]
    phase("exch")

    def bp_ld(e):
        e.wait_ge(s_v, v_w2_done)
        e.dma_start(out=wstage[:, 0:1024],
                    in_=bass.AP(bproj_in.tensor, 0, [[0, 128], [1, C]])
                    ).then_inc(s_ld, 16)
        cnt["ld"] += 16
    SY(bp_ld)
    ld_bp = cnt["ld"]

    def bp_c(e, ld_bp=ld_bp):
        e.wait_ge(s_ld, ld_bp)
        e.tensor_scalar(out=wsc2[:, 3:4], in0=wsc2[:, 2:3],
                        scalar1=float(np.float32(1.0 / 127.0)), scalar2=None,
                        op0=ALU.mult)
        e.tensor_copy(out=bpbc, in_=wstage[:, 0:1024]).then_inc(s_v, 1)
        cnt["v"] += 1
    VE(bp_c)
    v_bp = cnt["v"]

    wp_cast = [0] * 4
    for q4 in range(4):
        def wp_ld(e, q4=q4):
            e.wait_ge(s_v, v_bp if q4 == 0 else wp_cast[q4 - 1])
            if q4 == 0:
                e.wait_ge(s_gd, gd_exch)
            e.dma_start(out=wstage,
                        in_=bass.AP(wproj_t.tensor, 256 * q4,
                                    [[C, 128], [128 * C, 8], [1, 256]])
                        ).then_inc(s_ld, 16)
            cnt["ld"] += 16
        SY(wp_ld)
        ldw = cnt["ld"]

        def wp_c(e, q4=q4, ldw=ldw):
            e.wait_ge(s_ld, ldw)
            if q4 == 0:
                e.wait_ge(s_gd, gd_exch)
            e.tensor_copy(out=bass.AP(wpb.tensor, 256 * q4,
                                      [[8 * C, 128], [C, 8], [1, 256]]),
                          in_=wstage).then_inc(s_v, 1)
            cnt["v"] += 1
        VE(wp_c)
        wp_cast[q4] = cnt["v"]
    v_wp = cnt["v"]
    phase("wp")

    proj_done_v = [0] * NTT
    for tt in range(NTT):
        def xr_ld(e, tt=tt):
            if tt >= 2:
                e.wait_ge(s_v, proj_done_v[tt - 2])
            e.dma_start(out=xres[tt % 2],
                        in_=bass.AP(x_own.tensor, 128 * tt * C,
                                    [[C, 128], [1, C]])).then_inc(s_gd, 16)
            cnt["gd"] += 16
        GP(xr_ld)
        ld_xr = cnt["gd"]

        def pmm(e, tt=tt):
            e.wait_ge(s_v, v_wp)
            e.wait_ge(s_gd, gd_exch)
            if tt >= 2:
                e.wait_ge(s_v, proj_done_v[tt - 2])
            bset = sh0 if tt % 2 == 0 else sh1
            for dgrp in range(2):
                bank = bset[:, 512 * dgrp:512 * dgrp + 512]
                for ci in range(8):
                    if ci < 4:
                        sl = pstage[:, 2048 * ci + 128 * tt:
                                    2048 * ci + 128 * tt + 128]
                    else:
                        sl = peerO[:, 2048 * (ci - 4) + 128 * tt:
                                   2048 * (ci - 4) + 128 * tt + 128]
                    e.ldweights(sl)
                    li = e.matmul(out=bank, lhsT=sl,
                                  rhs=bass.AP(wpb.tensor, C * ci + 512 * dgrp,
                                              [[8 * C, 128], [1, 512]]),
                                  start=(ci == 0), stop=(ci == 7))
            li.then_inc(s_t, 1)
            cnt["t"] += 1
        TE(pmm)
        t_p = cnt["t"]

        def pev(e, tt=tt, t_p=t_p, ld_xr=ld_xr):
            e.wait_ge(s_t, t_p)
            e.wait_ge(s_gd, ld_xr)
            bset = sh0 if tt % 2 == 0 else sh1
            for dgrp in range(2):
                e.scalar_tensor_tensor(
                    out=tscr[:, 512 * dgrp:512 * dgrp + 512],
                    in0=bset[:, 512 * dgrp:512 * dgrp + 512], scalar=1.0,
                    in1=bpbc[:, 512 * dgrp:512 * dgrp + 512],
                    op0=ALU.mult, op1=ALU.add)
            e.tensor_tensor(out=sbf[:, C * tt:C * tt + C], in0=tscr,
                            in1=xres[tt % 2], op=ALU.add).then_inc(s_v, 1)
            cnt["v"] += 1
        VE(pev)
        proj_done_v[tt] = cnt["v"]
    v_proj = cnt["v"]

    phase("proj")
    for c in reversed(psum_ctx):
        c.__exit__(None, None, None)
    psum_ctx2 = [nc.psum_tensor("f1b", [128, 1536], F32),
                 nc.psum_tensor("f2b", [128, 1536], F32),
                 nc.psum_tensor("tpb", [128, 1024], BF16),
                 nc.psum_tensor("tpb2", [128, 1024], BF16)]
    f1b, f2b, tpb, tpb2 = [c.__enter__().ap() for c in psum_ctx2]

    b_v = [0, 0, 0]
    for bi, (src, dst, width) in enumerate(
            [(b1_in, b1bc[:, 0:2048], 2048),
             (None, b1bc[:, 2048:4096], 2048),
             (b2_in, b2bc, C)]):
        def bb_ld(e, bi=bi, width=width):
            e.wait_ge(s_v, v_proj if bi == 0 else b_v[bi - 1])
            base = 2048 if bi == 1 else 0
            e.dma_start(out=wstage[:, 0:width],
                        in_=bass.AP((b1_in if bi < 2 else b2_in).tensor, base,
                                    [[0, 128], [1, width]])
                        ).then_inc(s_ld, 16)
            cnt["ld"] += 16
        SY(bb_ld)
        ldb = cnt["ld"]

        def bb_c(e, bi=bi, dst=dst, width=width, ldb=ldb):
            e.wait_ge(s_ld, ldb)
            e.tensor_copy(out=dst, in_=wstage[:, 0:width]).then_inc(s_v, 1)
            cnt["v"] += 1
        VE(bb_c)
        b_v[bi] = cnt["v"]
    v_bias = cnt["v"]

    def ln1_stats(e):
        e.wait_ge(s_v, v_bias)
        for tt in range(NTT):
            sl = sbf[:, C * tt:C * tt + C]
            e.reduce_sum(out=stat[:, 16 + tt:17 + tt], in_=sl, axis=AX.X)
            e.scalar_tensor_tensor(out=pt[0][:, 0:C], in0=sl, scalar=1.0,
                                   in1=sl, op0=ALU.mult, op1=ALU.mult,
                                   accum_out=stat[:, tt:tt + 1])
            e.reduce_max(out=stat[:, 96 + tt:97 + tt], in_=sl, axis=AX.X)
            li = e.tensor_reduce(out=stat[:, 112 + tt:113 + tt], in_=sl,
                                 axis=AX.X, op=ALU.min)
        li.then_inc(s_v, 1)
        cnt["v"] += 1
    VE(ln1_stats)
    v_st = cnt["v"]

    def ln1_c1(e, v_st=v_st):
        e.wait_ge(s_v, v_st)
        e.tensor_scalar(out=stat[:, 16:32], in0=stat[:, 16:32],
                        scalar1=float(np.float32(1.0 / C)), scalar2=None, op0=ALU.mult)
        e.tensor_scalar(out=stat[:, 0:16], in0=stat[:, 0:16],
                        scalar1=float(np.float32(1.0 / C)), scalar2=1e-5,
                        op0=ALU.mult, op1=ALU.add)
        e.tensor_tensor(out=stat[:, 112:128], in0=stat[:, 16:32],
                        in1=stat[:, 112:128], op=ALU.subtract)
        e.tensor_tensor(out=stat[:, 96:112], in0=stat[:, 96:112],
                        in1=stat[:, 16:32], op=ALU.subtract)
        e.tensor_tensor(out=stat[:, 80:96], in0=stat[:, 16:32],
                        in1=stat[:, 16:32], op=ALU.mult)
        e.tensor_tensor(out=stat[:, 96:112], in0=stat[:, 96:112],
                        in1=stat[:, 112:128], op=ALU.max)
        e.tensor_tensor(out=stat[:, 0:16], in0=stat[:, 0:16],
                        in1=stat[:, 80:96], op=ALU.subtract
                        ).then_inc(s_v, 1)
        cnt["v"] += 1
    VE(ln1_c1)
    v_c1 = cnt["v"]

    def ln1_s1(e, v_c1=v_c1):
        e.wait_ge(s_v, v_c1)
        e.activation(out=stat[:, 32:48], in_=stat[:, 0:16], func=AF.Sqrt,
                     bias=np.float32(0.0)).then_inc(s_s, 1)
        cnt["s"] += 1
    SE(ln1_s1)
    s_sq1 = cnt["s"]

    def ln1_c2(e, s_sq1=s_sq1):
        e.wait_ge(s_s, s_sq1)
        e.reciprocal(out=stat[:, 32:48], in_=stat[:, 32:48])
        e.tensor_scalar(out=stat[:, 112:128], in0=stat[:, 0:16],
                        scalar1=-1e-5, scalar2=None, op0=ALU.add)
        e.tensor_tensor(out=stat[:, 48:64], in0=stat[:, 16:32],
                        in1=stat[:, 32:48], op=ALU.mult)
        e.reciprocal(out=stat[:, 16:32], in_=stat[:, 0:16])
        e.tensor_scalar(out=stat[:, 48:64], in0=stat[:, 48:64],
                        scalar1=-1.0, scalar2=None, op0=ALU.mult)
        e.tensor_tensor(out=stat[:, 112:128], in0=stat[:, 112:128],
                        in1=stat[:, 16:32], op=ALU.mult
                        ).then_inc(s_v, 1)
        cnt["v"] += 1
    VE(ln1_c2)
    v_c2 = cnt["v"]

    def ln1_s2(e, v_c2=v_c2):
        e.wait_ge(s_v, v_c2)
        e.activation(out=stat[:, 16:32], in_=stat[:, 112:128], func=AF.Sqrt,
                     bias=np.float32(1e-5)).then_inc(s_s, 1)
        cnt["s"] += 1
    SE(ln1_s2)
    s_sq2 = cnt["s"]

    def ln1_c3(e, s_sq2=s_sq2):
        e.wait_ge(s_s, s_sq2)
        e.reciprocal(out=stat[:, 16:32], in_=stat[:, 16:32])
        e.tensor_tensor(out=stat[:, 96:112], in0=stat[:, 96:112],
                        in1=stat[:, 32:48], op=ALU.mult)
        VF(e, 1)
        e.tensor_tensor(out=stat[:, 112:128], in0=stat[:, 96:112],
                        in1=stat[:, 16:32], op=ALU.mult)
        VF(e)
        e.tensor_scalar(out=stat[:, 112:128], in0=stat[:, 112:128],
                        scalar1=1e-5, scalar2=None, op0=ALU.max)
        VF(e, 1)
        e.tensor_scalar(out=stat[:, 80:96], in0=stat[:, 112:128],
                        scalar1=wsc2[:, 0:1], scalar2=float(np.float32(1.0 / 127.0)),
                        op0=ALU.mult, op1=ALU.mult)
        e.reciprocal(out=stat[:, 112:128], in_=stat[:, 112:128])
        VF(e)
        e.tensor_scalar(out=stat[:, 112:128], in0=stat[:, 112:128],
                        scalar1=127.0, scalar2=None, op0=ALU.mult)
        VF(e)
        e.tensor_tensor(out=stat[:, 64:80], in0=stat[:, 16:32],
                        in1=stat[:, 112:128], op=ALU.mult)
        VF(e).then_inc(s_v, 1)
        cnt["v"] += 1
    VE(ln1_c3)
    v_c3 = cnt["v"]

    def ln1_apply(e, v_c3=v_c3):
        e.wait_ge(s_v, v_c3)
        for tt in range(NTT):
            li = e.activation(out=sbf[:, C * tt:C * tt + C],
                              in_=sbf[:, C * tt:C * tt + C], func=AF.Identity,
                              bias=stat[:, 48 + tt:49 + tt],
                              scale=stat[:, 32 + tt:33 + tt])
        li.then_inc(s_s, 1)
        cnt["s"] += 1
    SE(ln1_apply)
    s_y = cnt["s"]
    phase("ln1")

    ffn_v = {}
    ffn_t = {}
    out_sent = []

    for grp4 in range(4):
        for ti in range(4):
            tt = 4 * grp4 + ti

            def xquant(e, tt=tt):
                e.wait_ge(s_s, s_y)
                if ("xqfree",) in ffn_t:
                    e.wait_ge(s_t, ffn_t[("xqfree",)])
                e.tensor_scalar(out=wi32, in0=sbf[:, C * tt:C * tt + C],
                                scalar1=stat[:, 64 + tt:65 + tt],
                                scalar2=None, op0=ALU.mult)
                e.tensor_copy(out=xq, in_=wi32).then_inc(s_v, 1)
                cnt["v"] += 1
            VE(xquant)
            v_xq = cnt["v"]

            def xqt(e, v_xq=v_xq):
                e.wait_ge(s_v, v_xq)
                if ("xqTev",) in ffn_v:
                    e.wait_ge(s_v, ffn_v[("xqTev",)])
                for c8 in range(8):
                    li = e.transpose(out=tpb[:, 128 * c8:128 * c8 + 128],
                                     in_=xq[:, 128 * c8:128 * c8 + 128],
                                     identity=eyeb_s)
                li.then_inc(s_t, 1)
                cnt["t"] += 1
            TE(xqt)
            ffn_t[("xqfree",)] = cnt["t"]
            t_xq = cnt["t"]

            def xqev(e, t_xq=t_xq):
                e.wait_ge(s_t, t_xq)
                e.tensor_copy(out=xqT, in_=tpb).then_inc(s_v, 1)
                cnt["v"] += 1
            VE(xqev)
            ffn_v[("xqTev",)] = cnt["v"]
            v_xqT = cnt["v"]

            for fg in range(8):
                def f1mm(e, tt=tt, fg=fg, v_xqT=v_xqT):
                    e.wait_ge(s_v, v_xqT)
                    e.wait_ge(s_v, v_w1_done)
                    if ("f1ev", fg % 3) in ffn_v:
                        e.wait_ge(s_v, ffn_v[("f1ev", fg % 3)])
                    bank = f1b[:, 512 * (fg % 3):512 * (fg % 3) + 512]
                    for c8 in range(8):
                        sl = xqT[:, 128 * c8:128 * c8 + 128]
                        e.ldweights(sl)
                        li = e.matmul(out=bank, lhsT=sl,
                                      rhs=w1q[:, FF * c8 + 512 * fg:
                                              FF * c8 + 512 * fg + 512],
                                      start=(c8 == 0), stop=(c8 == 7))
                    li.then_inc(s_t, 1)
                    cnt["t"] += 1
                TE(f1mm)
                t_f1 = cnt["t"]

                def f1ev(e, tt=tt, fg=fg, t_f1=t_f1):
                    e.wait_ge(s_t, t_f1)
                    bank = f1b[:, 512 * (fg % 3):512 * (fg % 3) + 512]
                    e.scalar_tensor_tensor(
                        out=hbuf[:, 512 * fg:512 * fg + 512], in0=bank,
                        scalar=stat[:, 80 + tt:81 + tt],
                        in1=b1bc[:, 512 * fg:512 * fg + 512],
                        op0=ALU.mult, op1=ALU.add).then_inc(s_v, 1)
                    cnt["v"] += 1
                VE(f1ev)
                ffn_v[("f1ev", fg % 3)] = cnt["v"]
                v_h = cnt["v"]

                def f1g(e, fg=fg, v_h=v_h):
                    e.wait_ge(s_v, v_h)
                    e.activation(out=hbuf[:, 512 * fg:512 * fg + 512],
                                 in_=hbuf[:, 512 * fg:512 * fg + 512],
                                 func=AF.Gelu,
                                 bias=np.float32(0.0)).then_inc(s_s, 1)
                    cnt["s"] += 1
                SE(f1g)
                ffn_v[("gelu", tt, fg)] = cnt["s"]

            def gsq(e, tt=tt, ti=ti):
                e.wait_ge(s_s, ffn_v[("gelu", tt, 7)])
                if ("qgfree",) in ffn_t:
                    e.wait_ge(s_t, ffn_t[("qgfree",)])
                e.activation(out=qg, in_=hbuf, func=AF.Square,
                             bias=np.float32(0.0),
                             accum_out=stat[:, ti:ti + 1]).then_inc(s_s, 1)
                cnt["s"] += 1
            SE(gsq)
            s_sq = cnt["s"]

            def gst(e, ti=ti, s_sq=s_sq):
                e.wait_ge(s_s, s_sq)
                e.reduce_max(out=stat[:, 4 + ti:5 + ti], in_=hbuf, axis=AX.X,
                             apply_absolute_value=True)
                e.tensor_scalar(out=stat[:, ti:ti + 1], in0=stat[:, ti:ti + 1],
                                scalar1=float(np.float32(1.0 / FF)), scalar2=1e-5,
                                op0=ALU.mult, op1=ALU.add)
                VF(e, 1)
                e.tensor_scalar(out=stat[:, 12:13], in0=stat[:, 4 + ti:5 + ti],
                                scalar1=1e-12, scalar2=None, op0=ALU.max)
                VF(e)
                e.reciprocal(out=stat[:, 12:13], in_=stat[:, 12:13])
                VF(e)
                e.tensor_scalar(out=stat[:, 12:13], in0=stat[:, 12:13],
                                scalar1=127.0, scalar2=None,
                                op0=ALU.mult)
                VF(e).then_inc(s_v, 1)
                cnt["v"] += 1
            VE(gst)
            v_gs = cnt["v"]

            def gquant(e, v_gs=v_gs):
                e.wait_ge(s_v, v_gs)
                for q in range(4):
                    e.tensor_scalar(out=wi32,
                                    in0=hbuf[:, 1024 * q:1024 * q + 1024],
                                    scalar1=stat[:, 12:13], scalar2=None,
                                    op0=ALU.mult)
                    li = e.tensor_copy(out=qg[:, 1024 * q:1024 * q + 1024],
                                       in_=wi32)
                li.then_inc(s_v, 1)
                cnt["v"] += 1
            VE(gquant)
            v_qg = cnt["v"]

            for sub in range(2):
                def qgt(e, sub=sub, v_qg=v_qg):
                    e.wait_ge(s_v, v_qg)
                    if ("qgTev",) in ffn_v:
                        e.wait_ge(s_v, ffn_v[("qgTev",)])
                    for k16 in range(16):
                        bank = tpb if k16 < 8 else tpb2
                        li = e.transpose(out=bank[:, 128 * (k16 % 8):
                                                  128 * (k16 % 8) + 128],
                                         in_=qg[:, 2048 * sub + 128 * k16:
                                                2048 * sub + 128 * k16 + 128],
                                         identity=eyeb_s)
                    li.then_inc(s_t, 1)
                    cnt["t"] += 1
                TE(qgt)
                ffn_t[("qgfree",)] = cnt["t"]
                tq = cnt["t"]

                def qgev(e, sub=sub, tq=tq):
                    e.wait_ge(s_t, tq)
                    e.tensor_copy(out=qgT[:, 2048 * sub:2048 * sub + 1024],
                                  in_=tpb)
                    e.tensor_copy(out=qgT[:, 2048 * sub + 1024:2048 * sub + 2048],
                                  in_=tpb2).then_inc(s_v, 1)
                    cnt["v"] += 1
                VE(qgev)
                ffn_v[("qgTev",)] = cnt["v"]
            v_qgT = cnt["v"]

            for dg in range(2):
                def f2mm(e, tt=tt, dg=dg, v_qgT=v_qgT):
                    e.wait_ge(s_v, v_qgT)
                    e.wait_ge(s_v, v_w2_done)
                    if ("f2ev", dg) in ffn_v:
                        e.wait_ge(s_v, ffn_v[("f2ev", dg)])
                    bank = f2b[:, 512 * dg:512 * dg + 512]
                    for c32 in range(32):
                        sl = qgT[:, 128 * c32:128 * c32 + 128]
                        e.ldweights(sl)
                        li = e.matmul(out=bank, lhsT=sl,
                                      rhs=w2q[:, C * c32 + 512 * dg:
                                              C * c32 + 512 * dg + 512],
                                      start=(c32 == 0), stop=(c32 == 31))
                    li.then_inc(s_t, 1)
                    cnt["t"] += 1
                TE(f2mm)
                t_f2 = cnt["t"]

                def f2ev(e, tt=tt, ti=ti, dg=dg, t_f2=t_f2):
                    e.wait_ge(s_t, t_f2)
                    e.tensor_copy(out=h2raw[:, C * ti + 512 * dg:
                                            C * ti + 512 * dg + 512],
                                  in_=f2b[:, 512 * dg:512 * dg + 512]
                                  ).then_inc(s_v, 1)
                    cnt["v"] += 1
                VE(f2ev)
                ffn_v[("f2ev", dg)] = cnt["v"]
        v_h2 = cnt["v"]

        def ep_s1(e, v_h2=v_h2):
            e.wait_ge(s_v, v_h2)
            e.activation(out=stat[:, 8:12], in_=stat[:, 0:4], func=AF.Sqrt,
                         bias=np.float32(0.0)).then_inc(s_s, 1)
            cnt["s"] += 1
        SE(ep_s1)
        s_e1 = cnt["s"]

        def ep_v1(e, s_e1=s_e1):
            e.wait_ge(s_s, s_e1)
            e.reciprocal(out=stat[:, 8:12], in_=stat[:, 8:12])
            VF(e)
            e.tensor_tensor(out=stat[:, 8:12], in0=stat[:, 8:12],
                            in1=stat[:, 4:8], op=ALU.mult)
            VF(e)
            e.tensor_scalar(out=stat[:, 8:12], in0=stat[:, 8:12],
                            scalar1=1e-5, scalar2=wsc2[:, 3:4],
                            op0=ALU.max, op1=ALU.mult)
            VF(e).then_inc(s_v, 1)
            cnt["v"] += 1
        VE(ep_v1)
        v_sig = cnt["v"]

        for ti in range(4):
            tt = 4 * grp4 + ti

            def ep_z(e, tt=tt, ti=ti, v_sig=v_sig):
                e.wait_ge(s_v, v_sig)
                e.scalar_tensor_tensor(out=tscr,
                                       in0=h2raw[:, C * ti:C * ti + C],
                                       scalar=stat[:, 8 + ti:9 + ti],
                                       in1=b2bc, op0=ALU.mult, op1=ALU.add)
                e.tensor_tensor(out=sbf[:, C * tt:C * tt + C], in0=tscr,
                                in1=sbf[:, C * tt:C * tt + C], op=ALU.add)
                sl = sbf[:, C * tt:C * tt + C]
                e.reduce_sum(out=stat[:, 16 + ti:17 + ti], in_=sl, axis=AX.X)
                e.scalar_tensor_tensor(out=pt[1][:, 0:C], in0=sl, scalar=1.0,
                                       in1=sl, op0=ALU.mult, op1=ALU.mult,
                                       accum_out=stat[:, 20 + ti:21 + ti]
                                       ).then_inc(s_v, 1)
                cnt["v"] += 1
            VE(ep_z)
        v_z4 = cnt["v"]

        def ep_v2(e, v_z4=v_z4):
            e.wait_ge(s_v, v_z4)
            e.tensor_scalar(out=stat[:, 16:20], in0=stat[:, 16:20],
                            scalar1=float(np.float32(-1.0 / C)), scalar2=None,
                            op0=ALU.mult)
            VF(e, 1)
            e.tensor_scalar(out=stat[:, 20:24], in0=stat[:, 20:24],
                            scalar1=float(np.float32(1.0 / C)), scalar2=None, op0=ALU.mult)
            VF(e, 1)
            e.tensor_tensor(out=stat[:, 28:32], in0=stat[:, 16:20],
                            in1=stat[:, 16:20], op=ALU.mult)
            VF(e)
            e.tensor_tensor(out=stat[:, 20:24], in0=stat[:, 20:24],
                            in1=stat[:, 28:32], op=ALU.subtract)
            cnt["v"] += 1
            VF(e, 1).then_inc(s_v, 1)
        VE(ep_v2)
        v_l2 = cnt["v"]

        def ep_s2(e, v_l2=v_l2):
            e.wait_ge(s_v, v_l2)
            e.activation(out=stat[:, 24:28], in_=stat[:, 20:24], func=AF.Sqrt,
                         bias=np.float32(1e-5)).then_inc(s_s, 1)
            cnt["s"] += 1
        SE(ep_s2)
        s_l2 = cnt["s"]

        def ep_v3(e, s_l2=s_l2):
            e.wait_ge(s_s, s_l2)
            e.reciprocal(out=stat[:, 24:28], in_=stat[:, 24:28])
            VF(e)
            e.tensor_tensor(out=stat[:, 28:32], in0=stat[:, 16:20],
                            in1=stat[:, 24:28], op=ALU.mult)
            VF(e).then_inc(s_v, 1)
            cnt["v"] += 1
        VE(ep_v3)
        v_l2r = cnt["v"]

        for ti in range(4):
            tt = 4 * grp4 + ti

            def ep_out(e, tt=tt, ti=ti, v_l2r=v_l2r):
                e.wait_ge(s_v, v_l2r)
                if len(out_sent) >= 2:
                    e.wait_ge(s_out, out_sent[-2])
                e.activation(out=ostg[tt % 2], in_=sbf[:, C * tt:C * tt + C],
                             func=AF.Identity, bias=stat[:, 28 + ti:29 + ti],
                             scale=stat[:, 24 + ti:25 + ti]).then_inc(s_s, 1)
                cnt["s"] += 1
            SE(ep_out)
            s_o = cnt["s"]

            def ep_dma(e, tt=tt, s_o=s_o):
                e.wait_ge(s_s, s_o)
                e.dma_start(out=bass.AP(out_d.tensor, 128 * tt * C,
                                        [[C, 128], [1, C]]),
                            in_=ostg[tt % 2]).then_inc(s_out, 16)
                cnt["out"] += 16
            GP(ep_dma)
            out_sent.append(cnt["out"])

    if DEBUG_DUMP:
        dump_chunks = [
            [(0, 4, wsc2), (8, 16, wsc[0:1, 0:16]), (32, 128, stat)],
            [(0, 1024, w1q[:, 0:1024])],
            [(0, 1024, w1q[:, 4096 * 7 + 2048:4096 * 7 + 3072])],
            [(0, 1024, w2q[:, 0:1024])],
            [(0, 1024, sbf[:, 0:1024])],
            [(0, 1024, sbf[:, 15 * 1024:16 * 1024])],
            [(0, 1024, h2raw[:, 0:1024])],
            [(0, 1024, h2raw[:, 3 * 1024:4 * 1024])],
            [(0, 1024, hbuf[:, 0:1024])],
            [(0, 1024, hbuf[:, 2048:3072])],
            [(0, 1024, qg[:, 0:1024])],
            [(0, 1024, qgT[:, 0:1024])],
            [(0, 1024, xq)],
            [(0, 1024, xqT)],
            [(0, 1024, b2bc)],
            [(0, 1024, tscr)],
        ]

        dump_base = cnt["out"]
        dump_v = []

        def dump_cp(e):
            for sem, key in [(s_ld, "ld"), (s_gd, "gd"), (s_t, "t"),
                             (s_s, "s"), (s_cc, "cc"), (s_out, "out")]:
                e.wait_ge(sem, cnt[key])
            for i, grp in enumerate(dump_chunks):
                bb = i % 2
                if i >= 2:
                    e.wait_ge(s_out, dump_base + 16 * (i - 1))
                e.memset(ostg[bb], 0.0)
                li = None
                for lo, w, src in grp:
                    pp = 1 if (lo == 8) else 128
                    li = e.tensor_copy(out=ostg[bb][0:pp, lo:lo + w], in_=src)
                li.then_inc(s_v, 1)
                cnt["v"] += 1
                dump_v.append(cnt["v"])
        VE(dump_cp)

        def dump_dma(e):
            for i in range(len(dump_chunks)):
                e.wait_ge(s_v, dump_v[i])
                e.dma_start(out=bass.AP(dbg_d.tensor, 128 * i * C,
                                        [[C, 128], [1, C]]),
                            in_=ostg[i % 2]).then_inc(s_out, 16)
                cnt["out"] += 16
        GP(dump_dma)

    def fin(e):
        e.wait_ge(s_out, cnt["out"])
    GP(fin)

    for c in reversed(semctx):
        c.__exit__(None, None, None)
    ctx.__exit__(None, None, None)

    f = nc.m.functions[0]
    order = {"main": 0, "SP": 1, "PE": 2, "Pool": 3, "DVE": 4, "Activation": 5}

    def bgrp(b):
        nm = b.name
        if nm == "main":
            return 0
        if nm.startswith("g_exch_if_"):
            return order["Pool"]
        parts = nm.split("_")
        if nm.startswith("block_") and len(parts) > 2 and parts[2] in order:
            return order[parts[2]]
        return 9
    f.blocks = sorted(f.blocks, key=bgrp)

    if upto is None:
        _CACHE["nc"] = nc
    return nc


def kernel(L0_attn_x, L0_attn_w_qkv, L0_attn_b_qkv, L0_attn_w_proj,
           L0_attn_b_proj, L0_attn_pos_independent, L0_ffn_w1, L0_ffn_b1,
           L0_ffn_w2, L0_ffn_b2):
    phi, psi, tri, eyef, eyeb = _host_tables()
    x = np.ascontiguousarray(np.asarray(L0_attn_x, dtype=np.float32))
    wqkv = np.asarray(L0_attn_w_qkv, dtype=np.float32)
    bqkv = np.asarray(L0_attn_b_qkv, dtype=np.float32)
    wproj = np.asarray(L0_attn_w_proj, dtype=np.float32)
    bproj = np.ascontiguousarray(np.asarray(L0_attn_b_proj, dtype=np.float32))
    pos_ind = np.ascontiguousarray(np.asarray(L0_attn_pos_independent,
                                              dtype=np.float32))
    w1 = np.asarray(L0_ffn_w1, dtype=np.float32)
    b1 = np.ascontiguousarray(np.asarray(L0_ffn_b1, dtype=np.float32))
    w2 = np.asarray(L0_ffn_w2, dtype=np.float32)
    b2 = np.ascontiguousarray(np.asarray(L0_ffn_b2, dtype=np.float32))

    zeros_tab = np.zeros_like(phi)
    zslab = np.zeros((8, 64), np.float32)
    pslab = np.ascontiguousarray(pos_ind.reshape(8, 64))
    nc = build_nc()

    in_maps = []
    for c in range(8):
        b, p = c // 2, c % 2
        perm = np.concatenate([np.arange(512) + 512 * p,
                               np.arange(512) + 512 * (1 - p)])
        im = dict(
            x_full=x[b],
            x_own=np.ascontiguousarray(x[b, 2048 * p:2048 * p + 2048]),
            wq_t=np.ascontiguousarray(wqkv[512 * p:512 * p + 512].T),
            wk_t=np.ascontiguousarray(
                wqkv[1024 + 512 * p:1024 + 512 * p + 512].T),
            wv_t=np.ascontiguousarray(
                wqkv[2048 + 512 * p:2048 + 512 * p + 512].T),
            bq_in=np.ascontiguousarray(bqkv[512 * p:512 * p + 512]),
            bk_in=np.ascontiguousarray(bqkv[1024 + 512 * p:1024 + 512 * p + 512]),
            bv_in=np.ascontiguousarray(bqkv[2048 + 512 * p:2048 + 512 * p + 512]),
            wproj_t=np.ascontiguousarray(wproj.T[perm, :]),
            bproj_in=bproj,
            w1_t=np.ascontiguousarray(w1.T),
            b1_in=b1,
            w2_t=np.ascontiguousarray(w2.T),
            b2_in=b2,
            pos_slab=(zslab if p == 0 else pslab),
            phi_in=(phi if p == 0 else zeros_tab),
            psi_in=(psi if p == 0 else zeros_tab),
            ebias_in=np.full(8, -(32.0 + MARGIN) if p == 0 else -MARGIN,
                             dtype=np.float32),
            tri_in=tri,
            eyef_in=eyef,
            eyeb_in=eyeb,
            psel=np.array([[p]], dtype=np.int32),
        )
        in_maps.append(im)

    res = run_bass_kernel_spmd(nc, in_maps, core_ids=list(range(8)))
    _LAST_RES.clear()
    _LAST_RES.extend(res.results if hasattr(res, "results") else res)
    out = np.empty((4, T, C), dtype=np.float32)
    for c in range(8):
        b, p = c // 2, c % 2
        out[b, 2048 * p:2048 * p + 2048] = np.asarray(res.results[c]["out"])
    return out


# version 09
# speedup vs baseline: 1.5842x; 1.0541x over previous
import math
import numpy as np
import ml_dtypes
import concourse.bass as bass
import concourse.mybir as mybir
from concourse.bass_utils import run_bass_kernel_spmd

F32 = mybir.dt.float32
BF16 = mybir.dt.bfloat16
I32 = mybir.dt.int32
F8 = mybir.dt.float8e4
AF = mybir.ActivationFunctionType
ALU = mybir.AluOpType
AX = mybir.AxisListType
f32 = np.float32

T = 4096
C = 1024
FF = 4096
NTT = 16
MARGIN = 10.0

_CACHE = {}
DEBUG_DUMP = False
_LAST_RES = []


def _host_tables():
    if "tabs" in _CACHE:
        return _CACHE["tabs"]
    dim = C // 2
    pos = np.arange(dim, dtype=np.float64)
    freqs = 1.0 / 10000.0 ** (pos / dim)
    active = int(np.sum(freqs * 2 * math.pi * T >= 1.0))
    active = min(active, dim - C // 4)
    assert active == 256
    fr = freqs[:active]
    th = np.arange(T, dtype=np.float64)[:, None] * fr[None, :]
    cos, sin = np.cos(th), np.sin(th)
    phi = np.zeros((8, 64, T), np.float64)
    psi = np.zeros((8, 64, T), np.float64)
    for h in range(8):
        sl = slice(32 * h, 32 * h + 32)
        phi[h, 0:32] = cos[:, sl].T
        phi[h, 32:64] = sin[:, sl].T
        psi[h, 0:32] = (cos[:, sl] - sin[:, sl]).T
        psi[h, 32:64] = (cos[:, sl] + sin[:, sl]).T
    kq = np.arange(128)
    tabs = (phi.astype(ml_dtypes.bfloat16), psi.astype(ml_dtypes.bfloat16),
            (kq[:, None] <= kq[None, :]).astype(ml_dtypes.bfloat16),
            np.eye(128, dtype=np.float32), np.eye(128, dtype=ml_dtypes.bfloat16))
    _CACHE["tabs"] = tabs
    return tabs


def _groups(ntk, w=3):
    return [list(range(i, min(i + w, ntk))) for i in range(0, ntk, w)]


class _StopEmission(Exception):
    pass


def build_nc(upto=None):
    if upto is None and "nc" in _CACHE:
        return _CACHE["nc"]
    nc = bass.Bass("TRN2", target_bir_lowering=False,
                   detect_race_conditions=False)
    dram = lambda n, s, d, k="ExternalInput": nc.dram_tensor(n, s, d, kind=k).ap()

    x_full = dram("x_full", [T, C], F32)
    x_own = dram("x_own", [2048, C], F32)
    wq_t = dram("wq_t", [C, 512], F32)
    wk_t = dram("wk_t", [C, 512], F32)
    wv_t = dram("wv_t", [C, 512], F32)
    bq_in = dram("bq_in", [512], F32)
    bk_in = dram("bk_in", [512], F32)
    bv_in = dram("bv_in", [512], F32)
    wproj_t = dram("wproj_t", [C, C], F32)
    bproj_in = dram("bproj_in", [C], F32)
    w1_t = dram("w1_t", [C, FF], F32)
    b1_in = dram("b1_in", [FF], F32)
    w2_t = dram("w2_t", [FF, C], F32)
    b2_in = dram("b2_in", [C], F32)
    pos_slab = dram("pos_slab", [8, 64], F32)
    phi_in = dram("phi_in", [8, 64, T], BF16)
    psi_in = dram("psi_in", [8, 64, T], BF16)
    ebias_in = dram("ebias_in", [8], F32)
    tri_in = dram("tri_in", [128, 128], BF16)
    eyef_in = dram("eyef_in", [128, 128], F32)
    eyeb_in = dram("eyeb_in", [128, 128], BF16)
    psel = dram("psel", [1, 1], I32)
    out_d = dram("out", [2048, C], F32, "ExternalOutput")
    if DEBUG_DUMP:
        dbg_d = dram("dbg", [2048, C], F32, "ExternalOutput")

    scr8 = nc.dram_tensor("scr8", [8], F32).ap()
    scr1 = nc.dram_tensor("scr1", [4], F32).ap()
    scr_rc = nc.dram_tensor("scr_rc", [2048], BF16).ap()
    gift = nc.dram_tensor("gift", [128, 4 * 2048], BF16).ap()
    gath = nc.dram_tensor("gath", [2, 128, 4 * 2048], BF16).ap()

    SB_BASE = 17152
    def at(name, shape, dt, off):
        nbytes = int(np.prod(shape[1:])) * mybir.dt.size(dt)
        assert SB_BASE + off + nbytes <= 229376, (name, off, nbytes)
        return nc.alloc_sbuf_tensor_at(name, shape, dt, offset=SB_BASE + off).ap()

    xT = at("xT", [128, 8 * T], BF16, 0)
    w1q = at("w1q", [128, 8 * FF], F8, 0)
    w2q = at("w2q", [128, 32 * C], F8, 32768)
    vbuf = at("vbuf", [128, 32 * 520], BF16, 65536)
    sbf = at("sbf", [128, NTT * C], BF16, 65536)
    kaug = [at(f"kaug{i}", [128, T], BF16, 98816 + 8192 * i) for i in range(3)]
    pstage = at("pstage", [128, 4 * 2048], BF16, 98816)
    hbuf = at("hbuf", [128, FF], F32, 98816)
    peerO = at("peerO", [128, 4 * 2048], BF16, 115200)
    ostg = [at(f"ostg{i}", [128, C], F32, 115200 + 4096 * i) for i in range(2)]
    qaug = [at(f"qaug{i}", [128, T], BF16, 123392 + 8192 * i) for i in range(3)]
    xres = [at(f"xres{i}", [128, C], F32, 131584 + 4096 * i) for i in range(2)]
    tscr = at("tscr", [128, C], F32, 139776)
    wi32 = at("wi32", [128, C], I32, 139776)
    bpbc = at("bpbc", [128, C], BF16, 143872)
    b2bc = at("b2bc", [128, C], BF16, 145920)
    onorm = at("onorm", [128, 4 * T], BF16, 147968)
    wvb = at("wvb", [128, FF], BF16, 147968)
    qg = at("qg", [128, FF], BF16, 147968)
    wpb = at("wpb", [128, 8 * C], BF16, 147968)
    qgT = at("qgT", [128, FF], BF16, 156160)
    xq = at("xq", [128, C], BF16, 164352)
    xqT = at("xqT", [128, C], BF16, 166400)
    b1bc = at("b1bc", [128, FF], BF16, 168448)
    h2raw = at("h2raw", [128, 4 * C], BF16, 176640)
    eyef_s = at("eyef_s", [128, 128], F32, 180736)
    pslab_s = at("pslab_s", [8, 64], F32, 181248)
    csum = at("csum", [8, 1], F32, 181504)
    ebrow = at("ebrow", [8, 1], F32, 181536)
    ebt = at("ebt", [8, 1], F32, 181568)
    wstage = at("wstage", [128, 2048], F32, 186880)
    wstgb = at("wstgb", [128, 2048], BF16, 195072)
    pt = [at("pt0", [128, 1536], BF16, 199424),
          at("pt1", [128, 1536], BF16, 202496)]
    rcbc = at("rcbc", [128, 2048], BF16, 205568)
    tri_s = at("tri_s", [128, 128], BF16, 209664)
    eyeb_s = at("eyeb_s", [128, 128], BF16, 209920)
    ebias_bc = at("ebias_bc", [128, 8], F32, 210176)
    bq8 = at("bq8", [128, 4], F32, 210208)
    bk_s = at("bk_s", [128, 4], F32, 210240)
    pb_s = at("pb_s", [1, 1], I32, 210272)
    bvbc = at("bvbc", [128, 512], BF16, 210304)
    stat = at("stat", [128, 128], F32, 211328)
    ones64 = at("ones64", [128, 64], F32, 211840)
    wsc = at("wsc", [1, 16], F32, 212096)
    wsc2 = at("wsc2", [128, 4], F32, 212160)
    vfill = at("vfill", [128, 64], F32, 181632)

    psum_ctx = [nc.psum_tensor("sh0", [128, 1536], F32),
                nc.psum_tensor("sh1", [128, 1536], F32),
                nc.psum_tensor("otb", [128, 512], F32),
                nc.psum_tensor("aux", [128, 512], F32)]
    sh0, sh1, otb, aux = [c.__enter__().ap() for c in psum_ctx]
    SHP = [sh0, sh1]

    ctx = nc.Block()
    block = ctx.__enter__()
    semctx = [nc.semaphore(n) for n in
              ["s_ld", "s_gd", "s_t", "s_v", "s_s", "s_cc", "s_out"]]
    s_ld, s_gd, s_t, s_v, s_s, s_cc, s_out = [c.__enter__() for c in semctx]
    cnt = dict(ld=0, gd=0, t=0, v=0, s=0, cc=0, out=0)

    stopped = [False]

    def VF(e, n=2):
        li = None
        for _ in range(n):
            li = e.tensor_scalar(out=vfill, in0=vfill, scalar1=1.0,
                                 scalar2=None, op0=ALU.mult)
        return li

    def SY(fn):
        if stopped[0]:
            return
        @block.sync
        def _(e):
            fn(e)

    def GP(fn):
        if stopped[0]:
            return
        @block.gpsimd
        def _(e):
            fn(e)

    def TE(fn):
        if stopped[0]:
            return
        @block.tensor
        def _(e):
            fn(e)

    def VE(fn):
        if stopped[0]:
            return
        @block.vector
        def _(e):
            fn(e)

    def SE(fn):
        if stopped[0]:
            return
        @block.scalar
        def _(e):
            fn(e)

    def phase(name):
        if stopped[0] or upto is None or name != upto:
            return
        def drain(e):
            for sem, key in [(s_ld, "ld"), (s_gd, "gd"), (s_t, "t"),
                             (s_v, "v"), (s_s, "s"), (s_cc, "cc")]:
                e.wait_ge(sem, cnt[key])
            e.memset(ostg[0], 0.0)
            e.dma_start(out=bass.AP(out_d.tensor, 0, [[C, 128], [1, C]]),
                        in_=ostg[0]).then_inc(s_out, 16)
            cnt["out"] += 16
            e.wait_ge(s_out, cnt["out"])
        GP(drain)
        stopped[0] = True

    def p0_sync(e):
        for dst, src in [(pb_s, psel), (tri_s, tri_in), (eyef_s, eyef_in),
                         (eyeb_s, eyeb_in), (pslab_s, pos_slab)]:
            e.dma_start(out=dst, in_=src).then_inc(s_ld, 16)
            cnt["ld"] += 16
        for dst, src in [(bq8, bq_in), (bk_s, bk_in)]:
            e.dma_start(out=dst,
                        in_=bass.AP(src.tensor, 0, [[1, 128], [128, 4]]),
                        allow_slow_non_contiguous=True).then_inc(s_ld, 16)
            cnt["ld"] += 16
        e.dma_start(out=ebrow,
                    in_=bass.AP(ebias_in.tensor, 0, [[1, 8], [1, 1]])
                    ).then_inc(s_ld, 16)
        cnt["ld"] += 16
        e.dma_start(out=wstage[:, 0:512],
                    in_=bass.AP(bv_in.tensor, 0, [[0, 128], [1, 512]])
                    ).then_inc(s_ld, 16)
        cnt["ld"] += 16
    SY(p0_sync)
    ld_p0 = cnt["ld"]

    def p0_vec(e):
        e.wait_ge(s_ld, ld_p0)
        e.reduce_sum(out=csum, in_=pslab_s, axis=AX.X)
        e.memset(bass.AP(vbuf.tensor, 64,
                         [[520 * 32, 128], [520, 32], [65, 8], [1, 1]]), 1.0)
        e.memset(ones64, 1.0)
        e.tensor_tensor(out=ebt, in0=ebrow, in1=csum, op=ALU.add)
        e.memset(stat, 0.0)
        e.memset(wsc, 0.0)
        e.tensor_tensor(out=ebrow, in0=ebt, in1=csum, op=ALU.subtract)
        e.memset(vfill, 0.0)
        e.tensor_scalar(out=bq8, in0=bq8, scalar1=0.125, scalar2=None,
                        op0=ALU.mult)
        e.tensor_copy(out=bvbc, in_=wstage[:, 0:512]).then_inc(s_v, 1)
        cnt["v"] += 1
    VE(p0_vec)
    v_p0 = cnt["v"]

    def p0_gp(e):
        e.wait_ge(s_v, v_p0)
        e.dma_start(out=scr8,
                    in_=bass.AP(ebrow.tensor, 0, [[1, 8], [1, 1]])
                    ).then_inc(s_gd, 16)
        cnt["gd"] += 16
        e.wait_ge(s_gd, cnt["gd"])
        for h in range(8):
            e.dma_start(out=ebias_bc[:, h:h + 1],
                        in_=bass.AP(scr8.tensor, h, [[0, 128], [1, 1]])
                        ).then_inc(s_gd, 16)
            cnt["gd"] += 16
    GP(p0_gp)
    phase("p0")

    xck_evac = [0] * 16
    for ck in range(16):
        def x_ld(e, ck=ck):
            if ck == 0:
                e.wait_ge(s_v, v_p0)
            else:
                e.wait_ge(s_v, xck_evac[ck - 1])
            e.dma_start(out=wstage,
                        in_=bass.AP(x_full.tensor, (256 * ck) * C,
                                    [[C, 128], [128 * C, 2], [1, C]])
                        ).then_inc(s_ld, 16)
            cnt["ld"] += 16
        SY(x_ld)
        ldk = cnt["ld"]

        def x_tp(e, ck=ck, ldk=ldk):
            e.wait_ge(s_ld, ldk)
            if ck >= 1:
                e.wait_ge(s_v, xck_evac[ck - 1])
            li = None
            for a in range(2):
                bank = sh0 if a == 0 else sh1
                for c8 in range(8):
                    li = e.transpose(out=bass.AP(bank.tensor, 128 * c8,
                                                 [[1536, 128], [1, 128]]),
                                     in_=wstage[:, 1024 * a + 128 * c8:
                                                1024 * a + 128 * c8 + 128],
                                     identity=eyef_s)
            li.then_inc(s_t, 1)
            cnt["t"] += 1
        TE(x_tp)
        t_tp = cnt["t"]

        def x_ev(e, ck=ck, t_tp=t_tp):
            e.wait_ge(s_t, t_tp)
            li = None
            for a in range(2):
                bank = sh0 if a == 0 else sh1
                for c8 in range(8):
                    li = e.tensor_copy(
                        out=xT[:, T * c8 + 256 * ck + 128 * a:
                               T * c8 + 256 * ck + 128 * a + 128],
                        in_=bank[:, 128 * c8:128 * c8 + 128])
            li.then_inc(s_v, 1)
            cnt["v"] += 1
        VE(x_ev)
        xck_evac[ck] = cnt["v"]
    phase("xT")

    wv_cast_v = [0, 0]
    for half in range(2):
        def wv_ld(e, half=half):
            e.wait_ge(s_v, xck_evac[15] if half == 0 else wv_cast_v[0])
            e.dma_start(out=wstage,
                        in_=bass.AP(wv_t.tensor, 256 * half,
                                    [[512, 128], [512 * 128, 8], [1, 256]])
                        ).then_inc(s_ld, 16)
            cnt["ld"] += 16
        SY(wv_ld)
        ldw = cnt["ld"]

        def wv_c(e, half=half, ldw=ldw):
            e.wait_ge(s_ld, ldw)
            e.tensor_copy(out=bass.AP(wvb.tensor, 256 * half,
                                      [[FF, 128], [512, 8], [1, 256]]),
                          in_=wstage).then_inc(s_v, 1)
            cnt["v"] += 1
        VE(wv_c)
        wv_cast_v[half] = cnt["v"]
    v_wvb = cnt["v"]

    vm_evac = [0] * 32
    for tt in range(32):
        def vmm(e, tt=tt):
            e.wait_ge(s_v, v_wvb)
            if tt >= 2:
                e.wait_ge(s_v, vm_evac[tt - 2])
            bank = otb if tt % 2 == 0 else aux
            for c8 in range(8):
                sl = xT[:, T * c8 + 128 * tt: T * c8 + 128 * tt + 128]
                e.ldweights(sl)
                li = e.matmul(out=bank, lhsT=sl,
                              rhs=wvb[:, 512 * c8:512 * c8 + 512],
                              start=(c8 == 0), stop=(c8 == 7))
            li.then_inc(s_t, 1)
            cnt["t"] += 1
        TE(vmm)
        t_vm = cnt["t"]

        def vev(e, tt=tt, t_vm=t_vm):
            e.wait_ge(s_t, t_vm)
            bank = otb if tt % 2 == 0 else aux
            e.scalar_tensor_tensor(
                out=bass.AP(vbuf.tensor, 520 * tt,
                            [[520 * 32, 128], [65, 8], [1, 64]]),
                in0=bass.AP(bank.tensor, 0, [[512, 128], [64, 8], [1, 64]]),
                scalar=1.0,
                in1=bass.AP(bvbc.tensor, 0, [[512, 128], [64, 8], [1, 64]]),
                op0=ALU.mult, op1=ALU.add).then_inc(s_v, 1)
            cnt["v"] += 1
        VE(vev)
        vm_evac[tt] = cnt["v"]
    phase("vmm")

    kq_ready = [0] * 8
    kq_aug_ld = [0] * 8
    exp_cp = {}
    mask_cp = {}
    pv_cp = {}
    sc_cp = {}
    head_last_sc = [0] * 8
    head_norm_v = [0] * 8
    stage_free = {"s": 0, "v": 0, "wq": 0}

    def emit_pair_fill(hp, guard_t):
        b0, b1 = (2 * hp) % 3, (2 * hp + 1) % 3

        def w_ld(e, hp=hp):
            if hp == 0:
                e.wait_ge(s_v, v_wvb)
            else:
                e.wait_ge(s_v, stage_free["v"])
            e.dma_start(out=wstage[:, 0:1024],
                        in_=bass.AP(wk_t.tensor, 128 * hp,
                                    [[512, 128], [512 * 128, 8], [1, 128]])
                        ).then_inc(s_ld, 16)
            e.dma_start(out=wstage[:, 1024:2048],
                        in_=bass.AP(wq_t.tensor, 128 * hp,
                                    [[512, 128], [512 * 128, 8], [1, 128]])
                        ).then_inc(s_ld, 16)
            cnt["ld"] += 32
        SY(w_ld)
        ld_w = cnt["ld"]

        def w_cast(e, ld_w=ld_w):
            e.wait_ge(s_ld, ld_w)
            e.tensor_copy(out=wstgb, in_=wstage).then_inc(s_v, 1)
            cnt["v"] += 1
        VE(w_cast)
        v_cast = cnt["v"]

        def aug_ld(e, hp=hp, b0=b0, b1=b1, guard_t=guard_t):
            e.wait_ge(s_t, guard_t)
            for h_i, (kb, qb) in enumerate([(kaug[b0], qaug[b0]),
                                            (kaug[b1], qaug[b1])]):
                h = 2 * hp + h_i
                ab = 64 if h % 2 == 0 else 0
                e.dma_start(out=kb[ab:ab + 64, :], in_=psi_in[h, :, :]
                            ).then_inc(s_ld, 16)
                e.dma_start(out=qb[ab:ab + 64, :], in_=phi_in[h, :, :]
                            ).then_inc(s_ld, 16)
                cnt["ld"] += 32
        SY(aug_ld)
        ld_aug = cnt["ld"]

        evk = []
        for which in range(2):
            wbase = 1024 * which
            for st in range(8):
                def kqmm(e, which=which, wbase=wbase, st=st, v_cast=v_cast,
                         guard_t=guard_t):
                    e.wait_ge(s_v, v_cast)
                    e.wait_ge(s_t, guard_t)
                    if evk:
                        e.wait_ge(s_s, evk[-1])
                    for c8 in range(8):
                        sl = wstgb[:, wbase + 128 * c8: wbase + 128 * c8 + 128]
                        e.ldweights(sl)
                        li = e.matmul(out=aux, lhsT=sl,
                                      rhs=xT[:, T * c8 + 512 * st:
                                             T * c8 + 512 * st + 512],
                                      start=(c8 == 0), stop=(c8 == 7))
                    li.then_inc(s_t, 1)
                    cnt["t"] += 1
                TE(kqmm)
                t_mm = cnt["t"]

                def kqev(e, which=which, st=st, t_mm=t_mm, hp=hp, b0=b0, b1=b1):
                    e.wait_ge(s_t, t_mm)
                    for h_i in range(2):
                        h = 2 * hp + h_i
                        dst = (kaug if which == 0 else qaug)[b0 if h_i == 0 else b1]
                        db = 0 if h % 2 == 0 else 64
                        src = aux[64 * h_i:64 * h_i + 64, :]
                        if which == 0:
                            li = e.activation(
                                out=dst[db:db + 64, 512 * st:512 * st + 512],
                                in_=src, func=AF.Identity,
                                bias=bk_s[64 * h_i:64 * h_i + 64, hp:hp + 1])
                        else:
                            li = e.activation(
                                out=dst[db:db + 64, 512 * st:512 * st + 512],
                                in_=src, func=AF.Identity,
                                bias=bq8[64 * h_i:64 * h_i + 64, hp:hp + 1],
                                scale=np.float32(0.125))
                    li.then_inc(s_s, 1)
                    cnt["s"] += 1
                SE(kqev)
                evk.append(cnt["s"])
        stage_free["s"] = cnt["s"]
        stage_free["v"] = v_cast
        for h_i in range(2):
            kq_ready[2 * hp + h_i] = cnt["s"]
            kq_aug_ld[2 * hp + h_i] = ld_aug

    def emit_head(h):
        Ka, Qa = kaug[h % 3], qaug[h % 3]
        items = []
        for st in range(8):
            ntk = 4 * st + 4
            grps = _groups(ntk)
            for g, kjs in enumerate(grps):
                items.append((st, g, kjs, ntk, len(grps)))

        def emit_pv(pend):
            st, g, kjs, ntk, glen, par = pend

            def t_pv(e, h=h, st=st, g=g, kjs=kjs, par=par, ntk=ntk):
                e.wait_ge(s_v, mask_cp[(h, st, g)])
                if g == 0 and ("otfree",) in pv_cp:
                    e.wait_ge(s_v, pv_cp[("otfree",)])
                for i, kj in enumerate(kjs):
                    c0 = max(0, kj - 4 * st) * 128
                    vsl = vbuf[:, 520 * kj + 65 * h: 520 * kj + 65 * h + 65]
                    e.ldweights(vsl)
                    li = e.matmul(out=otb[0:65, c0:512], lhsT=vsl,
                                  rhs=pt[par][:, 512 * i + c0:512 * i + 512],
                                  start=(kj == 0), stop=(kj == ntk - 1))
                li.then_inc(s_t, 1)
                cnt["t"] += 1
            TE(t_pv)
            pv_cp[(h, st, g)] = cnt["t"]
            pv_cp[("ptfree", par)] = cnt["t"]

            if g != glen - 1:
                return

            def v_strip(e, h=h, st=st, ngr=glen):
                e.wait_ge(s_t, pv_cp[(h, st, ngr - 1)])
                lo = 64 * (h % 2)
                e.tensor_copy(out=onorm[lo:lo + 64,
                                        T * (h // 2) + 512 * st:
                                        T * (h // 2) + 512 * st + 512],
                              in_=otb[0:64, :])
                e.tensor_copy(out=rcbc[0:1, 512 * (st % 4):512 * (st % 4) + 512],
                              in_=otb[64:65, :]).then_inc(s_v, 1)
                cnt["v"] += 1
            VE(v_strip)
            pv_cp[("otfree",)] = cnt["v"]

            if st in (3, 7):
                hh = (st - 3) // 4
                v_sums = cnt["v"]

                def g_bc(e, v_sums=v_sums):
                    e.wait_ge(s_v, v_sums)
                    e.dma_start(out=scr_rc, in_=rcbc[0:1, :]).then_inc(s_gd, 16)
                    cnt["gd"] += 16
                    e.wait_ge(s_gd, cnt["gd"])
                    e.dma_start(out=rcbc,
                                in_=bass.AP(scr_rc.tensor, 0,
                                            [[0, 128], [1, 2048]])
                                ).then_inc(s_gd, 16)
                    cnt["gd"] += 16
                GP(g_bc)
                gd_rc = cnt["gd"]

                def v_norm(e, h=h, hh=hh, gd_rc=gd_rc):
                    e.wait_ge(s_gd, gd_rc)
                    with nc.allow_low_precision(reason="bf16 softmax sums"):
                        e.reciprocal(out=rcbc, in_=rcbc)
                    lo = 64 * (h % 2)
                    sl = slice(T * (h // 2) + 2048 * hh,
                               T * (h // 2) + 2048 * hh + 2048)
                    e.tensor_tensor(out=onorm[lo:lo + 64, sl],
                                    in0=onorm[lo:lo + 64, sl],
                                    in1=rcbc[lo:lo + 64, :],
                                    op=ALU.mult).then_inc(s_v, 1)
                    cnt["v"] += 1
                VE(v_norm)

        pend = None
        for j, (st, g, kjs, ntk, glen) in enumerate(items):
            par = j % 2

            def t_sc(e, h=h, st=st, g=g, kjs=kjs, par=par):
                e.wait_ge(s_s, kq_ready[h])
                e.wait_ge(s_ld, kq_aug_ld[h])
                key = ("shfree", par)
                if key in exp_cp:
                    e.wait_ge(s_s, exp_cp[key])
                for i, kj in enumerate(kjs):
                    sl = Ka[:, 128 * kj:128 * kj + 128]
                    e.ldweights(sl)
                    li = e.matmul(out=SHP[par][:, 512 * i:512 * i + 512],
                                  lhsT=sl,
                                  rhs=Qa[:, 512 * st:512 * st + 512],
                                  start=True, stop=True)
                li.then_inc(s_t, 1)
                cnt["t"] += 1
            TE(t_sc)
            sc_cp[(h, st, g)] = cnt["t"]

            if pend is not None:
                emit_pv(pend)

            def s_exp(e, h=h, st=st, g=g, kjs=kjs, par=par):
                e.wait_ge(s_t, sc_cp[(h, st, g)])
                key = ("ptfree", par)
                if key in pv_cp:
                    e.wait_ge(s_t, pv_cp[key])
                w = 512 * len(kjs)
                e.activation(out=pt[par][:, 0:w], in_=SHP[par][:, 0:w],
                             func=AF.Exp,
                             bias=ebias_bc[:, h:h + 1]).then_inc(s_s, 1)
                cnt["s"] += 1
            SE(s_exp)
            exp_cp[(h, st, g)] = cnt["s"]
            exp_cp[("shfree", par)] = cnt["s"]

            def v_mask(e, h=h, st=st, g=g, kjs=kjs, par=par):
                e.wait_ge(s_s, exp_cp[(h, st, g)])
                li = None
                for i, kj in enumerate(kjs):
                    d = kj - 4 * st
                    if 0 <= d <= 3:
                        sl = slice(512 * i + 128 * d, 512 * i + 128 * d + 128)
                        li = e.tensor_tensor(out=pt[par][:, sl],
                                             in0=pt[par][:, sl],
                                             in1=tri_s, op=ALU.mult)
                if li is not None:
                    li.then_inc(s_v, 1)
                else:
                    e.nop().then_inc(s_v, 1)
                cnt["v"] += 1
            VE(v_mask)
            mask_cp[(h, st, g)] = cnt["v"]
            pend = (st, g, kjs, ntk, glen, par)
        emit_pv(pend)
        head_last_sc[h] = sc_cp[(h, 7, len(_groups(32)) - 1)]
        head_norm_v[h] = cnt["v"]

    def emit_wquant(w_in, wrows, wdst, wcols, m_col, race_t):
        nchunks = wrows // 128
        spl = wcols // 1024
        tw = 1024
        tiles = [(c, j) for c in range(nchunks) for j in range(spl)]
        absv = []
        for idx, (c, j) in enumerate(tiles):
            half = 1024 * (idx % 2)
            lsem = s_ld if idx % 2 == 0 else s_gd
            lkey = "ld" if idx % 2 == 0 else "gd"

            def wl(e, c=c, j=j, idx=idx, half=half, lsem=lsem, lkey=lkey):
                if idx < 2:
                    e.wait_ge(s_v, stage_free["v"])
                    e.wait_ge(s_v, stage_free["wq"])
                else:
                    e.wait_ge(s_v, absv[idx - 2])
                e.dma_start(out=wstage[:, half:half + tw],
                            in_=bass.AP(w_in.tensor, (128 * c) * wcols + tw * j,
                                        [[wcols, 128], [1, tw]])
                            ).then_inc(lsem, 16)
                cnt[lkey] += 16
            (SY if idx % 2 == 0 else GP)(wl)
            ldw = cnt[lkey]

            def wa(e, idx=idx, ldw=ldw, half=half, lsem=lsem):
                e.wait_ge(lsem, ldw)
                e.reduce_sum(out=stat[:, 64 + idx:65 + idx],
                             in_=wstage[:, half:half + tw], axis=AX.X,
                             apply_absolute_value=True).then_inc(s_v, 1)
                cnt["v"] += 1
            VE(wa)
            absv.append(cnt["v"])
        ntile = len(tiles)

        def w_zero(e, ntile=ntile):
            e.wait_ge(s_v, absv[-1])
            if ntile < 64:
                e.memset(stat[:, 64 + ntile:128], 0.0).then_inc(s_v, 1)
            cnt["v"] += 1
        VE(w_zero)
        v_z = cnt["v"]

        def w_mm(e, v_z=v_z):
            e.wait_ge(s_v, v_z)
            e.wait_ge(s_s, stage_free["s"])
            e.matmul(out=aux[0:64, 0:64], lhsT=ones64, rhs=stat[:, 64:128],
                     start=True, stop=True).then_inc(s_t, 1)
            cnt["t"] += 1
        TE(w_mm)
        t_mm = cnt["t"]

        def w_chain(e, t_mm=t_mm, m_col=m_col):
            e.wait_ge(s_t, t_mm)
            e.reduce_sum(out=wsc[0:1, m_col:m_col + 1], in_=aux[0:1, 0:64],
                         axis=AX.X)
            VF(e)
            e.tensor_scalar(out=wsc[0:1, m_col:m_col + 1],
                            in0=wsc[0:1, m_col:m_col + 1],
                            scalar1=float(np.float32(1.0 / (wrows * wcols))), scalar2=1e-5,
                            op0=ALU.mult, op1=ALU.max)
            VF(e)
            e.reciprocal(out=wsc[0:1, m_col + 1:m_col + 2],
                         in_=wsc[0:1, m_col:m_col + 1]).then_inc(s_v, 1)
            cnt["v"] += 1
        VE(w_chain)
        v_ch = cnt["v"]

        def g_wbc(e, v_ch=v_ch, m_col=m_col):
            e.wait_ge(s_v, v_ch)
            e.dma_start(out=scr1, in_=wsc[0:1, m_col:m_col + 4]
                        ).then_inc(s_gd, 16)
            cnt["gd"] += 16
            e.wait_ge(s_gd, cnt["gd"])
            for k in range(2):
                e.dma_start(out=wsc2[:, m_col + k:m_col + k + 1],
                            in_=bass.AP(scr1.tensor, k, [[0, 128], [1, 1]])
                            ).then_inc(s_gd, 16)
                cnt["gd"] += 16
        GP(g_wbc)
        gd_bc = cnt["gd"]

        rl = []
        for idx, (c, j) in enumerate(tiles):
            half = 1024 * (idx % 2)
            lsem = s_ld if idx % 2 == 0 else s_gd
            lkey = "ld" if idx % 2 == 0 else "gd"

            def wl2(e, c=c, j=j, idx=idx, half=half, lsem=lsem, lkey=lkey):
                if len(rl) >= 2:
                    e.wait_ge(s_v, rl[-2])
                else:
                    e.wait_ge(s_gd, gd_bc)
                e.dma_start(out=wstage[:, half:half + tw],
                            in_=bass.AP(w_in.tensor, (128 * c) * wcols + tw * j,
                                        [[wcols, 128], [1, tw]])
                            ).then_inc(lsem, 16)
                cnt[lkey] += 16
            (SY if idx % 2 == 0 else GP)(wl2)
            ldw = cnt[lkey]

            def wr(e, c=c, j=j, ldw=ldw, m_col=m_col, idx=idx, half=half,
                   lsem=lsem):
                e.wait_ge(lsem, ldw)
                if idx == 0:
                    e.wait_ge(s_gd, gd_bc)
                    e.wait_ge(s_t, race_t)
                e.tensor_scalar(out=wi32,
                                in0=wstage[:, half:half + tw],
                                scalar1=wsc2[:, m_col + 1:m_col + 2],
                                scalar2=None, op0=ALU.mult)
                li = e.tensor_scalar(
                    out=wdst[:, wcols * c + tw * j:wcols * c + tw * j + tw],
                    in0=wi32, scalar1=-1.0, scalar2=1.0,
                    op0=ALU.max, op1=ALU.min)
                li.then_inc(s_v, 1)
                cnt["v"] += 1
            VE(wr)
            rl.append(cnt["v"])
        stage_free["v"] = cnt["v"]
        stage_free["wq"] = cnt["v"]
        return cnt["v"]

    emit_pair_fill(0, guard_t=0)
    phase("fill0")
    emit_head(0)
    phase("head0")
    emit_pair_fill(1, guard_t=head_last_sc[0])
    emit_head(1)
    emit_head(2)
    emit_pair_fill(2, guard_t=head_last_sc[2])
    emit_head(3)
    phase("head3")
    emit_head(4)
    emit_pair_fill(3, guard_t=head_last_sc[4])
    emit_head(5)
    phase("head5")
    v_w1_done = emit_wquant(w1_t, C, w1q, FF, 0, race_t=head_last_sc[5])
    phase("w1q")
    emit_head(6)
    v_w2_done = emit_wquant(w2_t, FF, w2q, C, 2, race_t=head_last_sc[6])
    emit_head(7)
    phase("attn")

    def g_exch(e):
        for h in range(8):
            e.wait_ge(s_v, head_norm_v[h])
        with e.register("rp") as rp:
            e.reg_load(rp, pb_s[0:1, 0:1])
            with e.If_eq(rp, 0):
                for u in range(4):
                    e.dma_start(out=gift[:, 2048 * u:2048 * u + 2048],
                                in_=onorm[:, T * u + 2048:T * u + 4096]
                                ).then_inc(s_gd, 16)
            with e.Else():
                for u in range(4):
                    e.dma_start(out=gift[:, 2048 * u:2048 * u + 2048],
                                in_=onorm[:, T * u:T * u + 2048]
                                ).then_inc(s_gd, 16)
            cnt["gd"] += 64
            e.wait_ge(s_gd, cnt["gd"])
            e.collective_compute(
                "AllGather", ALU.bypass,
                replica_groups=[[0, 1], [2, 3], [4, 5], [6, 7]],
                ins=[gift], outs=[gath]).then_inc(s_cc, 1)
            cnt["cc"] += 1
            e.wait_ge(s_cc, cnt["cc"])
            with e.If_eq(rp, 0):
                for u in range(4):
                    e.dma_start(out=pstage[:, 2048 * u:2048 * u + 2048],
                                in_=onorm[:, T * u:T * u + 2048]
                                ).then_inc(s_gd, 16)
                e.dma_start(out=peerO, in_=gath[1]).then_inc(s_gd, 16)
            with e.Else():
                for u in range(4):
                    e.dma_start(out=pstage[:, 2048 * u:2048 * u + 2048],
                                in_=onorm[:, T * u + 2048:T * u + 4096]
                                ).then_inc(s_gd, 16)
                e.dma_start(out=peerO, in_=gath[0]).then_inc(s_gd, 16)
            cnt["gd"] += 80
    GP(g_exch)
    gd_exch = cnt["gd"]
    phase("exch")

    def bp_ld(e):
        e.wait_ge(s_v, v_w2_done)
        e.dma_start(out=wstage[:, 0:1024],
                    in_=bass.AP(bproj_in.tensor, 0, [[0, 128], [1, C]])
                    ).then_inc(s_ld, 16)
        cnt["ld"] += 16
    SY(bp_ld)
    ld_bp = cnt["ld"]

    def bp_c(e, ld_bp=ld_bp):
        e.wait_ge(s_ld, ld_bp)
        e.tensor_scalar(out=wsc2[:, 3:4], in0=wsc2[:, 2:3],
                        scalar1=float(np.float32(1.0 / 127.0)), scalar2=None,
                        op0=ALU.mult)
        e.tensor_copy(out=bpbc, in_=wstage[:, 0:1024]).then_inc(s_v, 1)
        cnt["v"] += 1
    VE(bp_c)
    v_bp = cnt["v"]

    wp_cast = [0] * 4
    for q4 in range(4):
        def wp_ld(e, q4=q4):
            e.wait_ge(s_v, v_bp if q4 == 0 else wp_cast[q4 - 1])
            if q4 == 0:
                e.wait_ge(s_gd, gd_exch)
            e.dma_start(out=wstage,
                        in_=bass.AP(wproj_t.tensor, 256 * q4,
                                    [[C, 128], [128 * C, 8], [1, 256]])
                        ).then_inc(s_ld, 16)
            cnt["ld"] += 16
        SY(wp_ld)
        ldw = cnt["ld"]

        def wp_c(e, q4=q4, ldw=ldw):
            e.wait_ge(s_ld, ldw)
            if q4 == 0:
                e.wait_ge(s_gd, gd_exch)
            e.tensor_copy(out=bass.AP(wpb.tensor, 256 * q4,
                                      [[8 * C, 128], [C, 8], [1, 256]]),
                          in_=wstage).then_inc(s_v, 1)
            cnt["v"] += 1
        VE(wp_c)
        wp_cast[q4] = cnt["v"]
    v_wp = cnt["v"]
    phase("wp")

    proj_done_v = [0] * NTT
    for tt in range(NTT):
        def xr_ld(e, tt=tt):
            if tt >= 2:
                e.wait_ge(s_v, proj_done_v[tt - 2])
            e.dma_start(out=xres[tt % 2],
                        in_=bass.AP(x_own.tensor, 128 * tt * C,
                                    [[C, 128], [1, C]])).then_inc(s_gd, 16)
            cnt["gd"] += 16
        GP(xr_ld)
        ld_xr = cnt["gd"]

        def pmm(e, tt=tt):
            e.wait_ge(s_v, v_wp)
            e.wait_ge(s_gd, gd_exch)
            if tt >= 2:
                e.wait_ge(s_v, proj_done_v[tt - 2])
            bset = sh0 if tt % 2 == 0 else sh1
            for dgrp in range(2):
                bank = bset[:, 512 * dgrp:512 * dgrp + 512]
                for ci in range(8):
                    if ci < 4:
                        sl = pstage[:, 2048 * ci + 128 * tt:
                                    2048 * ci + 128 * tt + 128]
                    else:
                        sl = peerO[:, 2048 * (ci - 4) + 128 * tt:
                                   2048 * (ci - 4) + 128 * tt + 128]
                    e.ldweights(sl)
                    li = e.matmul(out=bank, lhsT=sl,
                                  rhs=bass.AP(wpb.tensor, C * ci + 512 * dgrp,
                                              [[8 * C, 128], [1, 512]]),
                                  start=(ci == 0), stop=(ci == 7))
            li.then_inc(s_t, 1)
            cnt["t"] += 1
        TE(pmm)
        t_p = cnt["t"]

        def pev(e, tt=tt, t_p=t_p, ld_xr=ld_xr):
            e.wait_ge(s_t, t_p)
            e.wait_ge(s_gd, ld_xr)
            bset = sh0 if tt % 2 == 0 else sh1
            for dgrp in range(2):
                e.scalar_tensor_tensor(
                    out=tscr[:, 512 * dgrp:512 * dgrp + 512],
                    in0=bset[:, 512 * dgrp:512 * dgrp + 512], scalar=1.0,
                    in1=bpbc[:, 512 * dgrp:512 * dgrp + 512],
                    op0=ALU.mult, op1=ALU.add)
            e.tensor_tensor(out=sbf[:, C * tt:C * tt + C], in0=tscr,
                            in1=xres[tt % 2], op=ALU.add).then_inc(s_v, 1)
            cnt["v"] += 1
        VE(pev)
        proj_done_v[tt] = cnt["v"]
    v_proj = cnt["v"]

    phase("proj")
    for c in reversed(psum_ctx):
        c.__exit__(None, None, None)
    psum_ctx2 = [nc.psum_tensor("f1b", [128, 1536], F32),
                 nc.psum_tensor("f2b", [128, 1536], F32),
                 nc.psum_tensor("tpb", [128, 1024], BF16),
                 nc.psum_tensor("tpb2", [128, 1024], BF16)]
    f1b, f2b, tpb, tpb2 = [c.__enter__().ap() for c in psum_ctx2]

    b_v = [0, 0, 0]
    for bi, (src, dst, width) in enumerate(
            [(b1_in, b1bc[:, 0:2048], 2048),
             (None, b1bc[:, 2048:4096], 2048),
             (b2_in, b2bc, C)]):
        def bb_ld(e, bi=bi, width=width):
            e.wait_ge(s_v, v_proj if bi == 0 else b_v[bi - 1])
            base = 2048 if bi == 1 else 0
            e.dma_start(out=wstage[:, 0:width],
                        in_=bass.AP((b1_in if bi < 2 else b2_in).tensor, base,
                                    [[0, 128], [1, width]])
                        ).then_inc(s_ld, 16)
            cnt["ld"] += 16
        SY(bb_ld)
        ldb = cnt["ld"]

        def bb_c(e, bi=bi, dst=dst, width=width, ldb=ldb):
            e.wait_ge(s_ld, ldb)
            e.tensor_copy(out=dst, in_=wstage[:, 0:width]).then_inc(s_v, 1)
            cnt["v"] += 1
        VE(bb_c)
        b_v[bi] = cnt["v"]
    v_bias = cnt["v"]

    def ln1_stats(e):
        e.wait_ge(s_v, v_bias)
        for tt in range(NTT):
            sl = sbf[:, C * tt:C * tt + C]
            e.reduce_sum(out=stat[:, 16 + tt:17 + tt], in_=sl, axis=AX.X)
            e.scalar_tensor_tensor(out=pt[0][:, 0:C], in0=sl, scalar=1.0,
                                   in1=sl, op0=ALU.mult, op1=ALU.mult,
                                   accum_out=stat[:, tt:tt + 1])
            e.reduce_max(out=stat[:, 96 + tt:97 + tt], in_=sl, axis=AX.X)
            li = e.tensor_reduce(out=stat[:, 112 + tt:113 + tt], in_=sl,
                                 axis=AX.X, op=ALU.min)
        li.then_inc(s_v, 1)
        cnt["v"] += 1
    VE(ln1_stats)
    v_st = cnt["v"]

    def ln1_c1(e, v_st=v_st):
        e.wait_ge(s_v, v_st)
        e.tensor_scalar(out=stat[:, 16:32], in0=stat[:, 16:32],
                        scalar1=float(np.float32(1.0 / C)), scalar2=None, op0=ALU.mult)
        e.tensor_scalar(out=stat[:, 0:16], in0=stat[:, 0:16],
                        scalar1=float(np.float32(1.0 / C)), scalar2=1e-5,
                        op0=ALU.mult, op1=ALU.add)
        e.tensor_tensor(out=stat[:, 112:128], in0=stat[:, 16:32],
                        in1=stat[:, 112:128], op=ALU.subtract)
        e.tensor_tensor(out=stat[:, 96:112], in0=stat[:, 96:112],
                        in1=stat[:, 16:32], op=ALU.subtract)
        e.tensor_tensor(out=stat[:, 80:96], in0=stat[:, 16:32],
                        in1=stat[:, 16:32], op=ALU.mult)
        e.tensor_tensor(out=stat[:, 96:112], in0=stat[:, 96:112],
                        in1=stat[:, 112:128], op=ALU.max)
        e.tensor_tensor(out=stat[:, 0:16], in0=stat[:, 0:16],
                        in1=stat[:, 80:96], op=ALU.subtract
                        ).then_inc(s_v, 1)
        cnt["v"] += 1
    VE(ln1_c1)
    v_c1 = cnt["v"]

    def ln1_s1(e, v_c1=v_c1):
        e.wait_ge(s_v, v_c1)
        e.activation(out=stat[:, 32:48], in_=stat[:, 0:16], func=AF.Sqrt,
                     bias=np.float32(0.0)).then_inc(s_s, 1)
        cnt["s"] += 1
    SE(ln1_s1)
    s_sq1 = cnt["s"]

    def ln1_c2(e, s_sq1=s_sq1):
        e.wait_ge(s_s, s_sq1)
        e.reciprocal(out=stat[:, 32:48], in_=stat[:, 32:48])
        e.tensor_scalar(out=stat[:, 112:128], in0=stat[:, 0:16],
                        scalar1=-1e-5, scalar2=None, op0=ALU.add)
        e.tensor_tensor(out=stat[:, 48:64], in0=stat[:, 16:32],
                        in1=stat[:, 32:48], op=ALU.mult)
        e.reciprocal(out=stat[:, 16:32], in_=stat[:, 0:16])
        e.tensor_scalar(out=stat[:, 48:64], in0=stat[:, 48:64],
                        scalar1=-1.0, scalar2=None, op0=ALU.mult)
        e.tensor_tensor(out=stat[:, 112:128], in0=stat[:, 112:128],
                        in1=stat[:, 16:32], op=ALU.mult
                        ).then_inc(s_v, 1)
        cnt["v"] += 1
    VE(ln1_c2)
    v_c2 = cnt["v"]

    def ln1_s2(e, v_c2=v_c2):
        e.wait_ge(s_v, v_c2)
        e.activation(out=stat[:, 16:32], in_=stat[:, 112:128], func=AF.Sqrt,
                     bias=np.float32(1e-5)).then_inc(s_s, 1)
        cnt["s"] += 1
    SE(ln1_s2)
    s_sq2 = cnt["s"]

    def ln1_c3(e, s_sq2=s_sq2):
        e.wait_ge(s_s, s_sq2)
        e.reciprocal(out=stat[:, 16:32], in_=stat[:, 16:32])
        e.tensor_tensor(out=stat[:, 96:112], in0=stat[:, 96:112],
                        in1=stat[:, 32:48], op=ALU.mult)
        VF(e, 1)
        e.tensor_tensor(out=stat[:, 112:128], in0=stat[:, 96:112],
                        in1=stat[:, 16:32], op=ALU.mult)
        VF(e)
        e.tensor_scalar(out=stat[:, 112:128], in0=stat[:, 112:128],
                        scalar1=1e-5, scalar2=None, op0=ALU.max)
        VF(e, 1)
        e.tensor_scalar(out=stat[:, 80:96], in0=stat[:, 112:128],
                        scalar1=wsc2[:, 0:1], scalar2=float(np.float32(1.0 / 127.0)),
                        op0=ALU.mult, op1=ALU.mult)
        e.reciprocal(out=stat[:, 112:128], in_=stat[:, 112:128])
        VF(e)
        e.tensor_scalar(out=stat[:, 112:128], in0=stat[:, 112:128],
                        scalar1=127.0, scalar2=None, op0=ALU.mult)
        VF(e)
        e.tensor_tensor(out=stat[:, 64:80], in0=stat[:, 16:32],
                        in1=stat[:, 112:128], op=ALU.mult)
        VF(e).then_inc(s_v, 1)
        cnt["v"] += 1
    VE(ln1_c3)
    v_c3 = cnt["v"]

    def ln1_apply(e, v_c3=v_c3):
        e.wait_ge(s_v, v_c3)
        for tt in range(NTT):
            li = e.activation(out=sbf[:, C * tt:C * tt + C],
                              in_=sbf[:, C * tt:C * tt + C], func=AF.Identity,
                              bias=stat[:, 48 + tt:49 + tt],
                              scale=stat[:, 32 + tt:33 + tt])
        li.then_inc(s_s, 1)
        cnt["s"] += 1
    SE(ln1_apply)
    s_y = cnt["s"]
    phase("ln1")

    ffn_v = {}
    ffn_t = {}
    out_sent = []

    for grp4 in range(4):
        for ti in range(4):
            tt = 4 * grp4 + ti

            def xquant(e, tt=tt):
                e.wait_ge(s_s, s_y)
                if ("xqfree",) in ffn_t:
                    e.wait_ge(s_t, ffn_t[("xqfree",)])
                e.tensor_scalar(out=wi32, in0=sbf[:, C * tt:C * tt + C],
                                scalar1=stat[:, 64 + tt:65 + tt],
                                scalar2=None, op0=ALU.mult)
                e.tensor_copy(out=xq, in_=wi32).then_inc(s_v, 1)
                cnt["v"] += 1
            VE(xquant)
            v_xq = cnt["v"]

            def xqt(e, v_xq=v_xq):
                e.wait_ge(s_v, v_xq)
                if ("xqTev",) in ffn_v:
                    e.wait_ge(s_v, ffn_v[("xqTev",)])
                for c8 in range(8):
                    li = e.transpose(out=tpb[:, 128 * c8:128 * c8 + 128],
                                     in_=xq[:, 128 * c8:128 * c8 + 128],
                                     identity=eyeb_s)
                li.then_inc(s_t, 1)
                cnt["t"] += 1
            TE(xqt)
            ffn_t[("xqfree",)] = cnt["t"]
            t_xq = cnt["t"]

            def xqev(e, t_xq=t_xq):
                e.wait_ge(s_t, t_xq)
                e.tensor_copy(out=xqT, in_=tpb).then_inc(s_v, 1)
                cnt["v"] += 1
            VE(xqev)
            ffn_v[("xqTev",)] = cnt["v"]
            v_xqT = cnt["v"]

            for fg in range(8):
                def f1mm(e, tt=tt, fg=fg, v_xqT=v_xqT):
                    e.wait_ge(s_v, v_xqT)
                    e.wait_ge(s_v, v_w1_done)
                    if ("f1ev", fg % 3) in ffn_v:
                        e.wait_ge(s_v, ffn_v[("f1ev", fg % 3)])
                    bank = f1b[:, 512 * (fg % 3):512 * (fg % 3) + 512]
                    for c8 in range(8):
                        sl = xqT[:, 128 * c8:128 * c8 + 128]
                        e.ldweights(sl)
                        li = e.matmul(out=bank, lhsT=sl,
                                      rhs=w1q[:, FF * c8 + 512 * fg:
                                              FF * c8 + 512 * fg + 512],
                                      start=(c8 == 0), stop=(c8 == 7))
                    li.then_inc(s_t, 1)
                    cnt["t"] += 1
                TE(f1mm)
                t_f1 = cnt["t"]

                def f1ev(e, tt=tt, fg=fg, t_f1=t_f1):
                    e.wait_ge(s_t, t_f1)
                    bank = f1b[:, 512 * (fg % 3):512 * (fg % 3) + 512]
                    e.scalar_tensor_tensor(
                        out=hbuf[:, 512 * fg:512 * fg + 512], in0=bank,
                        scalar=stat[:, 80 + tt:81 + tt],
                        in1=b1bc[:, 512 * fg:512 * fg + 512],
                        op0=ALU.mult, op1=ALU.add).then_inc(s_v, 1)
                    cnt["v"] += 1
                VE(f1ev)
                ffn_v[("f1ev", fg % 3)] = cnt["v"]
                v_h = cnt["v"]

                def f1g(e, fg=fg, v_h=v_h):
                    e.wait_ge(s_v, v_h)
                    e.activation(out=hbuf[:, 512 * fg:512 * fg + 512],
                                 in_=hbuf[:, 512 * fg:512 * fg + 512],
                                 func=AF.Gelu,
                                 bias=np.float32(0.0)).then_inc(s_s, 1)
                    cnt["s"] += 1
                SE(f1g)
                ffn_v[("gelu", tt, fg)] = cnt["s"]

            def gsq(e, tt=tt, ti=ti):
                e.wait_ge(s_s, ffn_v[("gelu", tt, 7)])
                if ("qgfree",) in ffn_t:
                    e.wait_ge(s_t, ffn_t[("qgfree",)])
                e.activation(out=qg, in_=hbuf, func=AF.Square,
                             bias=np.float32(0.0),
                             accum_out=stat[:, ti:ti + 1]).then_inc(s_s, 1)
                cnt["s"] += 1
            SE(gsq)
            s_sq = cnt["s"]

            def gst(e, ti=ti, s_sq=s_sq):
                e.wait_ge(s_s, s_sq)
                e.reduce_max(out=stat[:, 4 + ti:5 + ti], in_=hbuf, axis=AX.X,
                             apply_absolute_value=True)
                e.tensor_scalar(out=stat[:, ti:ti + 1], in0=stat[:, ti:ti + 1],
                                scalar1=float(np.float32(1.0 / FF)), scalar2=1e-5,
                                op0=ALU.mult, op1=ALU.add)
                VF(e, 1)
                e.tensor_scalar(out=stat[:, 12:13], in0=stat[:, 4 + ti:5 + ti],
                                scalar1=1e-12, scalar2=None, op0=ALU.max)
                VF(e)
                e.reciprocal(out=stat[:, 12:13], in_=stat[:, 12:13])
                VF(e)
                e.tensor_scalar(out=stat[:, 12:13], in0=stat[:, 12:13],
                                scalar1=127.0, scalar2=None,
                                op0=ALU.mult)
                VF(e).then_inc(s_v, 1)
                cnt["v"] += 1
            VE(gst)
            v_gs = cnt["v"]

            def gquant(e, v_gs=v_gs):
                e.wait_ge(s_v, v_gs)
                for q in range(4):
                    e.tensor_scalar(out=wi32,
                                    in0=hbuf[:, 1024 * q:1024 * q + 1024],
                                    scalar1=stat[:, 12:13], scalar2=None,
                                    op0=ALU.mult)
                    li = e.tensor_copy(out=qg[:, 1024 * q:1024 * q + 1024],
                                       in_=wi32)
                li.then_inc(s_v, 1)
                cnt["v"] += 1
            VE(gquant)
            v_qg = cnt["v"]

            for sub in range(2):
                def qgt(e, sub=sub, v_qg=v_qg):
                    e.wait_ge(s_v, v_qg)
                    if ("qgTev",) in ffn_v:
                        e.wait_ge(s_v, ffn_v[("qgTev",)])
                    for k16 in range(16):
                        bank = tpb if k16 < 8 else tpb2
                        li = e.transpose(out=bank[:, 128 * (k16 % 8):
                                                  128 * (k16 % 8) + 128],
                                         in_=qg[:, 2048 * sub + 128 * k16:
                                                2048 * sub + 128 * k16 + 128],
                                         identity=eyeb_s)
                    li.then_inc(s_t, 1)
                    cnt["t"] += 1
                TE(qgt)
                ffn_t[("qgfree",)] = cnt["t"]
                tq = cnt["t"]

                def qgev(e, sub=sub, tq=tq):
                    e.wait_ge(s_t, tq)
                    e.tensor_copy(out=qgT[:, 2048 * sub:2048 * sub + 1024],
                                  in_=tpb)
                    e.tensor_copy(out=qgT[:, 2048 * sub + 1024:2048 * sub + 2048],
                                  in_=tpb2).then_inc(s_v, 1)
                    cnt["v"] += 1
                VE(qgev)
                ffn_v[("qgTev",)] = cnt["v"]
            v_qgT = cnt["v"]

            for dg in range(2):
                def f2mm(e, tt=tt, dg=dg, v_qgT=v_qgT):
                    e.wait_ge(s_v, v_qgT)
                    e.wait_ge(s_v, v_w2_done)
                    if ("f2ev", dg) in ffn_v:
                        e.wait_ge(s_v, ffn_v[("f2ev", dg)])
                    bank = f2b[:, 512 * dg:512 * dg + 512]
                    for c32 in range(32):
                        sl = qgT[:, 128 * c32:128 * c32 + 128]
                        e.ldweights(sl)
                        li = e.matmul(out=bank, lhsT=sl,
                                      rhs=w2q[:, C * c32 + 512 * dg:
                                              C * c32 + 512 * dg + 512],
                                      start=(c32 == 0), stop=(c32 == 31))
                    li.then_inc(s_t, 1)
                    cnt["t"] += 1
                TE(f2mm)
                t_f2 = cnt["t"]

                def f2ev(e, tt=tt, ti=ti, dg=dg, t_f2=t_f2):
                    e.wait_ge(s_t, t_f2)
                    e.tensor_copy(out=h2raw[:, C * ti + 512 * dg:
                                            C * ti + 512 * dg + 512],
                                  in_=f2b[:, 512 * dg:512 * dg + 512]
                                  ).then_inc(s_v, 1)
                    cnt["v"] += 1
                VE(f2ev)
                ffn_v[("f2ev", dg)] = cnt["v"]
        v_h2 = cnt["v"]

        def ep_s1(e, v_h2=v_h2):
            e.wait_ge(s_v, v_h2)
            e.activation(out=stat[:, 8:12], in_=stat[:, 0:4], func=AF.Sqrt,
                         bias=np.float32(0.0)).then_inc(s_s, 1)
            cnt["s"] += 1
        SE(ep_s1)
        s_e1 = cnt["s"]

        def ep_v1(e, s_e1=s_e1):
            e.wait_ge(s_s, s_e1)
            e.reciprocal(out=stat[:, 8:12], in_=stat[:, 8:12])
            VF(e)
            e.tensor_tensor(out=stat[:, 8:12], in0=stat[:, 8:12],
                            in1=stat[:, 4:8], op=ALU.mult)
            VF(e)
            e.tensor_scalar(out=stat[:, 8:12], in0=stat[:, 8:12],
                            scalar1=1e-5, scalar2=wsc2[:, 3:4],
                            op0=ALU.max, op1=ALU.mult)
            VF(e).then_inc(s_v, 1)
            cnt["v"] += 1
        VE(ep_v1)
        v_sig = cnt["v"]

        for ti in range(4):
            tt = 4 * grp4 + ti

            def ep_z(e, tt=tt, ti=ti, v_sig=v_sig):
                e.wait_ge(s_v, v_sig)
                e.scalar_tensor_tensor(out=tscr,
                                       in0=h2raw[:, C * ti:C * ti + C],
                                       scalar=stat[:, 8 + ti:9 + ti],
                                       in1=b2bc, op0=ALU.mult, op1=ALU.add)
                e.tensor_tensor(out=sbf[:, C * tt:C * tt + C], in0=tscr,
                                in1=sbf[:, C * tt:C * tt + C], op=ALU.add)
                sl = sbf[:, C * tt:C * tt + C]
                e.reduce_sum(out=stat[:, 16 + ti:17 + ti], in_=sl, axis=AX.X)
                e.scalar_tensor_tensor(out=pt[1][:, 0:C], in0=sl, scalar=1.0,
                                       in1=sl, op0=ALU.mult, op1=ALU.mult,
                                       accum_out=stat[:, 20 + ti:21 + ti]
                                       ).then_inc(s_v, 1)
                cnt["v"] += 1
            VE(ep_z)
        v_z4 = cnt["v"]

        def ep_v2(e, v_z4=v_z4):
            e.wait_ge(s_v, v_z4)
            e.tensor_scalar(out=stat[:, 16:20], in0=stat[:, 16:20],
                            scalar1=float(np.float32(-1.0 / C)), scalar2=None,
                            op0=ALU.mult)
            VF(e, 1)
            e.tensor_scalar(out=stat[:, 20:24], in0=stat[:, 20:24],
                            scalar1=float(np.float32(1.0 / C)), scalar2=None, op0=ALU.mult)
            VF(e, 1)
            e.tensor_tensor(out=stat[:, 28:32], in0=stat[:, 16:20],
                            in1=stat[:, 16:20], op=ALU.mult)
            VF(e)
            e.tensor_tensor(out=stat[:, 20:24], in0=stat[:, 20:24],
                            in1=stat[:, 28:32], op=ALU.subtract)
            cnt["v"] += 1
            VF(e, 1).then_inc(s_v, 1)
        VE(ep_v2)
        v_l2 = cnt["v"]

        def ep_s2(e, v_l2=v_l2):
            e.wait_ge(s_v, v_l2)
            e.activation(out=stat[:, 24:28], in_=stat[:, 20:24], func=AF.Sqrt,
                         bias=np.float32(1e-5)).then_inc(s_s, 1)
            cnt["s"] += 1
        SE(ep_s2)
        s_l2 = cnt["s"]

        def ep_v3(e, s_l2=s_l2):
            e.wait_ge(s_s, s_l2)
            e.reciprocal(out=stat[:, 24:28], in_=stat[:, 24:28])
            VF(e)
            e.tensor_tensor(out=stat[:, 28:32], in0=stat[:, 16:20],
                            in1=stat[:, 24:28], op=ALU.mult)
            VF(e).then_inc(s_v, 1)
            cnt["v"] += 1
        VE(ep_v3)
        v_l2r = cnt["v"]

        for ti in range(4):
            tt = 4 * grp4 + ti

            def ep_out(e, tt=tt, ti=ti, v_l2r=v_l2r):
                e.wait_ge(s_v, v_l2r)
                if len(out_sent) >= 2:
                    e.wait_ge(s_out, out_sent[-2])
                e.activation(out=ostg[tt % 2], in_=sbf[:, C * tt:C * tt + C],
                             func=AF.Identity, bias=stat[:, 28 + ti:29 + ti],
                             scale=stat[:, 24 + ti:25 + ti]).then_inc(s_s, 1)
                cnt["s"] += 1
            SE(ep_out)
            s_o = cnt["s"]

            def ep_dma(e, tt=tt, s_o=s_o):
                e.wait_ge(s_s, s_o)
                e.dma_start(out=bass.AP(out_d.tensor, 128 * tt * C,
                                        [[C, 128], [1, C]]),
                            in_=ostg[tt % 2]).then_inc(s_out, 16)
                cnt["out"] += 16
            GP(ep_dma)
            out_sent.append(cnt["out"])

    if DEBUG_DUMP:
        dump_chunks = [
            [(0, 4, wsc2), (8, 16, wsc[0:1, 0:16]), (32, 128, stat)],
            [(0, 1024, w1q[:, 0:1024])],
            [(0, 1024, w1q[:, 4096 * 7 + 2048:4096 * 7 + 3072])],
            [(0, 1024, w2q[:, 0:1024])],
            [(0, 1024, sbf[:, 0:1024])],
            [(0, 1024, sbf[:, 15 * 1024:16 * 1024])],
            [(0, 1024, h2raw[:, 0:1024])],
            [(0, 1024, h2raw[:, 3 * 1024:4 * 1024])],
            [(0, 1024, hbuf[:, 0:1024])],
            [(0, 1024, hbuf[:, 2048:3072])],
            [(0, 1024, qg[:, 0:1024])],
            [(0, 1024, qgT[:, 0:1024])],
            [(0, 1024, xq)],
            [(0, 1024, xqT)],
            [(0, 1024, b2bc)],
            [(0, 1024, tscr)],
        ]

        dump_base = cnt["out"]
        dump_v = []

        def dump_cp(e):
            for sem, key in [(s_ld, "ld"), (s_gd, "gd"), (s_t, "t"),
                             (s_s, "s"), (s_cc, "cc"), (s_out, "out")]:
                e.wait_ge(sem, cnt[key])
            for i, grp in enumerate(dump_chunks):
                bb = i % 2
                if i >= 2:
                    e.wait_ge(s_out, dump_base + 16 * (i - 1))
                e.memset(ostg[bb], 0.0)
                li = None
                for lo, w, src in grp:
                    pp = 1 if (lo == 8) else 128
                    li = e.tensor_copy(out=ostg[bb][0:pp, lo:lo + w], in_=src)
                li.then_inc(s_v, 1)
                cnt["v"] += 1
                dump_v.append(cnt["v"])
        VE(dump_cp)

        def dump_dma(e):
            for i in range(len(dump_chunks)):
                e.wait_ge(s_v, dump_v[i])
                e.dma_start(out=bass.AP(dbg_d.tensor, 128 * i * C,
                                        [[C, 128], [1, C]]),
                            in_=ostg[i % 2]).then_inc(s_out, 16)
                cnt["out"] += 16
        GP(dump_dma)

    def fin(e):
        e.wait_ge(s_out, cnt["out"])
    GP(fin)

    for c in reversed(semctx):
        c.__exit__(None, None, None)
    ctx.__exit__(None, None, None)

    f = nc.m.functions[0]
    order = {"main": 0, "SP": 1, "PE": 2, "Pool": 3, "DVE": 4, "Activation": 5}

    def bgrp(b):
        nm = b.name
        if nm == "main":
            return 0
        if nm.startswith("g_exch_if_"):
            return order["Pool"]
        parts = nm.split("_")
        if nm.startswith("block_") and len(parts) > 2 and parts[2] in order:
            return order[parts[2]]
        return 9
    f.blocks = sorted(f.blocks, key=bgrp)

    if upto is None:
        _CACHE["nc"] = nc
    return nc


def kernel(L0_attn_x, L0_attn_w_qkv, L0_attn_b_qkv, L0_attn_w_proj,
           L0_attn_b_proj, L0_attn_pos_independent, L0_ffn_w1, L0_ffn_b1,
           L0_ffn_w2, L0_ffn_b2):
    phi, psi, tri, eyef, eyeb = _host_tables()
    x = np.ascontiguousarray(np.asarray(L0_attn_x, dtype=np.float32))
    wqkv = np.asarray(L0_attn_w_qkv, dtype=np.float32)
    bqkv = np.asarray(L0_attn_b_qkv, dtype=np.float32)
    wproj = np.asarray(L0_attn_w_proj, dtype=np.float32)
    bproj = np.ascontiguousarray(np.asarray(L0_attn_b_proj, dtype=np.float32))
    pos_ind = np.ascontiguousarray(np.asarray(L0_attn_pos_independent,
                                              dtype=np.float32))
    w1 = np.asarray(L0_ffn_w1, dtype=np.float32)
    b1 = np.ascontiguousarray(np.asarray(L0_ffn_b1, dtype=np.float32))
    w2 = np.asarray(L0_ffn_w2, dtype=np.float32)
    b2 = np.ascontiguousarray(np.asarray(L0_ffn_b2, dtype=np.float32))

    zeros_tab = np.zeros_like(phi)
    zslab = np.zeros((8, 64), np.float32)
    pslab = np.ascontiguousarray(pos_ind.reshape(8, 64))
    nc = build_nc()

    in_maps = []
    for c in range(8):
        b, p = c // 2, c % 2
        perm = np.concatenate([np.arange(512) + 512 * p,
                               np.arange(512) + 512 * (1 - p)])
        im = dict(
            x_full=x[b],
            x_own=np.ascontiguousarray(x[b, 2048 * p:2048 * p + 2048]),
            wq_t=np.ascontiguousarray(wqkv[512 * p:512 * p + 512].T),
            wk_t=np.ascontiguousarray(
                wqkv[1024 + 512 * p:1024 + 512 * p + 512].T),
            wv_t=np.ascontiguousarray(
                wqkv[2048 + 512 * p:2048 + 512 * p + 512].T),
            bq_in=np.ascontiguousarray(bqkv[512 * p:512 * p + 512]),
            bk_in=np.ascontiguousarray(bqkv[1024 + 512 * p:1024 + 512 * p + 512]),
            bv_in=np.ascontiguousarray(bqkv[2048 + 512 * p:2048 + 512 * p + 512]),
            wproj_t=np.ascontiguousarray(wproj.T[perm, :]),
            bproj_in=bproj,
            w1_t=np.ascontiguousarray(w1.T),
            b1_in=b1,
            w2_t=np.ascontiguousarray(w2.T),
            b2_in=b2,
            pos_slab=(zslab if p == 0 else pslab),
            phi_in=(phi if p == 0 else zeros_tab),
            psi_in=(psi if p == 0 else zeros_tab),
            ebias_in=np.full(8, -(32.0 + MARGIN) if p == 0 else -MARGIN,
                             dtype=np.float32),
            tri_in=tri,
            eyef_in=eyef,
            eyeb_in=eyeb,
            psel=np.array([[p]], dtype=np.int32),
        )
        in_maps.append(im)

    res = run_bass_kernel_spmd(nc, in_maps, core_ids=list(range(8)))
    _LAST_RES.clear()
    _LAST_RES.extend(res.results if hasattr(res, "results") else res)
    out = np.empty((4, T, C), dtype=np.float32)
    for c in range(8):
        b, p = c // 2, c % 2
        out[b, 2048 * p:2048 * p + 2048] = np.asarray(res.results[c]["out"])
    return out


# version 10
# speedup vs baseline: 1.6022x; 1.0114x over previous
import math
import numpy as np
import ml_dtypes
import concourse.bass as bass
import concourse.mybir as mybir
from concourse.bass_utils import run_bass_kernel_spmd

F32 = mybir.dt.float32
BF16 = mybir.dt.bfloat16
I32 = mybir.dt.int32
F8 = mybir.dt.float8e4
AF = mybir.ActivationFunctionType
ALU = mybir.AluOpType
AX = mybir.AxisListType
f32 = np.float32

T = 4096
C = 1024
FF = 4096
NTT = 16
MARGIN = 10.0

_CACHE = {}
DEBUG_DUMP = False
_LAST_RES = []


def _host_tables():
    if "tabs" in _CACHE:
        return _CACHE["tabs"]
    dim = C // 2
    pos = np.arange(dim, dtype=np.float64)
    freqs = 1.0 / 10000.0 ** (pos / dim)
    active = int(np.sum(freqs * 2 * math.pi * T >= 1.0))
    active = min(active, dim - C // 4)
    assert active == 256
    fr = freqs[:active]
    th = np.arange(T, dtype=np.float64)[:, None] * fr[None, :]
    cos, sin = np.cos(th), np.sin(th)
    phi = np.zeros((8, 64, T), np.float64)
    psi = np.zeros((8, 64, T), np.float64)
    for h in range(8):
        sl = slice(32 * h, 32 * h + 32)
        phi[h, 0:32] = cos[:, sl].T
        phi[h, 32:64] = sin[:, sl].T
        psi[h, 0:32] = (cos[:, sl] - sin[:, sl]).T
        psi[h, 32:64] = (cos[:, sl] + sin[:, sl]).T
    kq = np.arange(128)
    tabs = (phi.astype(ml_dtypes.bfloat16), psi.astype(ml_dtypes.bfloat16),
            (kq[:, None] <= kq[None, :]).astype(ml_dtypes.bfloat16),
            np.eye(128, dtype=np.float32), np.eye(128, dtype=ml_dtypes.bfloat16))
    _CACHE["tabs"] = tabs
    return tabs


def _groups(ntk, w=3):
    return [list(range(i, min(i + w, ntk))) for i in range(0, ntk, w)]


class _StopEmission(Exception):
    pass


def build_nc(upto=None):
    if upto is None and "nc" in _CACHE:
        return _CACHE["nc"]
    nc = bass.Bass("TRN2", target_bir_lowering=False,
                   detect_race_conditions=False)
    dram = lambda n, s, d, k="ExternalInput": nc.dram_tensor(n, s, d, kind=k).ap()

    x_full = dram("x_full", [T, C], F32)
    x_own = dram("x_own", [2048, C], F32)
    wq_t = dram("wq_t", [C, 512], F32)
    wk_t = dram("wk_t", [C, 512], F32)
    wv_t = dram("wv_t", [C, 512], F32)
    bq_in = dram("bq_in", [512], F32)
    bk_in = dram("bk_in", [512], F32)
    bv_in = dram("bv_in", [512], F32)
    wproj_t = dram("wproj_t", [C, C], F32)
    bproj_in = dram("bproj_in", [C], F32)
    w1_t = dram("w1_t", [C, FF], F32)
    b1_in = dram("b1_in", [FF], F32)
    w2_t = dram("w2_t", [FF, C], F32)
    b2_in = dram("b2_in", [C], F32)
    pos_slab = dram("pos_slab", [8, 64], F32)
    phi_in = dram("phi_in", [8, 64, T], BF16)
    psi_in = dram("psi_in", [8, 64, T], BF16)
    ebias_in = dram("ebias_in", [8], F32)
    tri_in = dram("tri_in", [128, 128], BF16)
    eyef_in = dram("eyef_in", [128, 128], F32)
    eyeb_in = dram("eyeb_in", [128, 128], BF16)
    psel = dram("psel", [1, 1], I32)
    out_d = dram("out", [2048, C], F32, "ExternalOutput")
    if DEBUG_DUMP:
        dbg_d = dram("dbg", [2048, C], F32, "ExternalOutput")

    scr8 = nc.dram_tensor("scr8", [8], F32).ap()
    scr1 = nc.dram_tensor("scr1", [4], F32).ap()
    scr_rc = nc.dram_tensor("scr_rc", [2048], BF16).ap()
    gift = nc.dram_tensor("gift", [128, 4 * 2048], BF16).ap()
    gath = nc.dram_tensor("gath", [2, 128, 4 * 2048], BF16).ap()

    SB_BASE = 17152
    def at(name, shape, dt, off):
        nbytes = int(np.prod(shape[1:])) * mybir.dt.size(dt)
        assert SB_BASE + off + nbytes <= 229376, (name, off, nbytes)
        return nc.alloc_sbuf_tensor_at(name, shape, dt, offset=SB_BASE + off).ap()

    xT = at("xT", [128, 8 * T], BF16, 0)
    w1q = at("w1q", [128, 8 * FF], F8, 0)
    w2q = at("w2q", [128, 32 * C], F8, 32768)
    vbuf = at("vbuf", [128, 32 * 520], BF16, 65536)
    sbf = at("sbf", [128, NTT * C], BF16, 65536)
    kaug = [at(f"kaug{i}", [128, T], BF16, 98816 + 8192 * i) for i in range(3)]
    pstage = at("pstage", [128, 4 * 2048], BF16, 98816)
    hbuf = at("hbuf", [128, FF], F32, 98816)
    peerO = at("peerO", [128, 4 * 2048], BF16, 115200)
    ostg = [at(f"ostg{i}", [128, C], F32, 115200 + 4096 * i) for i in range(2)]
    qaug = [at(f"qaug{i}", [128, T], BF16, 123392 + 8192 * i) for i in range(3)]
    xres = [at(f"xres{i}", [128, C], F32, 131584 + 4096 * i) for i in range(2)]
    tscr = at("tscr", [128, C], F32, 139776)
    wi32 = at("wi32", [128, C], I32, 139776)
    bpbc = at("bpbc", [128, C], BF16, 143872)
    b2bc = at("b2bc", [128, C], BF16, 145920)
    onorm = at("onorm", [128, 4 * T], BF16, 147968)
    wvb = at("wvb", [128, FF], BF16, 147968)
    qg = at("qg", [128, FF], BF16, 147968)
    wpb = at("wpb", [128, 8 * C], BF16, 147968)
    qgT = at("qgT", [128, FF], BF16, 156160)
    xq = at("xq", [128, C], BF16, 164352)
    xqT = at("xqT", [128, C], BF16, 166400)
    b1bc = at("b1bc", [128, FF], BF16, 168448)
    h2raw = at("h2raw", [128, 4 * C], BF16, 176640)
    eyef_s = at("eyef_s", [128, 128], F32, 180736)
    pslab_s = at("pslab_s", [8, 64], F32, 181248)
    csum = at("csum", [8, 1], F32, 181504)
    ebrow = at("ebrow", [8, 1], F32, 181536)
    ebt = at("ebt", [8, 1], F32, 181568)
    wstage = at("wstage", [128, 2048], F32, 186880)
    wstgb = at("wstgb", [128, 2048], BF16, 195072)
    pt = [at("pt0", [128, 1536], BF16, 199424),
          at("pt1", [128, 1536], BF16, 202496)]
    rcbc = at("rcbc", [128, 2048], BF16, 205568)
    tri_s = at("tri_s", [128, 128], BF16, 209664)
    eyeb_s = at("eyeb_s", [128, 128], BF16, 209920)
    ebias_bc = at("ebias_bc", [128, 8], F32, 210176)
    bq8 = at("bq8", [128, 4], F32, 210208)
    bk_s = at("bk_s", [128, 4], F32, 210240)
    pb_s = at("pb_s", [1, 1], I32, 210272)
    bvbc = at("bvbc", [128, 512], BF16, 210304)
    stat = at("stat", [128, 128], F32, 211328)
    ones64 = at("ones64", [128, 64], F32, 211840)
    wsc = at("wsc", [1, 16], F32, 212096)
    wsc2 = at("wsc2", [128, 4], F32, 212160)
    vfill = at("vfill", [128, 64], F32, 181632)

    psum_ctx = [nc.psum_tensor("sh0", [128, 1536], F32),
                nc.psum_tensor("sh1", [128, 1536], F32),
                nc.psum_tensor("otb", [128, 512], F32),
                nc.psum_tensor("aux", [128, 512], F32)]
    sh0, sh1, otb, aux = [c.__enter__().ap() for c in psum_ctx]
    SHP = [sh0, sh1]

    ctx = nc.Block()
    block = ctx.__enter__()
    semctx = [nc.semaphore(n) for n in
              ["s_ld", "s_gd", "s_t", "s_v", "s_s", "s_cc", "s_out"]]
    s_ld, s_gd, s_t, s_v, s_s, s_cc, s_out = [c.__enter__() for c in semctx]
    cnt = dict(ld=0, gd=0, t=0, v=0, s=0, cc=0, out=0)

    stopped = [False]

    def VF(e, n=2):
        li = None
        for _ in range(n):
            li = e.tensor_scalar(out=vfill, in0=vfill, scalar1=1.0,
                                 scalar2=None, op0=ALU.mult)
        return li

    def SY(fn):
        if stopped[0]:
            return
        @block.sync
        def _(e):
            fn(e)

    def GP(fn):
        if stopped[0]:
            return
        @block.gpsimd
        def _(e):
            fn(e)

    def TE(fn):
        if stopped[0]:
            return
        @block.tensor
        def _(e):
            fn(e)

    def VE(fn):
        if stopped[0]:
            return
        @block.vector
        def _(e):
            fn(e)

    def SE(fn):
        if stopped[0]:
            return
        @block.scalar
        def _(e):
            fn(e)

    def phase(name):
        if stopped[0] or upto is None or name != upto:
            return
        def drain(e):
            for sem, key in [(s_ld, "ld"), (s_gd, "gd"), (s_t, "t"),
                             (s_v, "v"), (s_s, "s"), (s_cc, "cc")]:
                e.wait_ge(sem, cnt[key])
            e.memset(ostg[0], 0.0)
            e.dma_start(out=bass.AP(out_d.tensor, 0, [[C, 128], [1, C]]),
                        in_=ostg[0]).then_inc(s_out, 16)
            cnt["out"] += 16
            e.wait_ge(s_out, cnt["out"])
        GP(drain)
        stopped[0] = True

    def p0_sync(e):
        for dst, src in [(pb_s, psel), (tri_s, tri_in), (eyef_s, eyef_in),
                         (eyeb_s, eyeb_in), (pslab_s, pos_slab)]:
            e.dma_start(out=dst, in_=src).then_inc(s_ld, 16)
            cnt["ld"] += 16
        for dst, src in [(bq8, bq_in), (bk_s, bk_in)]:
            e.dma_start(out=dst,
                        in_=bass.AP(src.tensor, 0, [[1, 128], [128, 4]]),
                        allow_slow_non_contiguous=True).then_inc(s_ld, 16)
            cnt["ld"] += 16
        e.dma_start(out=ebrow,
                    in_=bass.AP(ebias_in.tensor, 0, [[1, 8], [1, 1]])
                    ).then_inc(s_ld, 16)
        cnt["ld"] += 16
        e.dma_start(out=wstage[:, 0:512],
                    in_=bass.AP(bv_in.tensor, 0, [[0, 128], [1, 512]])
                    ).then_inc(s_ld, 16)
        cnt["ld"] += 16
    SY(p0_sync)
    ld_p0 = cnt["ld"]

    def p0_vec(e):
        e.wait_ge(s_ld, ld_p0)
        e.reduce_sum(out=csum, in_=pslab_s, axis=AX.X)
        e.memset(bass.AP(vbuf.tensor, 64,
                         [[520 * 32, 128], [520, 32], [65, 8], [1, 1]]), 1.0)
        e.memset(ones64, 1.0)
        e.tensor_tensor(out=ebt, in0=ebrow, in1=csum, op=ALU.add)
        e.memset(stat, 0.0)
        e.memset(wsc, 0.0)
        e.tensor_tensor(out=ebrow, in0=ebt, in1=csum, op=ALU.subtract)
        e.memset(vfill, 0.0)
        e.tensor_scalar(out=bq8, in0=bq8, scalar1=0.125, scalar2=None,
                        op0=ALU.mult)
        e.tensor_copy(out=bvbc, in_=wstage[:, 0:512]).then_inc(s_v, 1)
        cnt["v"] += 1
    VE(p0_vec)
    v_p0 = cnt["v"]

    def p0_gp(e):
        e.wait_ge(s_v, v_p0)
        e.dma_start(out=scr8,
                    in_=bass.AP(ebrow.tensor, 0, [[1, 8], [1, 1]])
                    ).then_inc(s_gd, 16)
        cnt["gd"] += 16
        e.wait_ge(s_gd, cnt["gd"])
        for h in range(8):
            e.dma_start(out=ebias_bc[:, h:h + 1],
                        in_=bass.AP(scr8.tensor, h, [[0, 128], [1, 1]])
                        ).then_inc(s_gd, 16)
            cnt["gd"] += 16
    GP(p0_gp)
    phase("p0")

    xck_evac = [0] * 16
    for ck in range(16):
        def x_ld(e, ck=ck):
            if ck == 0:
                e.wait_ge(s_v, v_p0)
            else:
                e.wait_ge(s_v, xck_evac[ck - 1])
            e.dma_start(out=wstage,
                        in_=bass.AP(x_full.tensor, (256 * ck) * C,
                                    [[C, 128], [128 * C, 2], [1, C]])
                        ).then_inc(s_ld, 16)
            cnt["ld"] += 16
        SY(x_ld)
        ldk = cnt["ld"]

        def x_tp(e, ck=ck, ldk=ldk):
            e.wait_ge(s_ld, ldk)
            if ck >= 1:
                e.wait_ge(s_v, xck_evac[ck - 1])
            li = None
            for a in range(2):
                bank = sh0 if a == 0 else sh1
                for c8 in range(8):
                    li = e.transpose(out=bass.AP(bank.tensor, 128 * c8,
                                                 [[1536, 128], [1, 128]]),
                                     in_=wstage[:, 1024 * a + 128 * c8:
                                                1024 * a + 128 * c8 + 128],
                                     identity=eyef_s)
            li.then_inc(s_t, 1)
            cnt["t"] += 1
        TE(x_tp)
        t_tp = cnt["t"]

        def x_ev(e, ck=ck, t_tp=t_tp):
            e.wait_ge(s_t, t_tp)
            li = None
            for a in range(2):
                bank = sh0 if a == 0 else sh1
                for c8 in range(8):
                    li = e.tensor_copy(
                        out=xT[:, T * c8 + 256 * ck + 128 * a:
                               T * c8 + 256 * ck + 128 * a + 128],
                        in_=bank[:, 128 * c8:128 * c8 + 128])
            li.then_inc(s_v, 1)
            cnt["v"] += 1
        VE(x_ev)
        xck_evac[ck] = cnt["v"]
    phase("xT")

    wv_cast_v = [0, 0]
    for half in range(2):
        def wv_ld(e, half=half):
            e.wait_ge(s_v, xck_evac[15] if half == 0 else wv_cast_v[0])
            e.dma_start(out=wstage,
                        in_=bass.AP(wv_t.tensor, 256 * half,
                                    [[512, 128], [512 * 128, 8], [1, 256]])
                        ).then_inc(s_ld, 16)
            cnt["ld"] += 16
        SY(wv_ld)
        ldw = cnt["ld"]

        def wv_c(e, half=half, ldw=ldw):
            e.wait_ge(s_ld, ldw)
            e.tensor_copy(out=bass.AP(wvb.tensor, 256 * half,
                                      [[FF, 128], [512, 8], [1, 256]]),
                          in_=wstage).then_inc(s_v, 1)
            cnt["v"] += 1
        VE(wv_c)
        wv_cast_v[half] = cnt["v"]
    v_wvb = cnt["v"]

    vm_evac = [0] * 32
    for tt in range(32):
        def vmm(e, tt=tt):
            e.wait_ge(s_v, v_wvb)
            if tt >= 2:
                e.wait_ge(s_v, vm_evac[tt - 2])
            bank = otb if tt % 2 == 0 else aux
            for c8 in range(8):
                sl = xT[:, T * c8 + 128 * tt: T * c8 + 128 * tt + 128]
                e.ldweights(sl)
                li = e.matmul(out=bank, lhsT=sl,
                              rhs=wvb[:, 512 * c8:512 * c8 + 512],
                              start=(c8 == 0), stop=(c8 == 7))
            li.then_inc(s_t, 1)
            cnt["t"] += 1
        TE(vmm)
        t_vm = cnt["t"]

        def vev(e, tt=tt, t_vm=t_vm):
            e.wait_ge(s_t, t_vm)
            bank = otb if tt % 2 == 0 else aux
            e.scalar_tensor_tensor(
                out=bass.AP(vbuf.tensor, 520 * tt,
                            [[520 * 32, 128], [65, 8], [1, 64]]),
                in0=bass.AP(bank.tensor, 0, [[512, 128], [64, 8], [1, 64]]),
                scalar=1.0,
                in1=bass.AP(bvbc.tensor, 0, [[512, 128], [64, 8], [1, 64]]),
                op0=ALU.mult, op1=ALU.add).then_inc(s_v, 1)
            cnt["v"] += 1
        VE(vev)
        vm_evac[tt] = cnt["v"]
    phase("vmm")

    kq_ready = [0] * 8
    kq_aug_ld = [0] * 8
    exp_cp = {}
    mask_cp = {}
    pv_cp = {}
    sc_cp = {}
    head_last_sc = [0] * 8
    head_norm_v = [0] * 8
    stage_free = {"s": 0, "v": 0, "wq": 0}

    def emit_pair_fill(hp, guard_t):
        b0, b1 = (2 * hp) % 3, (2 * hp + 1) % 3

        def w_ld(e, hp=hp):
            if hp == 0:
                e.wait_ge(s_v, v_wvb)
            else:
                e.wait_ge(s_v, stage_free["v"])
            e.dma_start(out=wstage[:, 0:1024],
                        in_=bass.AP(wk_t.tensor, 128 * hp,
                                    [[512, 128], [512 * 128, 8], [1, 128]])
                        ).then_inc(s_ld, 16)
            e.dma_start(out=wstage[:, 1024:2048],
                        in_=bass.AP(wq_t.tensor, 128 * hp,
                                    [[512, 128], [512 * 128, 8], [1, 128]])
                        ).then_inc(s_ld, 16)
            cnt["ld"] += 32
        SY(w_ld)
        ld_w = cnt["ld"]

        def w_cast(e, ld_w=ld_w):
            e.wait_ge(s_ld, ld_w)
            e.tensor_copy(out=wstgb, in_=wstage).then_inc(s_v, 1)
            cnt["v"] += 1
        VE(w_cast)
        v_cast = cnt["v"]

        def aug_ld(e, hp=hp, b0=b0, b1=b1, guard_t=guard_t):
            e.wait_ge(s_t, guard_t)
            for h_i, (kb, qb) in enumerate([(kaug[b0], qaug[b0]),
                                            (kaug[b1], qaug[b1])]):
                h = 2 * hp + h_i
                ab = 64 if h % 2 == 0 else 0
                e.dma_start(out=kb[ab:ab + 64, :], in_=psi_in[h, :, :]
                            ).then_inc(s_ld, 16)
                e.dma_start(out=qb[ab:ab + 64, :], in_=phi_in[h, :, :]
                            ).then_inc(s_ld, 16)
                cnt["ld"] += 32
        SY(aug_ld)
        ld_aug = cnt["ld"]

        evk = []
        for which in range(2):
            wbase = 1024 * which
            for st in range(8):
                def kqmm(e, which=which, wbase=wbase, st=st, v_cast=v_cast,
                         guard_t=guard_t):
                    e.wait_ge(s_v, v_cast)
                    e.wait_ge(s_t, guard_t)
                    if evk:
                        e.wait_ge(s_s, evk[-1])
                    for c8 in range(8):
                        sl = wstgb[:, wbase + 128 * c8: wbase + 128 * c8 + 128]
                        e.ldweights(sl)
                        li = e.matmul(out=aux, lhsT=sl,
                                      rhs=xT[:, T * c8 + 512 * st:
                                             T * c8 + 512 * st + 512],
                                      start=(c8 == 0), stop=(c8 == 7))
                    li.then_inc(s_t, 1)
                    cnt["t"] += 1
                TE(kqmm)
                t_mm = cnt["t"]

                def kqev(e, which=which, st=st, t_mm=t_mm, hp=hp, b0=b0, b1=b1):
                    e.wait_ge(s_t, t_mm)
                    for h_i in range(2):
                        h = 2 * hp + h_i
                        dst = (kaug if which == 0 else qaug)[b0 if h_i == 0 else b1]
                        db = 0 if h % 2 == 0 else 64
                        src = aux[64 * h_i:64 * h_i + 64, :]
                        if which == 0:
                            li = e.activation(
                                out=dst[db:db + 64, 512 * st:512 * st + 512],
                                in_=src, func=AF.Identity,
                                bias=bk_s[64 * h_i:64 * h_i + 64, hp:hp + 1])
                        else:
                            li = e.activation(
                                out=dst[db:db + 64, 512 * st:512 * st + 512],
                                in_=src, func=AF.Identity,
                                bias=bq8[64 * h_i:64 * h_i + 64, hp:hp + 1],
                                scale=np.float32(0.125))
                    li.then_inc(s_s, 1)
                    cnt["s"] += 1
                SE(kqev)
                evk.append(cnt["s"])
        stage_free["s"] = cnt["s"]
        stage_free["v"] = v_cast
        for h_i in range(2):
            kq_ready[2 * hp + h_i] = cnt["s"]
            kq_aug_ld[2 * hp + h_i] = ld_aug

    def emit_head(h):
        Ka, Qa = kaug[h % 3], qaug[h % 3]
        items = []
        for st in range(8):
            ntk = 4 * st + 4
            grps = _groups(ntk)
            for g, kjs in enumerate(grps):
                items.append((st, g, kjs, ntk, len(grps)))

        def emit_pv(pend):
            st, g, kjs, ntk, glen, par = pend

            def t_pv(e, h=h, st=st, g=g, kjs=kjs, par=par, ntk=ntk):
                e.wait_ge(s_v, mask_cp[(h, st, g)])
                if g == 0 and ("otfree",) in pv_cp:
                    e.wait_ge(s_v, pv_cp[("otfree",)])
                for i, kj in enumerate(kjs):
                    c0 = max(0, kj - 4 * st) * 128
                    vsl = vbuf[:, 520 * kj + 65 * h: 520 * kj + 65 * h + 65]
                    e.ldweights(vsl)
                    li = e.matmul(out=otb[0:65, c0:512], lhsT=vsl,
                                  rhs=pt[par][:, 512 * i + c0:512 * i + 512],
                                  start=(kj == 0), stop=(kj == ntk - 1))
                li.then_inc(s_t, 1)
                cnt["t"] += 1
            TE(t_pv)
            pv_cp[(h, st, g)] = cnt["t"]
            pv_cp[("ptfree", par)] = cnt["t"]

            if g != glen - 1:
                return

            def v_strip(e, h=h, st=st, ngr=glen):
                e.wait_ge(s_t, pv_cp[(h, st, ngr - 1)])
                lo = 64 * (h % 2)
                e.tensor_copy(out=onorm[lo:lo + 64,
                                        T * (h // 2) + 512 * st:
                                        T * (h // 2) + 512 * st + 512],
                              in_=otb[0:64, :])
                e.tensor_copy(out=rcbc[0:1, 512 * (st % 4):512 * (st % 4) + 512],
                              in_=otb[64:65, :]).then_inc(s_v, 1)
                cnt["v"] += 1
            VE(v_strip)
            pv_cp[("otfree",)] = cnt["v"]

            if st in (3, 7):
                hh = (st - 3) // 4
                v_sums = cnt["v"]

                def g_bc(e, v_sums=v_sums):
                    e.wait_ge(s_v, v_sums)
                    e.dma_start(out=scr_rc, in_=rcbc[0:1, :]).then_inc(s_gd, 16)
                    cnt["gd"] += 16
                    e.wait_ge(s_gd, cnt["gd"])
                    e.dma_start(out=rcbc,
                                in_=bass.AP(scr_rc.tensor, 0,
                                            [[0, 128], [1, 2048]])
                                ).then_inc(s_gd, 16)
                    cnt["gd"] += 16
                GP(g_bc)
                gd_rc = cnt["gd"]

                def v_norm(e, h=h, hh=hh, gd_rc=gd_rc):
                    e.wait_ge(s_gd, gd_rc)
                    with nc.allow_low_precision(reason="bf16 softmax sums"):
                        e.reciprocal(out=rcbc, in_=rcbc)
                    lo = 64 * (h % 2)
                    sl = slice(T * (h // 2) + 2048 * hh,
                               T * (h // 2) + 2048 * hh + 2048)
                    e.tensor_tensor(out=onorm[lo:lo + 64, sl],
                                    in0=onorm[lo:lo + 64, sl],
                                    in1=rcbc[lo:lo + 64, :],
                                    op=ALU.mult).then_inc(s_v, 1)
                    cnt["v"] += 1
                VE(v_norm)

        pend = None
        for j, (st, g, kjs, ntk, glen) in enumerate(items):
            par = j % 2

            def t_sc(e, h=h, st=st, g=g, kjs=kjs, par=par):
                e.wait_ge(s_s, kq_ready[h])
                e.wait_ge(s_ld, kq_aug_ld[h])
                key = ("shfree", par)
                if key in exp_cp:
                    e.wait_ge(s_s, exp_cp[key])
                for i, kj in enumerate(kjs):
                    sl = Ka[:, 128 * kj:128 * kj + 128]
                    e.ldweights(sl)
                    li = e.matmul(out=SHP[par][:, 512 * i:512 * i + 512],
                                  lhsT=sl,
                                  rhs=Qa[:, 512 * st:512 * st + 512],
                                  start=True, stop=True)
                li.then_inc(s_t, 1)
                cnt["t"] += 1
            TE(t_sc)
            sc_cp[(h, st, g)] = cnt["t"]

            if pend is not None:
                emit_pv(pend)

            def s_exp(e, h=h, st=st, g=g, kjs=kjs, par=par):
                e.wait_ge(s_t, sc_cp[(h, st, g)])
                key = ("ptfree", par)
                if key in pv_cp:
                    e.wait_ge(s_t, pv_cp[key])
                w = 512 * len(kjs)
                e.activation(out=pt[par][:, 0:w], in_=SHP[par][:, 0:w],
                             func=AF.Exp,
                             bias=ebias_bc[:, h:h + 1]).then_inc(s_s, 1)
                cnt["s"] += 1
            SE(s_exp)
            exp_cp[(h, st, g)] = cnt["s"]
            exp_cp[("shfree", par)] = cnt["s"]

            def v_mask(e, h=h, st=st, g=g, kjs=kjs, par=par):
                e.wait_ge(s_s, exp_cp[(h, st, g)])
                li = None
                for i, kj in enumerate(kjs):
                    d = kj - 4 * st
                    if 0 <= d <= 3:
                        sl = slice(512 * i + 128 * d, 512 * i + 128 * d + 128)
                        li = e.tensor_tensor(out=pt[par][:, sl],
                                             in0=pt[par][:, sl],
                                             in1=tri_s, op=ALU.mult)
                if li is not None:
                    li.then_inc(s_v, 1)
                else:
                    e.nop().then_inc(s_v, 1)
                cnt["v"] += 1
            VE(v_mask)
            mask_cp[(h, st, g)] = cnt["v"]
            pend = (st, g, kjs, ntk, glen, par)
        emit_pv(pend)
        head_last_sc[h] = sc_cp[(h, 7, len(_groups(32)) - 1)]
        head_norm_v[h] = cnt["v"]

    def emit_wquant(w_in, wrows, wdst, wcols, m_col, race_t):
        nchunks = wrows // 128
        spl = wcols // 1024
        tw = 1024
        tiles = [(c, j) for c in range(nchunks) for j in range(spl)]
        absv = []
        for idx, (c, j) in enumerate(tiles):
            half = 1024 * (idx % 2)
            lsem = s_ld if idx % 2 == 0 else s_gd
            lkey = "ld" if idx % 2 == 0 else "gd"

            def wl(e, c=c, j=j, idx=idx, half=half, lsem=lsem, lkey=lkey):
                if idx < 2:
                    e.wait_ge(s_v, stage_free["v"])
                    e.wait_ge(s_v, stage_free["wq"])
                else:
                    e.wait_ge(s_v, absv[idx - 2])
                e.dma_start(out=wstage[:, half:half + tw],
                            in_=bass.AP(w_in.tensor, (128 * c) * wcols + tw * j,
                                        [[wcols, 128], [1, tw]])
                            ).then_inc(lsem, 16)
                cnt[lkey] += 16
            (SY if idx % 2 == 0 else GP)(wl)
            ldw = cnt[lkey]

            def wa(e, idx=idx, ldw=ldw, half=half, lsem=lsem):
                e.wait_ge(lsem, ldw)
                e.reduce_sum(out=stat[:, 64 + idx:65 + idx],
                             in_=wstage[:, half:half + tw], axis=AX.X,
                             apply_absolute_value=True).then_inc(s_v, 1)
                cnt["v"] += 1
            VE(wa)
            absv.append(cnt["v"])
        ntile = len(tiles)

        def w_zero(e, ntile=ntile):
            e.wait_ge(s_v, absv[-1])
            if ntile < 64:
                e.memset(stat[:, 64 + ntile:128], 0.0).then_inc(s_v, 1)
            cnt["v"] += 1
        VE(w_zero)
        v_z = cnt["v"]

        def w_mm(e, v_z=v_z):
            e.wait_ge(s_v, v_z)
            e.wait_ge(s_s, stage_free["s"])
            e.matmul(out=aux[0:64, 0:64], lhsT=ones64, rhs=stat[:, 64:128],
                     start=True, stop=True).then_inc(s_t, 1)
            cnt["t"] += 1
        TE(w_mm)
        t_mm = cnt["t"]

        def w_chain(e, t_mm=t_mm, m_col=m_col):
            e.wait_ge(s_t, t_mm)
            e.reduce_sum(out=wsc[0:1, m_col:m_col + 1], in_=aux[0:1, 0:64],
                         axis=AX.X)
            VF(e)
            e.tensor_scalar(out=wsc[0:1, m_col:m_col + 1],
                            in0=wsc[0:1, m_col:m_col + 1],
                            scalar1=float(np.float32(1.0 / (wrows * wcols))), scalar2=1e-5,
                            op0=ALU.mult, op1=ALU.max)
            VF(e)
            e.reciprocal(out=wsc[0:1, m_col + 1:m_col + 2],
                         in_=wsc[0:1, m_col:m_col + 1]).then_inc(s_v, 1)
            cnt["v"] += 1
        VE(w_chain)
        v_ch = cnt["v"]

        def g_wbc(e, v_ch=v_ch, m_col=m_col):
            e.wait_ge(s_v, v_ch)
            e.dma_start(out=scr1, in_=wsc[0:1, m_col:m_col + 4]
                        ).then_inc(s_gd, 16)
            cnt["gd"] += 16
            e.wait_ge(s_gd, cnt["gd"])
            for k in range(2):
                e.dma_start(out=wsc2[:, m_col + k:m_col + k + 1],
                            in_=bass.AP(scr1.tensor, k, [[0, 128], [1, 1]])
                            ).then_inc(s_gd, 16)
                cnt["gd"] += 16
        GP(g_wbc)
        gd_bc = cnt["gd"]

        rl = []
        for idx, (c, j) in enumerate(tiles):
            half = 1024 * (idx % 2)
            lsem = s_ld if idx % 2 == 0 else s_gd
            lkey = "ld" if idx % 2 == 0 else "gd"

            def wl2(e, c=c, j=j, idx=idx, half=half, lsem=lsem, lkey=lkey):
                if len(rl) >= 2:
                    e.wait_ge(s_v, rl[-2])
                else:
                    e.wait_ge(s_gd, gd_bc)
                e.dma_start(out=wstage[:, half:half + tw],
                            in_=bass.AP(w_in.tensor, (128 * c) * wcols + tw * j,
                                        [[wcols, 128], [1, tw]])
                            ).then_inc(lsem, 16)
                cnt[lkey] += 16
            (SY if idx % 2 == 0 else GP)(wl2)
            ldw = cnt[lkey]

            def wr(e, c=c, j=j, ldw=ldw, m_col=m_col, idx=idx, half=half,
                   lsem=lsem):
                e.wait_ge(lsem, ldw)
                if idx == 0:
                    e.wait_ge(s_gd, gd_bc)
                    e.wait_ge(s_t, race_t)
                e.tensor_scalar(out=wi32,
                                in0=wstage[:, half:half + tw],
                                scalar1=wsc2[:, m_col + 1:m_col + 2],
                                scalar2=None, op0=ALU.mult)
                li = e.tensor_scalar(
                    out=wdst[:, wcols * c + tw * j:wcols * c + tw * j + tw],
                    in0=wi32, scalar1=-1.0, scalar2=1.0,
                    op0=ALU.max, op1=ALU.min)
                li.then_inc(s_v, 1)
                cnt["v"] += 1
            VE(wr)
            rl.append(cnt["v"])
        stage_free["v"] = cnt["v"]
        stage_free["wq"] = cnt["v"]
        return cnt["v"]

    emit_pair_fill(0, guard_t=0)
    phase("fill0")
    emit_head(0)
    phase("head0")
    emit_pair_fill(1, guard_t=head_last_sc[0])
    emit_head(1)
    emit_head(2)
    emit_pair_fill(2, guard_t=head_last_sc[2])
    emit_head(3)
    phase("head3")
    emit_head(4)
    emit_pair_fill(3, guard_t=head_last_sc[4])
    emit_head(5)
    phase("head5")
    v_w1_done = emit_wquant(w1_t, C, w1q, FF, 0, race_t=head_last_sc[5])
    phase("w1q")
    emit_head(6)
    v_w2_done = emit_wquant(w2_t, FF, w2q, C, 2, race_t=head_last_sc[6])
    emit_head(7)
    phase("attn")

    def g_exch(e):
        for h in range(8):
            e.wait_ge(s_v, head_norm_v[h])
        with e.register("rp") as rp:
            e.reg_load(rp, pb_s[0:1, 0:1])
            with e.If_eq(rp, 0):
                for u in range(4):
                    e.dma_start(out=gift[:, 2048 * u:2048 * u + 2048],
                                in_=onorm[:, T * u + 2048:T * u + 4096]
                                ).then_inc(s_gd, 16)
            with e.Else():
                for u in range(4):
                    e.dma_start(out=gift[:, 2048 * u:2048 * u + 2048],
                                in_=onorm[:, T * u:T * u + 2048]
                                ).then_inc(s_gd, 16)
            cnt["gd"] += 64
            e.wait_ge(s_gd, cnt["gd"])
            e.collective_compute(
                "AllGather", ALU.bypass,
                replica_groups=[[0, 1], [2, 3], [4, 5], [6, 7]],
                ins=[gift], outs=[gath]).then_inc(s_cc, 1)
            cnt["cc"] += 1
            e.wait_ge(s_cc, cnt["cc"])
            with e.If_eq(rp, 0):
                for u in range(4):
                    e.dma_start(out=pstage[:, 2048 * u:2048 * u + 2048],
                                in_=onorm[:, T * u:T * u + 2048]
                                ).then_inc(s_gd, 16)
                e.dma_start(out=peerO, in_=gath[1]).then_inc(s_gd, 16)
            with e.Else():
                for u in range(4):
                    e.dma_start(out=pstage[:, 2048 * u:2048 * u + 2048],
                                in_=onorm[:, T * u + 2048:T * u + 4096]
                                ).then_inc(s_gd, 16)
                e.dma_start(out=peerO, in_=gath[0]).then_inc(s_gd, 16)
            cnt["gd"] += 80
    GP(g_exch)
    gd_exch = cnt["gd"]
    phase("exch")

    def bp_ld(e):
        e.wait_ge(s_v, v_w2_done)
        e.dma_start(out=wstage[:, 0:1024],
                    in_=bass.AP(bproj_in.tensor, 0, [[0, 128], [1, C]])
                    ).then_inc(s_ld, 16)
        cnt["ld"] += 16
    SY(bp_ld)
    ld_bp = cnt["ld"]

    def bp_c(e, ld_bp=ld_bp):
        e.wait_ge(s_ld, ld_bp)
        e.tensor_scalar(out=wsc2[:, 3:4], in0=wsc2[:, 2:3],
                        scalar1=float(np.float32(1.0 / 127.0)), scalar2=None,
                        op0=ALU.mult)
        e.tensor_copy(out=bpbc, in_=wstage[:, 0:1024]).then_inc(s_v, 1)
        cnt["v"] += 1
    VE(bp_c)
    v_bp = cnt["v"]

    wp_cast = [0] * 4
    for q4 in range(4):
        def wp_ld(e, q4=q4):
            e.wait_ge(s_v, v_bp if q4 == 0 else wp_cast[q4 - 1])
            if q4 == 0:
                e.wait_ge(s_gd, gd_exch)
            e.dma_start(out=wstage,
                        in_=bass.AP(wproj_t.tensor, 256 * q4,
                                    [[C, 128], [128 * C, 8], [1, 256]])
                        ).then_inc(s_ld, 16)
            cnt["ld"] += 16
        SY(wp_ld)
        ldw = cnt["ld"]

        def wp_c(e, q4=q4, ldw=ldw):
            e.wait_ge(s_ld, ldw)
            if q4 == 0:
                e.wait_ge(s_gd, gd_exch)
            e.tensor_copy(out=bass.AP(wpb.tensor, 256 * q4,
                                      [[8 * C, 128], [C, 8], [1, 256]]),
                          in_=wstage).then_inc(s_v, 1)
            cnt["v"] += 1
        VE(wp_c)
        wp_cast[q4] = cnt["v"]
    v_wp = cnt["v"]
    phase("wp")

    proj_done_v = [0] * NTT
    for tt in range(NTT):
        def xr_ld(e, tt=tt):
            if tt >= 2:
                e.wait_ge(s_v, proj_done_v[tt - 2])
            e.dma_start(out=xres[tt % 2],
                        in_=bass.AP(x_own.tensor, 128 * tt * C,
                                    [[C, 128], [1, C]])).then_inc(s_gd, 16)
            cnt["gd"] += 16
        GP(xr_ld)
        ld_xr = cnt["gd"]

        def pmm(e, tt=tt):
            e.wait_ge(s_v, v_wp)
            e.wait_ge(s_gd, gd_exch)
            if tt >= 2:
                e.wait_ge(s_v, proj_done_v[tt - 2])
            bset = sh0 if tt % 2 == 0 else sh1
            for dgrp in range(2):
                bank = bset[:, 512 * dgrp:512 * dgrp + 512]
                for ci in range(8):
                    if ci < 4:
                        sl = pstage[:, 2048 * ci + 128 * tt:
                                    2048 * ci + 128 * tt + 128]
                    else:
                        sl = peerO[:, 2048 * (ci - 4) + 128 * tt:
                                   2048 * (ci - 4) + 128 * tt + 128]
                    e.ldweights(sl)
                    li = e.matmul(out=bank, lhsT=sl,
                                  rhs=bass.AP(wpb.tensor, C * ci + 512 * dgrp,
                                              [[8 * C, 128], [1, 512]]),
                                  start=(ci == 0), stop=(ci == 7))
            li.then_inc(s_t, 1)
            cnt["t"] += 1
        TE(pmm)
        t_p = cnt["t"]

        def pev(e, tt=tt, t_p=t_p, ld_xr=ld_xr):
            e.wait_ge(s_t, t_p)
            e.wait_ge(s_gd, ld_xr)
            bset = sh0 if tt % 2 == 0 else sh1
            for dgrp in range(2):
                e.scalar_tensor_tensor(
                    out=tscr[:, 512 * dgrp:512 * dgrp + 512],
                    in0=bset[:, 512 * dgrp:512 * dgrp + 512], scalar=1.0,
                    in1=bpbc[:, 512 * dgrp:512 * dgrp + 512],
                    op0=ALU.mult, op1=ALU.add)
            e.tensor_tensor(out=sbf[:, C * tt:C * tt + C], in0=tscr,
                            in1=xres[tt % 2], op=ALU.add).then_inc(s_v, 1)
            cnt["v"] += 1
        VE(pev)
        proj_done_v[tt] = cnt["v"]
    v_proj = cnt["v"]

    phase("proj")
    for c in reversed(psum_ctx):
        c.__exit__(None, None, None)
    psum_ctx2 = [nc.psum_tensor("f1b", [128, 1536], F32),
                 nc.psum_tensor("f2b", [128, 1536], F32),
                 nc.psum_tensor("tpb", [128, 1024], BF16),
                 nc.psum_tensor("tpb2", [128, 1024], BF16)]
    f1b, f2b, tpb, tpb2 = [c.__enter__().ap() for c in psum_ctx2]

    b_v = [0, 0, 0]
    for bi, (src, dst, width) in enumerate(
            [(b1_in, b1bc[:, 0:2048], 2048),
             (None, b1bc[:, 2048:4096], 2048),
             (b2_in, b2bc, C)]):
        def bb_ld(e, bi=bi, width=width):
            e.wait_ge(s_v, v_proj if bi == 0 else b_v[bi - 1])
            base = 2048 if bi == 1 else 0
            e.dma_start(out=wstage[:, 0:width],
                        in_=bass.AP((b1_in if bi < 2 else b2_in).tensor, base,
                                    [[0, 128], [1, width]])
                        ).then_inc(s_ld, 16)
            cnt["ld"] += 16
        SY(bb_ld)
        ldb = cnt["ld"]

        def bb_c(e, bi=bi, dst=dst, width=width, ldb=ldb):
            e.wait_ge(s_ld, ldb)
            e.tensor_copy(out=dst, in_=wstage[:, 0:width]).then_inc(s_v, 1)
            cnt["v"] += 1
        VE(bb_c)
        b_v[bi] = cnt["v"]
    v_bias = cnt["v"]

    def ln1_stats(e):
        e.wait_ge(s_v, v_bias)
        for tt in range(NTT):
            sl = sbf[:, C * tt:C * tt + C]
            e.reduce_sum(out=stat[:, 16 + tt:17 + tt], in_=sl, axis=AX.X)
            e.scalar_tensor_tensor(out=pt[0][:, 0:C], in0=sl, scalar=1.0,
                                   in1=sl, op0=ALU.mult, op1=ALU.mult,
                                   accum_out=stat[:, tt:tt + 1])
            e.reduce_max(out=stat[:, 96 + tt:97 + tt], in_=sl, axis=AX.X)
            li = e.tensor_reduce(out=stat[:, 112 + tt:113 + tt], in_=sl,
                                 axis=AX.X, op=ALU.min)
        li.then_inc(s_v, 1)
        cnt["v"] += 1
    VE(ln1_stats)
    v_st = cnt["v"]

    def ln1_c1(e, v_st=v_st):
        e.wait_ge(s_v, v_st)
        e.tensor_scalar(out=stat[:, 16:32], in0=stat[:, 16:32],
                        scalar1=float(np.float32(1.0 / C)), scalar2=None, op0=ALU.mult)
        e.tensor_scalar(out=stat[:, 0:16], in0=stat[:, 0:16],
                        scalar1=float(np.float32(1.0 / C)), scalar2=1e-5,
                        op0=ALU.mult, op1=ALU.add)
        e.tensor_tensor(out=stat[:, 112:128], in0=stat[:, 16:32],
                        in1=stat[:, 112:128], op=ALU.subtract)
        e.tensor_tensor(out=stat[:, 96:112], in0=stat[:, 96:112],
                        in1=stat[:, 16:32], op=ALU.subtract)
        e.tensor_tensor(out=stat[:, 80:96], in0=stat[:, 16:32],
                        in1=stat[:, 16:32], op=ALU.mult)
        e.tensor_tensor(out=stat[:, 96:112], in0=stat[:, 96:112],
                        in1=stat[:, 112:128], op=ALU.max)
        e.tensor_tensor(out=stat[:, 0:16], in0=stat[:, 0:16],
                        in1=stat[:, 80:96], op=ALU.subtract
                        ).then_inc(s_v, 1)
        cnt["v"] += 1
    VE(ln1_c1)
    v_c1 = cnt["v"]

    def ln1_s1(e, v_c1=v_c1):
        e.wait_ge(s_v, v_c1)
        e.activation(out=stat[:, 32:48], in_=stat[:, 0:16], func=AF.Sqrt,
                     bias=np.float32(0.0)).then_inc(s_s, 1)
        cnt["s"] += 1
    SE(ln1_s1)
    s_sq1 = cnt["s"]

    def ln1_c2(e, s_sq1=s_sq1):
        e.wait_ge(s_s, s_sq1)
        e.reciprocal(out=stat[:, 32:48], in_=stat[:, 32:48])
        e.tensor_scalar(out=stat[:, 112:128], in0=stat[:, 0:16],
                        scalar1=-1e-5, scalar2=None, op0=ALU.add)
        e.tensor_tensor(out=stat[:, 48:64], in0=stat[:, 16:32],
                        in1=stat[:, 32:48], op=ALU.mult)
        e.reciprocal(out=stat[:, 16:32], in_=stat[:, 0:16])
        e.tensor_scalar(out=stat[:, 48:64], in0=stat[:, 48:64],
                        scalar1=-1.0, scalar2=None, op0=ALU.mult)
        e.tensor_tensor(out=stat[:, 112:128], in0=stat[:, 112:128],
                        in1=stat[:, 16:32], op=ALU.mult
                        ).then_inc(s_v, 1)
        cnt["v"] += 1
    VE(ln1_c2)
    v_c2 = cnt["v"]

    def ln1_s2(e, v_c2=v_c2):
        e.wait_ge(s_v, v_c2)
        e.activation(out=stat[:, 16:32], in_=stat[:, 112:128], func=AF.Sqrt,
                     bias=np.float32(1e-5)).then_inc(s_s, 1)
        cnt["s"] += 1
    SE(ln1_s2)
    s_sq2 = cnt["s"]

    def ln1_c3(e, s_sq2=s_sq2):
        e.wait_ge(s_s, s_sq2)
        e.reciprocal(out=stat[:, 16:32], in_=stat[:, 16:32])
        e.tensor_tensor(out=stat[:, 96:112], in0=stat[:, 96:112],
                        in1=stat[:, 32:48], op=ALU.mult)
        VF(e, 1)
        e.tensor_tensor(out=stat[:, 112:128], in0=stat[:, 96:112],
                        in1=stat[:, 16:32], op=ALU.mult)
        VF(e)
        e.tensor_scalar(out=stat[:, 112:128], in0=stat[:, 112:128],
                        scalar1=1e-5, scalar2=None, op0=ALU.max)
        VF(e, 1)
        e.tensor_scalar(out=stat[:, 80:96], in0=stat[:, 112:128],
                        scalar1=wsc2[:, 0:1], scalar2=float(np.float32(1.0 / 127.0)),
                        op0=ALU.mult, op1=ALU.mult)
        e.reciprocal(out=stat[:, 112:128], in_=stat[:, 112:128])
        VF(e)
        e.tensor_scalar(out=stat[:, 112:128], in0=stat[:, 112:128],
                        scalar1=127.0, scalar2=None, op0=ALU.mult)
        VF(e)
        e.tensor_tensor(out=stat[:, 64:80], in0=stat[:, 16:32],
                        in1=stat[:, 112:128], op=ALU.mult)
        VF(e).then_inc(s_v, 1)
        cnt["v"] += 1
    VE(ln1_c3)
    v_c3 = cnt["v"]

    def ln1_apply(e, v_c3=v_c3):
        e.wait_ge(s_v, v_c3)
        for tt in range(NTT):
            li = e.activation(out=sbf[:, C * tt:C * tt + C],
                              in_=sbf[:, C * tt:C * tt + C], func=AF.Identity,
                              bias=stat[:, 48 + tt:49 + tt],
                              scale=stat[:, 32 + tt:33 + tt])
        li.then_inc(s_s, 1)
        cnt["s"] += 1
    SE(ln1_apply)
    s_y = cnt["s"]
    phase("ln1")

    ffn_v = {}
    ffn_t = {}
    out_sent = []

    for grp4 in range(4):
        for ti in range(4):
            tt = 4 * grp4 + ti

            def xquant(e, tt=tt):
                e.wait_ge(s_s, s_y)
                if ("xqfree",) in ffn_t:
                    e.wait_ge(s_t, ffn_t[("xqfree",)])
                e.tensor_scalar(out=wi32, in0=sbf[:, C * tt:C * tt + C],
                                scalar1=stat[:, 64 + tt:65 + tt],
                                scalar2=None, op0=ALU.mult)
                e.tensor_copy(out=xq, in_=wi32).then_inc(s_v, 1)
                cnt["v"] += 1
            VE(xquant)
            v_xq = cnt["v"]

            def xqt(e, v_xq=v_xq):
                e.wait_ge(s_v, v_xq)
                if ("xqTev",) in ffn_v:
                    e.wait_ge(s_v, ffn_v[("xqTev",)])
                for c8 in range(8):
                    li = e.transpose(out=tpb[:, 128 * c8:128 * c8 + 128],
                                     in_=xq[:, 128 * c8:128 * c8 + 128],
                                     identity=eyeb_s)
                li.then_inc(s_t, 1)
                cnt["t"] += 1
            TE(xqt)
            ffn_t[("xqfree",)] = cnt["t"]
            t_xq = cnt["t"]

            def xqev(e, t_xq=t_xq):
                e.wait_ge(s_t, t_xq)
                e.tensor_copy(out=xqT, in_=tpb).then_inc(s_v, 1)
                cnt["v"] += 1
            VE(xqev)
            ffn_v[("xqTev",)] = cnt["v"]
            v_xqT = cnt["v"]

            for fg in range(8):
                def f1mm(e, tt=tt, fg=fg, v_xqT=v_xqT):
                    e.wait_ge(s_v, v_xqT)
                    e.wait_ge(s_v, v_w1_done)
                    if ("f1ev", fg % 3) in ffn_v:
                        e.wait_ge(s_v, ffn_v[("f1ev", fg % 3)])
                    bank = f1b[:, 512 * (fg % 3):512 * (fg % 3) + 512]
                    for c8 in range(8):
                        sl = xqT[:, 128 * c8:128 * c8 + 128]
                        e.ldweights(sl)
                        li = e.matmul(out=bank, lhsT=sl,
                                      rhs=w1q[:, FF * c8 + 512 * fg:
                                              FF * c8 + 512 * fg + 512],
                                      start=(c8 == 0), stop=(c8 == 7))
                    li.then_inc(s_t, 1)
                    cnt["t"] += 1
                TE(f1mm)
                t_f1 = cnt["t"]

                def f1ev(e, tt=tt, fg=fg, t_f1=t_f1):
                    e.wait_ge(s_t, t_f1)
                    bank = f1b[:, 512 * (fg % 3):512 * (fg % 3) + 512]
                    e.scalar_tensor_tensor(
                        out=hbuf[:, 512 * fg:512 * fg + 512], in0=bank,
                        scalar=stat[:, 80 + tt:81 + tt],
                        in1=b1bc[:, 512 * fg:512 * fg + 512],
                        op0=ALU.mult, op1=ALU.add).then_inc(s_v, 1)
                    cnt["v"] += 1
                VE(f1ev)
                ffn_v[("f1ev", fg % 3)] = cnt["v"]
                v_h = cnt["v"]

                def f1g(e, fg=fg, v_h=v_h):
                    e.wait_ge(s_v, v_h)
                    e.activation(out=hbuf[:, 512 * fg:512 * fg + 512],
                                 in_=hbuf[:, 512 * fg:512 * fg + 512],
                                 func=AF.Gelu,
                                 bias=np.float32(0.0)).then_inc(s_s, 1)
                    cnt["s"] += 1
                SE(f1g)
                ffn_v[("gelu", tt, fg)] = cnt["s"]

            def gsq(e, tt=tt, ti=ti):
                e.wait_ge(s_s, ffn_v[("gelu", tt, 7)])
                if ("qgfree",) in ffn_t:
                    e.wait_ge(s_t, ffn_t[("qgfree",)])
                e.scalar_tensor_tensor(out=qg, in0=hbuf, scalar=1.0,
                                       in1=hbuf, op0=ALU.mult, op1=ALU.mult,
                                       accum_out=stat[:, ti:ti + 1]
                                       ).then_inc(s_v, 1)
                cnt["v"] += 1
            VE(gsq)
            v_sq = cnt["v"]

            def gst(e, ti=ti, v_sq=v_sq):
                e.wait_ge(s_v, v_sq)
                e.reduce_max(out=stat[:, 4 + ti:5 + ti], in_=hbuf, axis=AX.X,
                             apply_absolute_value=True)
                e.tensor_scalar(out=stat[:, ti:ti + 1], in0=stat[:, ti:ti + 1],
                                scalar1=float(np.float32(1.0 / FF)), scalar2=1e-5,
                                op0=ALU.mult, op1=ALU.add)
                VF(e, 1)
                e.tensor_scalar(out=stat[:, 12:13], in0=stat[:, 4 + ti:5 + ti],
                                scalar1=1e-12, scalar2=None, op0=ALU.max)
                VF(e)
                e.reciprocal(out=stat[:, 12:13], in_=stat[:, 12:13])
                VF(e)
                e.tensor_scalar(out=stat[:, 12:13], in0=stat[:, 12:13],
                                scalar1=127.0, scalar2=None,
                                op0=ALU.mult)
                VF(e).then_inc(s_v, 1)
                cnt["v"] += 1
            VE(gst)
            v_gs = cnt["v"]

            def gquant(e, v_gs=v_gs):
                e.wait_ge(s_v, v_gs)
                for q in range(4):
                    e.tensor_scalar(out=wi32,
                                    in0=hbuf[:, 1024 * q:1024 * q + 1024],
                                    scalar1=stat[:, 12:13], scalar2=None,
                                    op0=ALU.mult)
                    li = e.tensor_copy(out=qg[:, 1024 * q:1024 * q + 1024],
                                       in_=wi32)
                li.then_inc(s_v, 1)
                cnt["v"] += 1
            VE(gquant)
            v_qg = cnt["v"]

            for sub in range(2):
                def qgt(e, sub=sub, v_qg=v_qg):
                    e.wait_ge(s_v, v_qg)
                    if ("qgTev",) in ffn_v:
                        e.wait_ge(s_v, ffn_v[("qgTev",)])
                    for k16 in range(16):
                        bank = tpb if k16 < 8 else tpb2
                        li = e.transpose(out=bank[:, 128 * (k16 % 8):
                                                  128 * (k16 % 8) + 128],
                                         in_=qg[:, 2048 * sub + 128 * k16:
                                                2048 * sub + 128 * k16 + 128],
                                         identity=eyeb_s)
                    li.then_inc(s_t, 1)
                    cnt["t"] += 1
                TE(qgt)
                ffn_t[("qgfree",)] = cnt["t"]
                tq = cnt["t"]

                def qgev(e, sub=sub, tq=tq):
                    e.wait_ge(s_t, tq)
                    e.tensor_copy(out=qgT[:, 2048 * sub:2048 * sub + 1024],
                                  in_=tpb)
                    e.tensor_copy(out=qgT[:, 2048 * sub + 1024:2048 * sub + 2048],
                                  in_=tpb2).then_inc(s_v, 1)
                    cnt["v"] += 1
                VE(qgev)
                ffn_v[("qgTev",)] = cnt["v"]
            v_qgT = cnt["v"]

            for dg in range(2):
                def f2mm(e, tt=tt, dg=dg, v_qgT=v_qgT):
                    e.wait_ge(s_v, v_qgT)
                    e.wait_ge(s_v, v_w2_done)
                    if ("f2ev", dg) in ffn_v:
                        e.wait_ge(s_v, ffn_v[("f2ev", dg)])
                    bank = f2b[:, 512 * dg:512 * dg + 512]
                    for c32 in range(32):
                        sl = qgT[:, 128 * c32:128 * c32 + 128]
                        e.ldweights(sl)
                        li = e.matmul(out=bank, lhsT=sl,
                                      rhs=w2q[:, C * c32 + 512 * dg:
                                              C * c32 + 512 * dg + 512],
                                      start=(c32 == 0), stop=(c32 == 31))
                    li.then_inc(s_t, 1)
                    cnt["t"] += 1
                TE(f2mm)
                t_f2 = cnt["t"]

                def f2ev(e, tt=tt, ti=ti, dg=dg, t_f2=t_f2):
                    e.wait_ge(s_t, t_f2)
                    e.tensor_copy(out=h2raw[:, C * ti + 512 * dg:
                                            C * ti + 512 * dg + 512],
                                  in_=f2b[:, 512 * dg:512 * dg + 512]
                                  ).then_inc(s_v, 1)
                    cnt["v"] += 1
                VE(f2ev)
                ffn_v[("f2ev", dg)] = cnt["v"]
        v_h2 = cnt["v"]

        def ep_s1(e, v_h2=v_h2):
            e.wait_ge(s_v, v_h2)
            e.activation(out=stat[:, 8:12], in_=stat[:, 0:4], func=AF.Sqrt,
                         bias=np.float32(0.0)).then_inc(s_s, 1)
            cnt["s"] += 1
        SE(ep_s1)
        s_e1 = cnt["s"]

        def ep_v1(e, s_e1=s_e1):
            e.wait_ge(s_s, s_e1)
            e.reciprocal(out=stat[:, 8:12], in_=stat[:, 8:12])
            VF(e)
            e.tensor_tensor(out=stat[:, 8:12], in0=stat[:, 8:12],
                            in1=stat[:, 4:8], op=ALU.mult)
            VF(e)
            e.tensor_scalar(out=stat[:, 8:12], in0=stat[:, 8:12],
                            scalar1=1e-5, scalar2=wsc2[:, 3:4],
                            op0=ALU.max, op1=ALU.mult)
            VF(e).then_inc(s_v, 1)
            cnt["v"] += 1
        VE(ep_v1)
        v_sig = cnt["v"]

        for ti in range(4):
            tt = 4 * grp4 + ti

            def ep_z(e, tt=tt, ti=ti, v_sig=v_sig):
                e.wait_ge(s_v, v_sig)
                e.scalar_tensor_tensor(out=tscr,
                                       in0=h2raw[:, C * ti:C * ti + C],
                                       scalar=stat[:, 8 + ti:9 + ti],
                                       in1=b2bc, op0=ALU.mult, op1=ALU.add)
                e.tensor_tensor(out=sbf[:, C * tt:C * tt + C], in0=tscr,
                                in1=sbf[:, C * tt:C * tt + C], op=ALU.add)
                sl = sbf[:, C * tt:C * tt + C]
                e.reduce_sum(out=stat[:, 16 + ti:17 + ti], in_=sl, axis=AX.X)
                e.scalar_tensor_tensor(out=pt[1][:, 0:C], in0=sl, scalar=1.0,
                                       in1=sl, op0=ALU.mult, op1=ALU.mult,
                                       accum_out=stat[:, 20 + ti:21 + ti]
                                       ).then_inc(s_v, 1)
                cnt["v"] += 1
            VE(ep_z)
        v_z4 = cnt["v"]

        def ep_v2(e, v_z4=v_z4):
            e.wait_ge(s_v, v_z4)
            e.tensor_scalar(out=stat[:, 16:20], in0=stat[:, 16:20],
                            scalar1=float(np.float32(-1.0 / C)), scalar2=None,
                            op0=ALU.mult)
            VF(e, 1)
            e.tensor_scalar(out=stat[:, 20:24], in0=stat[:, 20:24],
                            scalar1=float(np.float32(1.0 / C)), scalar2=None, op0=ALU.mult)
            VF(e, 1)
            e.tensor_tensor(out=stat[:, 28:32], in0=stat[:, 16:20],
                            in1=stat[:, 16:20], op=ALU.mult)
            VF(e)
            e.tensor_tensor(out=stat[:, 20:24], in0=stat[:, 20:24],
                            in1=stat[:, 28:32], op=ALU.subtract)
            cnt["v"] += 1
            VF(e, 1).then_inc(s_v, 1)
        VE(ep_v2)
        v_l2 = cnt["v"]

        def ep_s2(e, v_l2=v_l2):
            e.wait_ge(s_v, v_l2)
            e.activation(out=stat[:, 24:28], in_=stat[:, 20:24], func=AF.Sqrt,
                         bias=np.float32(1e-5)).then_inc(s_s, 1)
            cnt["s"] += 1
        SE(ep_s2)
        s_l2 = cnt["s"]

        def ep_v3(e, s_l2=s_l2):
            e.wait_ge(s_s, s_l2)
            e.reciprocal(out=stat[:, 24:28], in_=stat[:, 24:28])
            VF(e)
            e.tensor_tensor(out=stat[:, 28:32], in0=stat[:, 16:20],
                            in1=stat[:, 24:28], op=ALU.mult)
            VF(e).then_inc(s_v, 1)
            cnt["v"] += 1
        VE(ep_v3)
        v_l2r = cnt["v"]

        for ti in range(4):
            tt = 4 * grp4 + ti

            def ep_out(e, tt=tt, ti=ti, v_l2r=v_l2r):
                e.wait_ge(s_v, v_l2r)
                if len(out_sent) >= 2:
                    e.wait_ge(s_out, out_sent[-2])
                e.activation(out=ostg[tt % 2], in_=sbf[:, C * tt:C * tt + C],
                             func=AF.Identity, bias=stat[:, 28 + ti:29 + ti],
                             scale=stat[:, 24 + ti:25 + ti]).then_inc(s_s, 1)
                cnt["s"] += 1
            SE(ep_out)
            s_o = cnt["s"]

            def ep_dma(e, tt=tt, s_o=s_o):
                e.wait_ge(s_s, s_o)
                e.dma_start(out=bass.AP(out_d.tensor, 128 * tt * C,
                                        [[C, 128], [1, C]]),
                            in_=ostg[tt % 2]).then_inc(s_out, 16)
                cnt["out"] += 16
            GP(ep_dma)
            out_sent.append(cnt["out"])

    if DEBUG_DUMP:
        dump_chunks = [
            [(0, 4, wsc2), (8, 16, wsc[0:1, 0:16]), (32, 128, stat)],
            [(0, 1024, w1q[:, 0:1024])],
            [(0, 1024, w1q[:, 4096 * 7 + 2048:4096 * 7 + 3072])],
            [(0, 1024, w2q[:, 0:1024])],
            [(0, 1024, sbf[:, 0:1024])],
            [(0, 1024, sbf[:, 15 * 1024:16 * 1024])],
            [(0, 1024, h2raw[:, 0:1024])],
            [(0, 1024, h2raw[:, 3 * 1024:4 * 1024])],
            [(0, 1024, hbuf[:, 0:1024])],
            [(0, 1024, hbuf[:, 2048:3072])],
            [(0, 1024, qg[:, 0:1024])],
            [(0, 1024, qgT[:, 0:1024])],
            [(0, 1024, xq)],
            [(0, 1024, xqT)],
            [(0, 1024, b2bc)],
            [(0, 1024, tscr)],
        ]

        dump_base = cnt["out"]
        dump_v = []

        def dump_cp(e):
            for sem, key in [(s_ld, "ld"), (s_gd, "gd"), (s_t, "t"),
                             (s_s, "s"), (s_cc, "cc"), (s_out, "out")]:
                e.wait_ge(sem, cnt[key])
            for i, grp in enumerate(dump_chunks):
                bb = i % 2
                if i >= 2:
                    e.wait_ge(s_out, dump_base + 16 * (i - 1))
                e.memset(ostg[bb], 0.0)
                li = None
                for lo, w, src in grp:
                    pp = 1 if (lo == 8) else 128
                    li = e.tensor_copy(out=ostg[bb][0:pp, lo:lo + w], in_=src)
                li.then_inc(s_v, 1)
                cnt["v"] += 1
                dump_v.append(cnt["v"])
        VE(dump_cp)

        def dump_dma(e):
            for i in range(len(dump_chunks)):
                e.wait_ge(s_v, dump_v[i])
                e.dma_start(out=bass.AP(dbg_d.tensor, 128 * i * C,
                                        [[C, 128], [1, C]]),
                            in_=ostg[i % 2]).then_inc(s_out, 16)
                cnt["out"] += 16
        GP(dump_dma)

    def fin(e):
        e.wait_ge(s_out, cnt["out"])
    GP(fin)

    for c in reversed(semctx):
        c.__exit__(None, None, None)
    ctx.__exit__(None, None, None)

    f = nc.m.functions[0]
    order = {"main": 0, "SP": 1, "PE": 2, "Pool": 3, "DVE": 4, "Activation": 5}

    def bgrp(b):
        nm = b.name
        if nm == "main":
            return 0
        if nm.startswith("g_exch_if_"):
            return order["Pool"]
        parts = nm.split("_")
        if nm.startswith("block_") and len(parts) > 2 and parts[2] in order:
            return order[parts[2]]
        return 9
    f.blocks = sorted(f.blocks, key=bgrp)

    if upto is None:
        _CACHE["nc"] = nc
    return nc


def kernel(L0_attn_x, L0_attn_w_qkv, L0_attn_b_qkv, L0_attn_w_proj,
           L0_attn_b_proj, L0_attn_pos_independent, L0_ffn_w1, L0_ffn_b1,
           L0_ffn_w2, L0_ffn_b2):
    phi, psi, tri, eyef, eyeb = _host_tables()
    x = np.ascontiguousarray(np.asarray(L0_attn_x, dtype=np.float32))
    wqkv = np.asarray(L0_attn_w_qkv, dtype=np.float32)
    bqkv = np.asarray(L0_attn_b_qkv, dtype=np.float32)
    wproj = np.asarray(L0_attn_w_proj, dtype=np.float32)
    bproj = np.ascontiguousarray(np.asarray(L0_attn_b_proj, dtype=np.float32))
    pos_ind = np.ascontiguousarray(np.asarray(L0_attn_pos_independent,
                                              dtype=np.float32))
    w1 = np.asarray(L0_ffn_w1, dtype=np.float32)
    b1 = np.ascontiguousarray(np.asarray(L0_ffn_b1, dtype=np.float32))
    w2 = np.asarray(L0_ffn_w2, dtype=np.float32)
    b2 = np.ascontiguousarray(np.asarray(L0_ffn_b2, dtype=np.float32))

    zeros_tab = np.zeros_like(phi)
    zslab = np.zeros((8, 64), np.float32)
    pslab = np.ascontiguousarray(pos_ind.reshape(8, 64))
    nc = build_nc()

    in_maps = []
    for c in range(8):
        b, p = c // 2, c % 2
        perm = np.concatenate([np.arange(512) + 512 * p,
                               np.arange(512) + 512 * (1 - p)])
        im = dict(
            x_full=x[b],
            x_own=np.ascontiguousarray(x[b, 2048 * p:2048 * p + 2048]),
            wq_t=np.ascontiguousarray(wqkv[512 * p:512 * p + 512].T),
            wk_t=np.ascontiguousarray(
                wqkv[1024 + 512 * p:1024 + 512 * p + 512].T),
            wv_t=np.ascontiguousarray(
                wqkv[2048 + 512 * p:2048 + 512 * p + 512].T),
            bq_in=np.ascontiguousarray(bqkv[512 * p:512 * p + 512]),
            bk_in=np.ascontiguousarray(bqkv[1024 + 512 * p:1024 + 512 * p + 512]),
            bv_in=np.ascontiguousarray(bqkv[2048 + 512 * p:2048 + 512 * p + 512]),
            wproj_t=np.ascontiguousarray(wproj.T[perm, :]),
            bproj_in=bproj,
            w1_t=np.ascontiguousarray(w1.T),
            b1_in=b1,
            w2_t=np.ascontiguousarray(w2.T),
            b2_in=b2,
            pos_slab=(zslab if p == 0 else pslab),
            phi_in=(phi if p == 0 else zeros_tab),
            psi_in=(psi if p == 0 else zeros_tab),
            ebias_in=np.full(8, -(32.0 + MARGIN) if p == 0 else -MARGIN,
                             dtype=np.float32),
            tri_in=tri,
            eyef_in=eyef,
            eyeb_in=eyeb,
            psel=np.array([[p]], dtype=np.int32),
        )
        in_maps.append(im)

    res = run_bass_kernel_spmd(nc, in_maps, core_ids=list(range(8)))
    _LAST_RES.clear()
    _LAST_RES.extend(res.results if hasattr(res, "results") else res)
    out = np.empty((4, T, C), dtype=np.float32)
    for c in range(8):
        b, p = c // 2, c % 2
        out[b, 2048 * p:2048 * p + 2048] = np.asarray(res.results[c]["out"])
    return out
